# version 12
# baseline (speedup 1.0000x reference)
import numpy as np
import ml_dtypes
import concourse.bass as bass
import concourse.mybir as mybir
from concourse.bass_utils import run_bass_kernel_spmd

F32 = mybir.dt.float32
BF16 = mybir.dt.bfloat16
AF = mybir.ActivationFunctionType
ALU = mybir.AluOpType
AX = mybir.AxisListType

T = 2050
NCH = 8
EPS = 1e-5

SL = [(i * 256, (i + 1) * 256) for i in range(8)] + [(2048, 2050)]
SL5 = [(0, 512), (512, 1024), (1024, 1536), (1536, 2048), (2048, 2050)]
SLO = [(2 + i * 512, 2 + (i + 1) * 512) for i in range(4)]
CH17 = [(i * 128, i * 128 + 128) for i in range(16)] + [(2048, 2050)]
CHO16 = [(2 + i * 128, 2 + i * 128 + 128) for i in range(16)]
RG = [[0, 1], [2, 3], [4, 5], [6, 7]]


def build_nc(stage=0):
    import os
    SKIP = set(os.environ.get("KERNEL_SKIP", "").split(","))
    nc = bass.Bass("TRN2", target_bir_lowering=False)

    xt = nc.dram_tensor("xt", [1024, T], F32, kind="ExternalInput").ap()
    wqkv = nc.dram_tensor("wqkv", [1024, 3072], F32, kind="ExternalInput").ap()
    wproj_d = nc.dram_tensor("wproj", [1024, 1024], F32, kind="ExternalInput").ap()
    wrt_d = nc.dram_tensor("wrt", [1024, 1024], F32, kind="ExternalInput").ap()
    wkt_d = nc.dram_tensor("wkt", [1024, 1024], F32, kind="ExternalInput").ap()
    wvt_d = nc.dram_tensor("wvt", [1024, 1024], F32, kind="ExternalInput").ap()
    mu_d = nc.dram_tensor("mud", [2, 8, 128], F32, kind="ExternalInput").ap()
    bmask_d = nc.dram_tensor("bmaskd", [128, 1], F32, kind="ExternalInput").ap()
    identbf_d = nc.dram_tensor("identbfd", [128, 128], BF16, kind="ExternalInput").ap()
    ints_d = nc.dram_tensor("intsd", [128, 3, 64], BF16, kind="ExternalInput").ap()
    onescol_d = nc.dram_tensor("onescold", [128, 1], BF16, kind="ExternalInput").ap()
    onesbd2_d = nc.dram_tensor("onesbd2d", [128, 2], BF16, kind="ExternalInput").ap()
    bconst_d = nc.dram_tensor("bconstd", [1, 2, 128], F32, kind="ExternalInput").ap()
    b2_d = nc.dram_tensor("b2d", [2, 128], F32, kind="ExternalInput").ap()

    out_d = nc.dram_tensor("out", [1024, 2048], F32, kind="ExternalOutput").ap()

    xsp = nc.dram_tensor("xsp", [128, NCH, T], F32).ap()
    cc3in = nc.dram_tensor("cc3in", [1024, 32], BF16)
    cc3out = nc.dram_tensor("cc3out", [2, 1024, 32], BF16)
    cc4in = nc.dram_tensor("cc4in", [128, 520], F32)
    cc4out = nc.dram_tensor("cc4out", [128, 520], F32)

    streams = {e: [] for e in ("T", "V", "S", "G", "Y")}
    cnt = {"T": 0, "V": 0, "S": 0, "G": 0, "dY": 0, "dG": 0, "CC": 0}
    semh = {}

    def op(eng, fn, inc=None, n=1):
        if inc is None:
            streams[eng].append(lambda e, fn=fn: fn(e))
            return None
        cnt[inc] += n
        def thunk(e, fn=fn, inc=inc, n=n):
            fn(e).then_inc(semh[inc], n)
        streams[eng].append(thunk)
        return cnt[inc]

    def wt(eng, sem, val):
        if val is not None and val > 0:
            streams[eng].append(lambda e, sem=sem, val=val: e.wait_ge(semh[sem], val))

    from contextlib import ExitStack
    with ExitStack() as stack:
        ec = stack.enter_context
        XRb = ec(nc.sbuf_tensor("XR", [128, 16400], F32))
        WA = ec(nc.sbuf_tensor("WA", [128, 24576], BF16))
        A3 = ec(nc.sbuf_tensor("A3", [128, 16400], BF16))
        SCR = ec(nc.sbuf_tensor("SCR", [128, 4618], F32))
        GT2 = ec(nc.sbuf_tensor("GT2", [128, 4100], BF16))
        ROWS = ec(nc.sbuf_tensor("ROWS", [128, 4, 2050], F32))
        identbf = ec(nc.sbuf_tensor("identbf", [128, 128], BF16))
        ints = ec(nc.sbuf_tensor("ints", [128, 3, 64], BF16))
        onescol = ec(nc.sbuf_tensor("onescol", [128, 1], BF16))
        onesbd2 = ec(nc.sbuf_tensor("onesbd2", [128, 2], BF16))
        bconst = ec(nc.sbuf_tensor("bconst", [1, 2, 128], F32))
        b2c = ec(nc.sbuf_tensor("b2c", [2, 128], F32))
        mu_sb = ec(nc.sbuf_tensor("mu_sb", [128, 2, 8], F32))
        omu_sb = ec(nc.sbuf_tensor("omu_sb", [128, 2, 8], F32))
        bmask_sb = ec(nc.sbuf_tensor("bmask_sb", [128, 1], F32))
        sums3p = ec(nc.sbuf_tensor("sums3p", [128, 8, 4], F32))
        s3tot = ec(nc.sbuf_tensor("s3tot", [128, 8], F32))
        denr = ec(nc.sbuf_tensor("denr", [128, 8], F32))
        rc16 = ec(nc.sbuf_tensor("rc16", [128, 48], F32))
        rc8 = ec(nc.sbuf_tensor("rc8", [128, 8], F32))
        cs2 = ec(nc.sbuf_tensor("cs2", [2, 8, 64], F32))
        mx2 = ec(nc.sbuf_tensor("mx2", [2, 8], F32))
        PSA = ec(nc.psum_tensor("PSA", [128, 1024], F32))
        PSB = ec(nc.psum_tensor("PSB", [128, 1024], F32))
        PSC = ec(nc.psum_tensor("PSC", [128, 1024], F32))
        PSD = ec(nc.psum_tensor("PSD", [128, 512], F32))
        PST = ec(nc.psum_tensor("PST", [128, 1024], BF16))
        semT = ec(nc.semaphore("semT"))
        semV = ec(nc.semaphore("semV"))
        semS = ec(nc.semaphore("semS"))
        semG = ec(nc.semaphore("semG"))
        dY = ec(nc.semaphore("dY"))
        dG = ec(nc.semaphore("dG"))
        scc = ec(nc.semaphore("scc"))
        block = ec(nc.Block())
        semh.update({"T": semT, "V": semV, "S": semS, "G": semG,
                     "dY": dY, "dG": dG, "CC": scc})

        def view(base, off, shape):
            n = int(np.prod(shape))
            apx = base[:, off:off + n]
            if len(shape) == 2:
                apx = apx.rearrange("p (a b) -> p a b", b=shape[1])
            elif len(shape) == 3:
                apx = apx.rearrange("p (a b c) -> p a b c", b=shape[1], c=shape[2])
            return apx

        X = view(XRb, 0, [NCH, T])
        XRbf = XRb[:, :].bitcast(BF16)
        QALL = view(XRbf, 0, [NCH, T])
        K1T = view(XRbf, NCH * T, [NCH, T])
        PL1 = view(XRb, 8200, [NCH, 1024])

        WRT = view(WA, 0, [NCH, 1024])
        WKT = view(WA, 8192, [NCH, 1024])
        WVT = view(WA, 16384, [NCH, 1024])
        WG = [view(WA, 0, [NCH, 384]), view(WA, 3072, [NCH, 384])]
        WG4 = [view(WA, 0, [NCH, 3, 128]), view(WA, 3072, [NCH, 3, 128])]
        BDQ = view(WA, 6144, [NCH, 128])
        K2E = view(WA, 7168, [NCH, 64])
        BDKLM = view(WA, 7680, [NCH, 128])
        KLM = view(WA, 8704, [NCH, 64])
        POOLED = view(WA, 9216, [NCH, 32])
        POOLEDG = view(WA, 9472, [NCH, 64])
        KG = WA[:, 9984:9984 + T]
        VG = WA[:, 12034:12034 + T]
        K3Eb = WA[:, 14084:14084 + 2048]
        K3T = view(WA, 16132, [16, 128])
        VTOK = view(WA, 18180, [16, 128])
        E1B = [WA[:, 20228:20228 + 1024], WA[:, 21252:21252 + 1024],
               WA[:, 22276:22276 + 1024]]
        KG2 = GT2[:, 0:T]
        VG2 = GT2[:, T:2 * T]
        K3Eb2 = K3Eb
        K3T2 = K3T
        VTOK2 = VTOK
        KTBD = view(WA, 0, [NCH, 128])
        PTBD = view(WA, 1024, [NCH, 128])
        VTBD = view(WA, 2048, [NCH, 128])
        VS_ = view(WA, 3072, [NCH, 64])
        NS1 = view(WA, 3584, [NCH, 64])
        NS2 = view(WA, 4096, [NCH, 64])
        NS3 = view(WA, 4608, [NCH, 64])
        BDP = view(WA, 5120, [NCH, 128])
        K3VN = view(WA, 9984, [NCH, 64])
        WPROJ = view(WA, 10496, [NCH, 1024])

        XR2 = view(A3, 0, [NCH, 2, 256])
        XK2 = view(A3, 4096, [NCH, 2, 256])
        KACT = view(A3, 8192, [NCH, 256])
        X2SL = [view(A3, 0, [NCH, 256]), view(A3, 2048, [NCH, 256])]
        SQSL = [view(A3, 4096, [NCH, 256]), view(A3, 6144, [NCH, 256])]
        LNSLB = view(A3, 10240, [NCH, 257])
        DSLB = view(A3, 12296, [NCH, 256])
        LN2BF = view(A3, 0, [NCH, T])
        SVALL = view(A3, 0, [NCH, T])

        LNSL = view(SCR, 0, [NCH, 257])
        T1DB = view(SCR, 2056, [2, 256])
        KTMP2 = view(SCR, 2568, [2, 256])
        RCH2 = view(SCR, 3080, [2, 256])
        TCH2 = view(SCR, 3592, [2, 256])
        BCN = SCR[:, 4104:4104 + 257]
        BCS = SCR[:, 4361:4361 + 257]
        LNSCR = view(SCR, 0, [NCH, 256])
        PL2 = view(SCR, 0, [NCH, 512])
        PL3 = view(XRb, 8200, [NCH, 256])
        PL4 = view(SCR, 0, [NCH, 128])
        PL5 = view(XRb, 8200 + 4096, [NCH, 64])
        PL6 = view(SCR, 1024, [NCH, 32])
        CCS = SCR[:, 0:520]
        CCR = SCR[:, 520:1040]

        srow = ROWS[0:1, 0, :]
        qrow = ROWS[0:1, 1, :]
        mrow = ROWS[0:1, 2, :]
        vrow = ROWS[0:1, 3, :]

        op("Y", lambda e: e.dma_start(out=X[:], in_=xt.rearrange("(c p) t -> p c t", p=128)), "dY", 16)
        for src, dst in [(identbf_d, identbf), (ints_d, ints), (onescol_d, onescol),
                         (onesbd2_d, onesbd2), (bconst_d, bconst),
                         (b2_d, b2c), (bmask_d, bmask_sb)]:
            op("Y", lambda e, s=src, d=dst: e.dma_start(out=d[:], in_=s), "dY", 16)
        (op("Y", lambda e: e.dma_start(out=mu_sb[:], in_=mu_d.rearrange("i c p -> p i c"),
                                      allow_slow_non_contiguous=True), "dY", 16)
         if "mu" not in SKIP else op("Y", lambda e: e.dma_start(out=mu_sb[:, 0, 0:1], in_=mu_d[0:1, 0, 0:1].transpose([1, 0]) if False else mu_d[0, 0:1, 0:1]), "dY", 16))
        PRO_dY = cnt["dY"]

        def wg_load_thunks(g, buf):
            outs = []
            for t3 in range(3):
                col = t3 * 1024 + g * 128
                outs.append(lambda e, t3=t3, col=col, buf=buf: e.dma_start(
                    out=view(WA, buf * 3072 + t3 * 128, [NCH, 128]) if False else WG4[buf][:, :, t3, :],
                    in_=wqkv[:, col:col + 128].rearrange("(c p) d -> p c d", p=128)))
            return outs

        wt("V", "dY", PRO_dY)
        op("V", lambda e: e.tensor_scalar(out=omu_sb[:], in0=mu_sb[:], scalar1=-1.0,
                                          scalar2=1.0, op0=ALU.mult, op1=ALU.add))

        SCRbf = SCR[:, :].bitcast(BF16)
        X2SLB = [GT2[:, 0:2048].rearrange("p (a b) -> p a b", b=256),
                 GT2[:, 2048:4096].rearrange("p (a b) -> p a b", b=256)]
        SQSLB = [SCRbf[:, 4112:4112 + 2048].rearrange("p (a b) -> p a b", b=256),
                 SCRbf[:, 6160:6160 + 2048].rearrange("p (a b) -> p a b", b=256)]

        def make_rows(src_dram, gate_sem, gate_val, slice_gates=None, slots=None):
            SX, SQ = slots if slots is not None else (X2SL, SQSL)
            """Lazy per-slice LN stats. Returns rows_of(i) -> (sem, tick): emits
            stats slices up to i+2 and per-slice finishes up to i on demand, so
            the T stream interleaves stat matmuls with consuming bcasts."""
            sum_T = {}
            sq_T = {}
            ev_sum_S = {}
            ev_sq_S = {}
            castG = {}
            sqV = {}
            fin_tick = {}

            def emit_stats(i):
                a, b = SL[i]
                w = b - a
                pg = i % 2
                if slice_gates is not None:
                    wt("G", slice_gates[i][0], slice_gates[i][1])
                elif gate_sem is not None and i == 0:
                    wt("G", gate_sem, gate_val)
                if i >= 2:
                    wt("G", "T", sum_T[i - 2])
                    wt("G", "V", sqV[i - 2])
                if src_dram is not None:
                    castG[i] = op("G", lambda e, a=a, w=w, pg=pg: e.dma_start(
                        out=SX[pg][:, :, 0:w],
                        in_=src_dram.rearrange("(c p) t -> p c t", p=128)[:, :, a:a + w]), "dG", 16)
                else:
                    castG[i] = op("G", lambda e, a=a, w=w, pg=pg: e.dma_start(
                        out=SX[pg][:, :, 0:w], in_=X[:, :, a:a + w]), "dG", 16)
                wt("V", "dG", castG[i])
                if i >= 2:
                    wt("V", "T", sq_T[i - 2])
                sqV[i] = op("V", lambda e, w=w, pg=pg: e.tensor_tensor(
                    out=SQ[pg][:, :, 0:w], in0=SX[pg][:, :, 0:w],
                    in1=SX[pg][:, :, 0:w], op=ALU.mult), "V")
                wt("T", "dG", castG[i])
                if i == 0:
                    wt("T", "dY", PRO_dY)
                if i >= 2:
                    wt("T", "S", ev_sum_S[i - 2])
                def mm_sum(e, w=w, pg=pg):
                    ins = None
                    for kc in range(NCH):
                        ins = e.matmul(PSC[0:1, pg * 512:pg * 512 + w], onescol[:],
                                       SX[pg][:, kc, 0:w], start=(kc == 0), stop=(kc == NCH - 1))
                    return ins
                sum_T[i] = op("T", mm_sum, "T")
                wt("T", "V", sqV[i])
                if i >= 2:
                    wt("T", "S", ev_sq_S[i - 2])
                def mm_sq(e, w=w, pg=pg):
                    ins = None
                    for kc in range(NCH):
                        ins = e.matmul(PSB[0:1, pg * 512:pg * 512 + w], onescol[:],
                                       SQ[pg][:, kc, 0:w], start=(kc == 0), stop=(kc == NCH - 1))
                    return ins
                sq_T[i] = op("T", mm_sq, "T")
                wt("S", "T", sum_T[i])
                ev_sum_S[i] = op("S", lambda e, a=a, w=w, pg=pg: e.copy(
                    out=srow[:, a:a + w], in_=PSC[0:1, pg * 512:pg * 512 + w]), "S")
                wt("S", "T", sq_T[i])
                ev_sq_S[i] = op("S", lambda e, a=a, w=w, pg=pg: e.copy(
                    out=qrow[:, a:a + w], in_=PSB[0:1, pg * 512:pg * 512 + w]), "S")

            def emit_finish(k):
                a = SL[k][0]
                b = 2050 if k == 7 else SL[k][1]
                wt("V", "S", ev_sq_S[8 if k == 7 else k])
                op("V", lambda e, a=a, b=b: e.tensor_tensor(
                    out=vrow[:, a:b], in0=srow[:, a:b], in1=srow[:, a:b], op=ALU.mult))
                op("V", lambda e, a=a, b=b: e.scalar_tensor_tensor(
                    out=vrow[:, a:b], in0=vrow[:, a:b], scalar=1.0 / 1024.0,
                    in1=qrow[:, a:b], op0=ALU.mult, op1=ALU.subtract))
                epsV = op("V", lambda e, a=a, b=b: e.tensor_scalar(
                    out=vrow[:, a:b], in0=vrow[:, a:b], scalar1=-1.0 / 1024.0,
                    scalar2=EPS, op0=ALU.mult, op1=ALU.add), "V")
                wt("S", "V", epsV)
                sdS = op("S", lambda e, a=a, b=b: e.activation(
                    out=vrow[:, a:b], in_=vrow[:, a:b], func=AF.Sqrt), "S")
                wt("V", "S", sdS)
                fin_tick[k] = op("V", lambda e, a=a, b=b: e.reciprocal(
                    out=vrow[:, a:b], in_=vrow[:, a:b]), "V")

            def rows_of(i):
                k = min(i, 7)
                for j in range(len(castG), min(i + 1 + rows_of.prefetch, 9)):
                    emit_stats(j)
                for j in range(len(fin_tick), k + 1):
                    if j == 7 and 8 not in castG:
                        emit_stats(8)
                    emit_finish(j)
                return ("V", fin_tick[k])

            rows_of.prefetch = 2
            rows_of.last_S = lambda: ev_sq_S[8]
            return rows_of

        def emit_bcast_slice(i, a, b, rows_of, prev_user_V, prev_user_G=None):
            w = b - a
            tick = rows_of(i)
            wt("T", tick[0], tick[1])
            if prev_user_V is not None:
                wt("T", "V", prev_user_V)
            if prev_user_G is not None:
                wt("T", "G", prev_user_G)
            def mm_bc(e, a=a, w=w):
                e.matmul(PSD[:, 0:w], bconst[0:1, 0, :], srow[:, a:a + w], start=True, stop=True)
                return e.matmul(PSD[:, 256:256 + w], bconst[0:1, 1, :], vrow[:, a:a + w], start=True, stop=True)
            t2 = op("T", mm_bc, "T")
            wt("V", "T", t2)
            op("V", lambda e, w=w: e.tensor_copy(out=BCN[:, 1:1 + w], in_=PSD[:, 0:w]))
            vv = op("V", lambda e, w=w: e.tensor_copy(out=BCS[:, 1:1 + w], in_=PSD[:, 256:256 + w]), "V")
            if w < 64:
                op("V", lambda e: e.drain())
            return vv

        def emit_ffn(rows_V, w_dG, snap_cb=None, stats_S=None):
            snap_cb2 = [snap_cb]
            bc_user = None
            reluS = {}
            kactV = {}
            sigS = {}
            rvV = {}
            addG = {}
            mixV = {}
            t1hist = []
            mix_hist = []
            vT = {}
            NRV = 0
            def stageA(i):
                nonlocal bc_user
                a, b = SL[i]
                w = b - a
                bc_user = emit_bcast_slice(i, a, b, rows_V, bc_user)
                if i >= 2:
                    wt("V", "T", vT[(i - 2, NCH - 1)])
                if i == 0:
                    op("V", lambda e: e.memset(LNSLB[:, :, 0:1], 0.0))
                else:
                    op("V", lambda e: e.tensor_copy(out=LNSLB[:, :, 0:1], in_=LNSLB[:, :, 256:257]))
                if w < 64:
                    op("V", lambda e: e.drain())
                op("V", lambda e, a=a, w=w: e.tensor_tensor(
                    out=LNSL[:, :, 1:1 + w], in0=X[:, :, a:a + w],
                    in1=BCN[:, 1:1 + w].unsqueeze(1).broadcast_to([128, NCH, w]),
                    op=ALU.add))
                if w < 64:
                    op("V", lambda e: e.drain())
                lnV = op("V", lambda e, w=w: e.tensor_tensor(
                    out=LNSLB[:, :, 1:1 + w], in0=LNSL[:, :, 1:1 + w],
                    in1=BCS[:, 1:1 + w].unsqueeze(1).broadcast_to([128, NCH, w]),
                    op=ALU.mult), "V")
                if w < 64:
                    op("V", lambda e: e.drain())
                if i == len(SL) - 1 and 15 not in snaps:
                    snaps[15] = ({k: len(v) for k, v in streams.items()}, dict(cnt),
                                 ("V", lnV), "LNSL131")
                pg = i % 2
                op("V", lambda e, w=w: e.tensor_tensor(
                    out=DSLB[:, :, 0:w], in0=LNSLB[:, :, 1:1 + w],
                    in1=LNSLB[:, :, 0:w], op=ALU.subtract))
                for mi, dst2 in [(0, XR2), (1, XK2)]:
                    for c in range(NCH):
                        op("V", lambda e, c=c, w=w, mi=mi, dst2=dst2, pg=pg:
                           e.scalar_tensor_tensor(
                               out=dst2[:, c, pg, 0:w], in0=DSLB[:, c, 0:w],
                               scalar=mu_sb[:, mi, c:c + 1], in1=LNSLB[:, c, 0:w],
                               op0=ALU.mult, op1=ALU.add), "V")
                mixV[i] = cnt["V"]

            def stageB(i):
                a, b = SL[i]
                w = b - a
                pg = i % 2
                wt("T", "V", mixV[i])
                if i == 0:
                    wt("T", "dG", w_dG)
                for dp in range(NCH):
                    half = (dp % 2) * 512
                    prev_relu = reluS.get((i, dp - 2)) if dp >= 2 else reluS.get((i - 1, dp + 6))
                    if prev_relu is not None:
                        wt("T", "S", prev_relu)
                    def mm_k(e, w=w, pg=pg, dp=dp, half=half):
                        ins = None
                        for c in range(NCH):
                            ins = e.matmul(PSA[:, half:half + w], WKT[:, c, dp * 128:dp * 128 + 128],
                                           XK2[:, c, pg, 0:w], start=(c == 0), stop=(c == NCH - 1))
                        return ins
                    kTt = op("T", mm_k, "T")
                    wt("S", "T", kTt)
                    op("S", lambda e, w=w, dp=dp, half=half: e.activation(
                        out=KTMP2[:, dp % 2, 0:w], in_=PSA[:, half:half + w], func=AF.Relu))
                    reluS[(i, dp)] = op("S", lambda e, w=w, dp=dp: e.activation(
                        out=KACT[:, dp, 0:w], in_=KTMP2[:, dp % 2, 0:w], func=AF.Square), "S")
                    kactV[(i, dp)] = reluS[(i, dp)]

                if i == 0 and stats_S is not None:
                    wt("T", "S", stats_S())
                for dp in range(NCH):
                    half = (dp % 2) * 512
                    n = i * NCH + dp
                    prev_sig = sigS.get(n - 2)
                    if prev_sig is not None:
                        wt("T", "S", prev_sig)
                    def mm_r(e, w=w, pg=pg, dp=dp, half=half):
                        ins = None
                        for c in range(NCH):
                            ins = e.matmul(PSB[:, half:half + w], WRT[:, c, dp * 128:dp * 128 + 128],
                                           XR2[:, c, pg, 0:w], start=(c == 0), stop=(c == NCH - 1))
                        return ins
                    rTt = op("T", mm_r, "T")
                    if dp == 0:
                        wt("T", "S", kactV[(i, NCH - 1)])
                    prev_rv = rvV.get(n - 2)
                    if prev_rv is not None:
                        wt("T", "V", prev_rv)
                    def mm_v(e, w=w, dp=dp, half=half):
                        ins = None
                        for c in range(NCH):
                            ins = e.matmul(PSC[:, half:half + w], WVT[:, c, dp * 128:dp * 128 + 128],
                                           KACT[:, c, 0:w], start=(c == 0), stop=(c == NCH - 1))
                        return ins
                    vT[(i, dp)] = op("T", mm_v, "T")
                    wt("S", "T", rTt)
                    prev_rv1 = rvV.get(n - 1)
                    if prev_rv1 is not None:
                        wt("S", "V", prev_rv1)
                    sigS[n] = op("S", lambda e, w=w, dp=dp, half=half: e.activation(
                        out=RCH2[:, dp % 2, 0:w], in_=PSB[:, half:half + w], func=AF.Sigmoid), "S")
                    wt("V", "S", sigS[n])
                    wt("V", "T", vT[(i, dp)])
                    prev_add = addG.get(n - 2)
                    if prev_add is not None:
                        wt("V", "G", prev_add)
                    rvV[n] = op("V", lambda e, w=w, dp=dp, half=half: e.scalar_tensor_tensor(
                        out=TCH2[:, dp % 2, 0:w], in0=PSC[:, half:half + w], scalar=0.5,
                        in1=RCH2[:, dp % 2, 0:w], op0=ALU.mult, op1=ALU.mult), "V")
                    wt("G", "V", rvV[n])
                    addG[n] = op("G", lambda e, a=a, b=b, w=w, dp=dp: e.tensor_tensor(
                        out=X[:, dp, a:b], in0=X[:, dp, a:b], in1=TCH2[:, dp % 2, 0:w],
                        op=ALU.add), "G")
                if i == 0 and snap_cb is not None:
                    snap_cb2[0] = None

            for i in range(len(SL) + 1):
                if i < len(SL):
                    stageA(i)
                if i >= 1:
                    stageB(i - 1)
                    if i - 1 == 0 and snap_cb is not None:
                        snap_cb(cnt["G"])
            return cnt["G"], bc_user, addG

        snaps = {}
        def snap(n, gate, src_ap):
            snaps[n] = ({k: len(v) for k, v in streams.items()}, dict(cnt), gate, src_ap)

        rows1_of = make_rows(xt, None, None)
        rows1_of.prefetch = 9
        rows1_t0 = rows1_of(8)
        if "wcast" not in SKIP:
            op("G", lambda e: e.dma_start(out=WRT, in_=wrt_d.rearrange("(c p) d -> p c d", p=128)), "dG", 16)
            op("G", lambda e: e.dma_start(out=WKT, in_=wkt_d.rearrange("(c p) d -> p c d", p=128)), "dG", 16)
            op("G", lambda e: e.dma_start(out=WVT, in_=wvt_d.rearrange("(c p) d -> p c d", p=128)), "dG", 16)
        FFNW_dG = cnt["dG"]
        snap(11, ("dG", FFNW_dG), X[:, :, 2:T])
        if stage in (12, 121):
            rows1_full = rows1_of(8)
        else:
            rows1_full = rows1_t0
        snap(12, rows1_full, X[:, :, 2:T])
        snaps[121] = (snaps[12][0], snaps[12][1], rows1_full, "ROWS121")
        def _snap13(g_tick):
            snap(13, ("G", g_tick), X[:, :, 2:T])
            snaps[131] = (snaps[13][0], snaps[13][1], ("G", g_tick), "LNSL131")
            snaps[132] = (snaps[13][0], snaps[13][1], ("G", g_tick), "XRK132")
        ffn1_G, _, ffn1_addG = emit_ffn(rows1_of, FFNW_dG, snap_cb=_snap13,
                                        stats_S=rows1_of.last_S)
        ffn1_T = cnt["T"]
        snaps[14] = ({k: len(v) for k, v in streams.items()}, dict(cnt), ("G", ffn1_G), "LNSL131")
        snaps[141] = ({k: len(v) for k, v in streams.items()}, dict(cnt), ("G", ffn1_G), "XRK132")

        snap(1, ("G", ffn1_G), X[:, :, 2:T])
        rows2_of = make_rows(None, "G", ffn1_G,
                             slice_gates=[("G", ffn1_addG[i * NCH + NCH - 1]) for i in range(len(SL))],
                             slots=(X2SLB, SQSLB))

        wt("Y", "G", ffn1_G)
        XSPILL_dY = op("Y", lambda e: e.dma_start(out=xsp, in_=X[:]), "dY", 16)

        bc_user = None
        bc_userG = None
        ln2V = {}
        for i, (a, b) in enumerate(SL):
            w = b - a
            vv = emit_bcast_slice(i, a, b, rows2_of, bc_user, bc_userG)
            wt("G", "V", vv)
            bc_userG = op("G", lambda e, a=a, w=w: e.tensor_tensor(
                out=LNSCR[:, :, 0:w], in0=X[:, :, a:a + w],
                in1=BCN[:, 1:1 + w].unsqueeze(1).broadcast_to([128, NCH, w]), op=ALU.add), "G")
            wt("V", "G", bc_userG)
            bc_user = op("V", lambda e, a=a, w=w: e.tensor_tensor(
                out=LN2BF[:, :, a:a + w], in0=LNSCR[:, :, 0:w],
                in1=BCS[:, 1:1 + w].unsqueeze(1).broadcast_to([128, NCH, w]), op=ALU.mult), "V")
            if w < 64:
                op("V", lambda e: e.drain())
            ln2V[i] = bc_user
        LN2_V = cnt["V"]

        snaps[21] = ({k: len(v) for k, v in streams.items()}, dict(cnt),
                     ("V", LN2_V), "LN2BF21")
        snaps[22] = ({k: len(v) for k, v in streams.items()}, dict(cnt),
                     ("V", LN2_V), "ROWS121")
        wt("V", "dY", XSPILL_dY)
        op("V", lambda e: e.tensor_tensor(out=PL1[:], in0=LN2BF[:, :, 2:2049:2],
                                          in1=LN2BF[:, :, 3:2050:2], op=ALU.add))
        op("V", lambda e: e.tensor_tensor(out=PL2[:], in0=PL1[:, :, 0:1024:2],
                                          in1=PL1[:, :, 1:1024:2], op=ALU.add))
        op("V", lambda e: e.tensor_tensor(out=PL3[:], in0=PL2[:, :, 0:512:2],
                                          in1=PL2[:, :, 1:512:2], op=ALU.add))
        op("V", lambda e: e.tensor_tensor(out=PL4[:], in0=PL3[:, :, 0:256:2],
                                          in1=PL3[:, :, 1:256:2], op=ALU.add))
        op("V", lambda e: e.tensor_tensor(out=PL5[:], in0=PL4[:, :, 0:128:2],
                                          in1=PL4[:, :, 1:128:2], op=ALU.add))
        op("V", lambda e: e.tensor_tensor(out=PL6[:], in0=PL5[:, :, 0:64:2],
                                          in1=PL5[:, :, 1:64:2], op=ALU.add))
        POOL_V = op("V", lambda e: e.tensor_scalar(out=POOLED[:], in0=PL6[:], scalar1=1.0 / 64.0,
                                                   scalar2=None, op0=ALU.mult), "V")

        wt("G", "T", ffn1_T)
        WGL = {}
        for th in wg_load_thunks(0, 0):
            WGL[0] = op("G", th, "dG", 16)
        for th in wg_load_thunks(1, 1):
            WGL[1] = op("G", th, "dG", 16)

        wt("G", "V", POOL_V)
        op("G", lambda e: e.dma_start(out=cc3in.ap().rearrange("(c p) s -> p c s", p=128),
                                      in_=POOLED[:]), "dG", 16)
        wt("G", "dG", cnt["dG"])
        CC3 = op("G", lambda e: e.collective_compute(
            "AllGather", ALU.bypass, replica_groups=RG,
            ins=[cc3in.ap().opt()], outs=[cc3out.ap().opt()]), "CC", 1)
        wt("G", "CC", CC3)
        op("G", lambda e: e.dma_start(
            out=POOLEDG[:, :, 0:32],
            in_=cc3out.ap()[0].rearrange("(c p) s -> p c s", p=128)), "dG", 16)
        PGL_dG = op("G", lambda e: e.dma_start(
            out=POOLEDG[:, :, 32:64],
            in_=cc3out.ap()[1].rearrange("(c p) s -> p c s", p=128)), "dG", 16)

        snap(2, ("dG", PGL_dG), X[:, :, 2:T])
        wt("V", "T", ffn1_T)
        op("V", lambda e: e.memset(BDQ[:], 0.0))
        BDZERO_V = op("V", lambda e: e.memset(BDKLM[:], 0.0), "V")

        lastWG_T = {}
        qlmV = {}
        klmT = {}
        l2T = {}
        k3eS = {}
        k3tV = {}
        vtokS = {}
        k3vT = {}
        kevV = {}
        vevS = {}
        KGP = [KG, KG2]
        VGP = [VG, VG2]
        K3EP = [K3Eb, K3Eb2]
        K3TP = [K3T, K3T2]
        VTOKP = [VTOK, VTOK2]
        evhist = []

        def stage_Q(g):
            wg = WG[g % 2]
            pq = g % 2
            wt("T", "dG", WGL[g])
            for t3 in range(3):
                for si, (a, b) in enumerate(SL5):
                    w = b - a
                    half = ((t3 * 5 + si) % 2) * 512
                    if len(evhist) >= 2:
                        wt("T", evhist[-2][0], evhist[-2][1])
                    if g == 0:
                        wt("T", "V", ln2V[min(len(SL) - 1, (a + w) // 256)])
                    def mm_qkv(e, wg=wg, t3=t3, a=a, w=w, half=half):
                        ins = None
                        for kc in range(NCH):
                            ins = e.matmul(PSA[:, half:half + w], wg[:, kc, t3 * 128:t3 * 128 + 128],
                                           LN2BF[:, kc, a:a + w], start=(kc == 0), stop=(kc == NCH - 1))
                        return ins
                    tt = op("T", mm_qkv, "T")
                    if t3 == 0:
                        if g == 0 and si == 0:
                            wt("S", "dY", XSPILL_dY)
                        wt("S", "T", tt)
                        sv = op("S", lambda e, g=g, a=a, w=w, half=half: e.copy(
                            out=QALL[:, g, a:a + w], in_=PSA[:, half:half + w]), "S")
                        evhist.append(("S", sv))
                    elif t3 == 1:
                        wt("V", "T", tt)
                        if si == 0 and g >= 2:
                            wt("V", "T", k3vT[g - 2])
                        sv = op("V", lambda e, a=a, w=w, half=half, pq=pq: e.tensor_copy(
                            out=KGP[pq][:, a:a + w], in_=PSA[:, half:half + w]), "V")
                        evhist.append(("V", sv))
                    else:
                        wt("S", "T", tt)
                        if si == 0 and g >= 2:
                            wt("S", "T", k3vT[g - 2])
                        sv = op("S", lambda e, a=a, w=w, half=half, pq=pq: e.copy(
                            out=VGP[pq][:, a:a + w], in_=PSA[:, half:half + w]), "S")
                        evhist.append(("S", sv))
            kevV[g] = evhist[-6][1]
            vevS[g] = evhist[-1][1]
            lastWG_T[g] = cnt["T"]

        def stage_R(g):
            wg = WG[g % 2]
            pq = g % 2
            wt("T", "dG", PGL_dG)
            if g >= 1:
                wt("T", "V", qlmV[g - 1])
                wt("T", "S", k3eS[(g - 1, 3)])
            def mm_qlm(e, wg=wg):
                ins = None
                for kc in range(NCH):
                    ins = e.matmul(PSB[:, 512:576], wg[:, kc, 0:128], POOLEDG[:, kc, :],
                                   start=(kc == 0), stop=(kc == NCH - 1))
                return ins
            t_qlm = op("T", mm_qlm, "T")
            def mm_klm(e, wg=wg):
                ins = None
                for kc in range(NCH):
                    ins = e.matmul(PSB[:, 576:640], wg[:, kc, 128:256], POOLEDG[:, kc, :],
                                   start=(kc == 0), stop=(kc == NCH - 1))
                return ins
            klmT[g] = op("T", mm_klm, "T")
            if g + 2 < 8:
                wt("G", "T", klmT[g])
                for th in wg_load_thunks(g + 2, g % 2):
                    WGL[g + 2] = op("G", th, "dG", 16)
            wt("V", "T", klmT[g])
            wt("V", "V", BDZERO_V)
            op("V", lambda e, g=g: e.tensor_scalar(out=BDQ[0:64, g, 0:64], in0=PSB[0:64, 512:576],
                                                   scalar1=0.125, scalar2=None, op0=ALU.mult))
            op("V", lambda e, g=g: e.tensor_scalar(out=BDQ[64:128, g, 64:128], in0=PSB[64:128, 512:576],
                                                   scalar1=0.125, scalar2=None, op0=ALU.mult))
            qlmV[g] = op("V", lambda e, g=g: e.tensor_copy(out=KLM[:, g, :], in_=PSB[:, 576:640]), "V")
            wt("T", "V", qlmV[g])
            l2T[g] = op("T", lambda e, g=g: e.matmul(PSD[:, g * 64:g * 64 + 64], BDQ[:, g, :],
                                                     KLM[:, g, :], start=True, stop=True), "T")

            for si, (a, b) in enumerate(SLO):
                wt("T", "V", kevV[g])
                prev = k3eS.get((g, si - 2)) if si >= 2 else k3eS.get((g - 1, si + 2))
                if prev is not None:
                    wt("T", "S", prev)
                t3t = op("T", lambda e, g=g, a=a, si=si, pq=pq: e.matmul(
                    PSB[:, (si % 2) * 512:(si % 2) * 512 + 512], BDQ[:, g, :],
                    KGP[pq][:, a:a + 512], start=True, stop=True), "T")
                wt("S", "T", t3t)
                k3eS[(g, si)] = op("S", lambda e, g=g, si=si, pq=pq: e.activation(
                    out=K3EP[pq][:, si * 512:si * 512 + 512],
                    in_=PSB[:, (si % 2) * 512:(si % 2) * 512 + 512],
                    func=AF.Exp, accum_out=sums3p[:, g, si:si + 1]), "S")

            for half in range(2):
                base = half * 8
                wt("T", "S", k3eS[(g, 2 * half + 1)])
                prev_ev = k3tV.get((g, half - 1)) if half >= 1 else k3tV.get((g - 1, 1))
                if prev_ev is not None:
                    wt("T", "V", prev_ev)
                def mm_tr3(e, base=base, pq=pq):
                    ins = None
                    for j in range(8):
                        oc = base + j
                        ins = e.transpose(PST[:, j * 128:j * 128 + 128],
                                          K3EP[pq][:, oc * 128:oc * 128 + 128], identbf[:])
                    return ins
                tr = op("T", mm_tr3, "T")
                wt("V", "T", tr)
                if g >= 2 and half == 0:
                    wt("V", "T", k3vT[g - 2])
                k3tV[(g, half)] = op("V", lambda e, base=base, pq=pq: e.tensor_copy(
                    out=K3TP[pq][:, base:base + 8, :],
                    in_=PST[:].rearrange("p (a b) -> p a b", b=128)), "V")
            for half in range(2):
                base = half * 8
                wt("T", "S", vevS[g])
                if half == 0:
                    wt("T", "V", k3tV[(g, 1)])
                    prev_ev = vtokS.get((g - 1, 1))
                else:
                    prev_ev = vtokS.get((g, 0))
                if prev_ev is not None:
                    wt("T", "S", prev_ev)
                def mm_trv(e, base=base, pq=pq):
                    ins = None
                    for j in range(8):
                        a0 = CHO16[base + j][0]
                        ins = e.transpose(PST[:, j * 128:j * 128 + 128],
                                          VGP[pq][:, a0:a0 + 128], identbf[:])
                    return ins
                trv = op("T", mm_trv, "T")
                wt("S", "T", trv)
                if g >= 2 and half == 0:
                    wt("S", "T", k3vT[g - 2])
                vtokS[(g, half)] = op("S", lambda e, base=base, pq=pq: e.copy(
                    out=VTOKP[pq][:, base:base + 8, :],
                    in_=PST[:].rearrange("p (a b) -> p a b", b=128)), "S")

            wt("T", "V", k3tV[(g, 1)])
            wt("T", "S", vtokS[(g, 1)])
            def mm_k3v(e, g=g, pq=pq):
                ins = None
                for oc in range(16):
                    ins = e.matmul(PSC[:, g * 128:g * 128 + 128], K3TP[pq][:, oc, :],
                                   VTOKP[pq][:, oc, :], start=(oc == 0), stop=(oc == 15))
                return ins
            k3vT[g] = op("T", mm_k3v, "T")

        stage_Q(0)
        stage_Q(1)
        for g in range(8):
            stage_R(g)
            if g + 2 < 8:
                stage_Q(g + 2)

        QALL_S = cnt["S"]

        wt("V", "T", k3vT[7])
        op("V", lambda e: e.tensor_copy(
            out=CCS[0:64, 0:512].rearrange("p (g x) -> p g x", x=64),
            in_=PSC[0:64, :].rearrange("p (g x) -> p g x", x=128)[:, :, 0:64]))
        op("V", lambda e: e.tensor_copy(
            out=CCS[64:128, 0:512].rearrange("p (g x) -> p g x", x=64),
            in_=PSC[64:128, :].rearrange("p (g x) -> p g x", x=128)[:, :, 64:128]))
        wt("V", "S", k3eS[(7, 3)])
        CCS_V = op("V", lambda e: e.tensor_reduce(
            out=CCS[:, 512:520], in_=sums3p[:], axis=AX.X, op=ALU.add), "V")
        wt("G", "V", CCS_V)
        op("G", lambda e: e.dma_start(out=cc4in.ap(), in_=CCS[:, 0:520]), "dG", 16)
        wt("G", "dG", cnt["dG"])
        CC4 = op("G", lambda e: e.collective_compute(
            "AllReduce", ALU.add, replica_groups=RG,
            ins=[cc4in.ap().opt()], outs=[cc4out.ap().opt()]), "CC", 1)
        wt("G", "CC", CC4)
        CCR_dG = op("G", lambda e: e.dma_start(out=CCR[:, 0:520], in_=cc4out.ap()), "dG", 16)

        snap(3, ("dG", CCR_dG), CCR)
        wt("G", "T", k3vT[7])
        WPROJ_dG = op("G", lambda e: e.dma_start(
            out=WPROJ[:], in_=wproj_d.rearrange("(c p) d -> p c d", p=128)), "dG", 16)

        wt("S", "T", l2T[7])
        wt("S", "T", k3vT[7])
        K2e_S = op("S", lambda e: e.activation(out=K2E[:], in_=PSD[:].rearrange("p (g x) -> p g x", x=64),
                                               func=AF.Exp), "S")
        wt("V", "S", K2e_S)
        op("V", lambda e: e.tensor_reduce(out=rc8[:], in_=K2E[:], axis=AX.X, op=ALU.add))
        op("V", lambda e: e.drain())
        op("V", lambda e: e.reciprocal(out=rc8[:], in_=rc8[:]))
        op("V", lambda e: e.drain())
        K2n_V = op("V", lambda e: e.tensor_tensor(out=K2E[:], in0=K2E[:],
                                                  in1=rc8[:].unsqueeze(2).broadcast_to([128, 8, 64]),
                                                  op=ALU.mult), "V")
        wt("T", "V", K2n_V)
        def mm_cs(e):
            ins = None
            for g in range(8):
                ins = e.matmul(PSB[0:2, g * 64:g * 64 + 64], onesbd2[:], K2E[:, g, :],
                               start=True, stop=True)
            return ins
        cs_T = op("T", mm_cs, "T")
        wt("V", "T", cs_T)
        op("V", lambda e: e.tensor_copy(out=cs2[:], in_=PSB[0:2, 0:512].rearrange("p (g x) -> p g x", x=64)))
        op("V", lambda e: e.tensor_reduce(out=mx2[:], in_=cs2[:], axis=AX.X, op=ALU.max))
        op("V", lambda e: e.drain())
        mx_V = op("V", lambda e: e.reciprocal(out=mx2[:], in_=mx2[:]), "V")
        op("V", lambda e: e.drain())
        wt("T", "V", mx_V)
        dn_T = op("T", lambda e: e.matmul(PSB[:, 512:520], b2c[:], mx2[:, :], start=True, stop=True), "T")
        wt("V", "T", dn_T)
        dn_V = op("V", lambda e: e.tensor_copy(out=denr[:], in_=PSB[:, 512:520]), "V")
        op("V", lambda e: e.drain())

        op("V", lambda e: e.memset(BDP[:], 0.0))
        op("V", lambda e: e.tensor_copy(out=BDP[0:64, :, 0:64], in_=K2E[0:64, :, :]))
        bdk_V = op("V", lambda e: e.tensor_copy(out=BDP[64:128, :, 64:128], in_=K2E[64:128, :, :]), "V")
        wt("T", "V", bdk_V)
        def mm_trk(e):
            ins = None
            for g in range(8):
                ins = e.transpose(PST[:, g * 128:g * 128 + 128], BDP[:, g, :], identbf[:])
            return ins
        trk_T = op("T", mm_trk, "T")
        wt("V", "T", trk_T)
        op("V", lambda e: e.tensor_copy(out=KTBD[:], in_=PST[:].rearrange("p (a b) -> p a b", b=128)))
        wt("V", "V", dn_V)
        op("V", lambda e: e.tensor_tensor(out=VS_[0:64, :, :], in0=KTBD[0:64, :, 0:64],
                                          in1=denr[0:64, :].unsqueeze(2).broadcast_to([64, 8, 64]), op=ALU.mult))
        op("V", lambda e: e.tensor_tensor(out=VS_[64:128, :, :], in0=KTBD[64:128, :, 64:128],
                                          in1=denr[64:128, :].unsqueeze(2).broadcast_to([64, 8, 64]), op=ALU.mult))
        itV = op("V", lambda e: e.tensor_tensor(out=VTBD[:], in0=BDP[:],
                                                in1=denr[:].unsqueeze(2).broadcast_to([128, 8, 128]), op=ALU.mult), "V")

        pst_user = [None]

        def newton_iter(it, state):
            itV = state["itV"]
            wt("T", "V", itV)
            if it == 0:
                wt("T", "V", CCS_V)
            def mm_P(e):
                ins = None
                for g in range(8):
                    ins = e.matmul(PSC[:, g * 64:g * 64 + 64], KTBD[:, g, :], VS_[:, g, :],
                                   start=True, stop=True)
                return ins
            P_T = op("T", mm_P, "T")
            wt("V", "T", P_T)
            ns1_V = op("V", lambda e: e.scalar_tensor_tensor(
                out=NS1[:], in0=PSC[:, 0:512].rearrange("p (g x) -> p g x", x=64),
                scalar=-1.0, in1=ints[:, 0, :].unsqueeze(1).broadcast_to([128, 8, 64]),
                op0=ALU.mult, op1=ALU.add), "V")
            wt("S", "T", P_T)
            op("S", lambda e: e.copy(out=BDP[0:64, :, 0:64],
                                     in_=PSC[0:64, 0:512].rearrange("p (g x) -> p g x", x=64)))
            bdp_V = op("S", lambda e: e.copy(out=BDP[64:128, :, 64:128],
                                             in_=PSC[64:128, 0:512].rearrange("p (g x) -> p g x", x=64)), "S")
            wt("T", "S", bdp_V)
            if pst_user[0] is not None:
                wt("T", pst_user[0][0], pst_user[0][1])
            def mm_trp(e):
                ins = None
                for g in range(8):
                    ins = e.transpose(PST[:, g * 128:g * 128 + 128], BDP[:, g, :], identbf[:])
                return ins
            trp_T = op("T", mm_trp, "T")
            wt("S", "T", trp_T)
            ptbd_V = op("S", lambda e: e.copy(out=PTBD[:], in_=PST[:].rearrange("p (a b) -> p a b", b=128)), "S")
            pst_user[0] = ("S", ptbd_V)
            wt("T", "S", ptbd_V)
            wt("T", "V", ns1_V)
            def mm_T2(e):
                ins = None
                for g in range(8):
                    ins = e.matmul(PSC[:, 512 + g * 64:512 + g * 64 + 64], PTBD[:, g, :], NS1[:, g, :],
                                   start=True, stop=True)
                return ins
            T2_T = op("T", mm_T2, "T")
            wt("V", "T", T2_T)
            s2_V = op("V", lambda e: e.scalar_tensor_tensor(
                out=NS2[:], in0=PSC[:, 512:1024].rearrange("p (g x) -> p g x", x=64),
                scalar=-1.0, in1=ints[:, 1, :].unsqueeze(1).broadcast_to([128, 8, 64]),
                op0=ALU.mult, op1=ALU.add), "V")
            wt("T", "V", s2_V)
            wt("T", "V", ns1_V)
            wt("T", "S", bdp_V)
            def mm_T3(e):
                ins = None
                for g in range(8):
                    ins = e.matmul(PSC[:, g * 64:g * 64 + 64], PTBD[:, g, :], NS2[:, g, :],
                                   start=True, stop=True)
                return ins
            T3_T = op("T", mm_T3, "T")
            wt("V", "T", T3_T)
            s3_V = op("V", lambda e: e.scalar_tensor_tensor(
                out=NS3[:], in0=PSC[:, 0:512].rearrange("p (g x) -> p g x", x=64),
                scalar=-1.0, in1=ints[:, 2, :].unsqueeze(1).broadcast_to([128, 8, 64]),
                op0=ALU.mult, op1=ALU.add), "V")
            wt("T", "V", s3_V)
            def mm_Vn(e):
                ins = None
                for g in range(8):
                    ins = e.matmul(PSC[:, 512 + g * 64:512 + g * 64 + 64], VTBD[:, g, :], NS3[:, g, :],
                                   start=True, stop=True)
                return ins
            Vn_T = op("T", mm_Vn, "T")
            wt("V", "T", Vn_T)
            op("V", lambda e: e.tensor_scalar(out=VS_[:], in0=PSC[:, 512:1024].rearrange("p (g x) -> p g x", x=64),
                                              scalar1=0.25, scalar2=None, op0=ALU.mult))
            op("V", lambda e: e.tensor_scalar(out=BDP[0:64, :, 0:64],
                                              in0=PSC[0:64, 512:1024].rearrange("p (g x) -> p g x", x=64),
                                              scalar1=0.25, scalar2=None, op0=ALU.mult))
            bdv_V = op("V", lambda e: e.tensor_scalar(out=BDP[64:128, :, 64:128],
                                                      in0=PSC[64:128, 512:1024].rearrange("p (g x) -> p g x", x=64),
                                                      scalar1=0.25, scalar2=None, op0=ALU.mult), "V")
            wt("T", "V", bdv_V)
            if pst_user[0] is not None:
                wt("T", pst_user[0][0], pst_user[0][1])
            def mm_trv2(e):
                ins = None
                for g in range(8):
                    ins = e.transpose(PST[:, g * 128:g * 128 + 128], BDP[:, g, :], identbf[:])
                return ins
            trv2_T = op("T", mm_trv2, "T")
            wt("S", "T", trv2_T)
            itS = op("S", lambda e: e.copy(out=VTBD[:], in_=PST[:].rearrange("p (a b) -> p a b", b=128)), "S")
            pst_user[0] = ("S", itS)
            wt("V", "S", itS)
            state["itV"] = op("V", lambda e: e.engine_nop() if False else e.nop(), "V")

        wt("V", "V", qlmV[7])
        op("V", lambda e: e.tensor_copy(out=BDKLM[0:64, :, 0:64], in_=KLM[0:64, :, :]))
        bdklm_V = op("V", lambda e: e.tensor_copy(out=BDKLM[64:128, :, 64:128], in_=KLM[64:128, :, :]), "V")
        e1S = {}
        e1V = {}
        trS = {}

        tr1T = {}

        def l1_mm(ci):
            c0, c1 = CH17[ci]
            wc = c1 - c0
            pg = ci % 2
            s3 = ci % 3
            PSL = PSA if pg == 0 else PSB
            wt("T", "V", bdklm_V)
            if ci == 0:
                wt("T", "S", QALL_S)
                wt("T", "V", kevV[7])
            if ci == 1:
                wt("T", "S", k3eS[(7, 3)])
                wt("T", "V", dn_V)
            prev = e1S.get(ci - 2)
            if prev is not None:
                wt("T", "S", prev)
            def mm_l1(e, c0=c0, wc=wc, PSL=PSL):
                ins = None
                for g in range(8):
                    ins = e.matmul(PSL[0:wc, g * 128:g * 128 + 128], QALL[:, g, c0:c0 + wc],
                                   BDKLM[:, g, :], start=True, stop=True)
                return ins
            e1T = op("T", mm_l1, "T")
            wt("S", "T", e1T)
            prevv = e1V.get(ci - 3)
            if prevv is not None:
                wt("S", "V", prevv)
            if ci - 3 in tr1T:
                wt("S", "T", tr1T[ci - 3])
            e1S[ci] = op("S", lambda e, wc=wc, s3=s3, PSL=PSL: e.activation(
                out=E1B[s3][0:wc, :], in_=PSL[0:wc, :], func=AF.Exp, scale=0.125), "S")
            wt("V", "S", e1S[ci])
            op("V", lambda e, wc=wc, s3=s3: e.tensor_reduce(
                out=rc16[0:wc, s3 * 16:s3 * 16 + 16],
                in_=E1B[s3][0:wc, :].rearrange("p (h x) -> p h x", x=64),
                axis=AX.X, op=ALU.add))

        def l1_recip(ci, guard=False):
            wc = CH17[ci][1] - CH17[ci][0]
            s3 = ci % 3
            if guard:
                op("V", lambda e: e.drain())
            op("V", lambda e, wc=wc, s3=s3: e.reciprocal(
                out=rc16[0:wc, s3 * 16:s3 * 16 + 16], in_=rc16[0:wc, s3 * 16:s3 * 16 + 16]))

        def l1_fin(ci, guard=False):
            c0, c1 = CH17[ci]
            wc = c1 - c0
            s3 = ci % 3
            if guard:
                op("V", lambda e: e.drain())
            e1V[ci] = op("V", lambda e, wc=wc, s3=s3: e.tensor_tensor(
                out=E1B[s3][0:wc, :].rearrange("p (h x) -> p h x", x=64),
                in0=E1B[s3][0:wc, :].rearrange("p (h x) -> p h x", x=64),
                in1=rc16[0:wc, s3 * 16:s3 * 16 + 16].unsqueeze(2).broadcast_to([wc, 16, 64]),
                op=ALU.mult), "V")
            wt("T", "V", e1V[ci])
            if pst_user[0] is not None:
                wt("T", pst_user[0][0], pst_user[0][1])
            def mm_tr1(e, wc=wc, s3=s3):
                ins = None
                for g in range(8):
                    ins = e.transpose(PST[:, g * wc:g * wc + wc],
                                      E1B[s3][0:wc, g * 128:g * 128 + 128], identbf[0:wc, 0:wc])
                return ins
            tr1T[ci] = op("T", mm_tr1, "T")
            wt("S", "T", tr1T[ci])
            trS[ci] = op("S", lambda e, c0=c0, wc=wc: e.copy(
                out=K1T[:, :, c0:c0 + wc],
                in_=PST[:, 0:8 * wc].rearrange("p (g x) -> p g x", x=wc)), "S")
            pst_user[0] = ("S", trS[ci])

        nw_state = {"itV": itV}
        l1_sched = [[0, 1], [2, 3], [4, 5], [6, 7], [8, 9], [10, 11, 12], [13, 14, 15, 16]]
        for step in range(7):
            for ci in l1_sched[step]:
                l1_mm(ci)
                if ci - 1 >= 0:
                    l1_recip(ci - 1)
                if ci - 2 >= 0:
                    l1_fin(ci - 2)
            if step < 6:
                newton_iter(step, nw_state)
        l1_fin(15)
        l1_recip(16)
        l1_fin(16, guard=True)
        NEWT_V = nw_state["itV"]
        K1T_S = cnt["S"]
        snap(4, ("S", K1T_S), K1T[:, :, 0:2048])

        wt("V", "dG", CCR_dG)
        op("V", lambda e: e.reciprocal(out=s3tot[:], in_=CCR[:, 512:520]))
        op("V", lambda e: e.drain())
        k3vn_V = op("V", lambda e: e.tensor_tensor(
            out=K3VN[:], in0=CCR[:, 0:512].rearrange("p (g x) -> p g x", x=64),
            in1=s3tot[:].unsqueeze(2).broadcast_to([128, 8, 64]), op=ALU.mult), "V")
        wt("T", "V", k3vn_V)
        wt("T", "V", NEWT_V)
        def mm_Z(e):
            ins = None
            for g in range(8):
                ins = e.matmul(PSD[:, g * 64:g * 64 + 64], VTBD[:, g, :], K3VN[:, g, :],
                               start=True, stop=True)
            return ins
        Z_T = op("T", mm_Z, "T")
        wt("V", "T", Z_T)
        op("V", lambda e: e.tensor_copy(out=BDQ[0:64, :, 0:64],
                                        in_=PSD[0:64, :].rearrange("p (g x) -> p g x", x=64)))
        bdz_V = op("V", lambda e: e.tensor_copy(out=BDQ[64:128, :, 64:128],
                                                in_=PSD[64:128, :].rearrange("p (g x) -> p g x", x=64)), "V")

        svS = {}
        flat = []
        for g in range(8):
            for si, (a, b) in enumerate(SL5):
                flat.append((g, si, a, b))
        for n, (g, si, a, b) in enumerate(flat):
            w = b - a
            half = (n % 2) * 512
            wt("T", "V", bdz_V)
            wt("T", "S", K1T_S)
            if n >= 2:
                wt("T", "S", svS[n - 2])
            def mm_sv(e, g=g, a=a, w=w, half=half):
                return e.matmul(PSA[:, half:half + w], BDQ[:, g, :], K1T[:, g, a:a + w],
                                start=True, stop=True)
            svT = op("T", mm_sv, "T")
            wt("S", "T", svT)
            svS[n] = op("S", lambda e, g=g, a=a, w=w, half=half: e.copy(
                out=SVALL[:, g, a:a + w], in_=PSA[:, half:half + w]), "S")
        SV_S = cnt["S"]

        wt("Y", "S", SV_S)
        XRL_dY = op("Y", lambda e: e.dma_start(out=X[:], in_=xsp), "dY", 16)

        projV = {}
        flat = []
        for dp in range(NCH):
            for si, (a, b) in enumerate(SL5):
                flat.append((dp, si, a, b))
        for n, (dp, si, a, b) in enumerate(flat):
            w = b - a
            half = (n % 2) * 512
            if n == 0:
                wt("T", "S", SV_S)
                wt("T", "dG", WPROJ_dG)
            if n >= 2:
                wt("T", "V", projV[n - 2])
            def mm_pj(e, dp=dp, a=a, w=w, half=half):
                ins = None
                for c in range(NCH):
                    ins = e.matmul(PSB[:, half:half + w], WPROJ[:, c, dp * 128:dp * 128 + 128],
                                   SVALL[:, c, a:a + w], start=(c == 0), stop=(c == NCH - 1))
                return ins
            pjT = op("T", mm_pj, "T")
            wt("V", "T", pjT)
            if n == 0:
                wt("V", "dY", XRL_dY)
            projV[n] = op("V", lambda e, dp=dp, a=a, w=w, half=half: e.tensor_tensor(
                out=X[:, dp, a:a + w], in0=X[:, dp, a:a + w], in1=PSB[:, half:half + w],
                op=ALU.add), "V")
        MASK_V = op("V", lambda e: e.tensor_scalar(out=X[:, :, 0:2], in0=X[:, :, 0:2],
                                                   scalar1=bmask_sb[:, 0:1], scalar2=None,
                                                   op0=ALU.mult), "V")
        PROJ_T = cnt["T"]
        snap(5, ("V", MASK_V), X[:, :, 2:T])

        wt("G", "T", PROJ_T)
        wt("G", "V", MASK_V)
        op("G", lambda e: e.dma_start(out=WRT, in_=wrt_d.rearrange("(c p) d -> p c d", p=128)), "dG", 16)
        op("G", lambda e: e.dma_start(out=WKT, in_=wkt_d.rearrange("(c p) d -> p c d", p=128)), "dG", 16)
        op("G", lambda e: e.dma_start(out=WVT, in_=wvt_d.rearrange("(c p) d -> p c d", p=128)), "dG", 16)
        FFNW2_dG = cnt["dG"]

        rows3_of = make_rows(None, "V", MASK_V,
                             slice_gates=[("V", MASK_V if i == 0 else projV[35 + min(4, i // 2)])
                                          for i in range(len(SL))])
        rows3_of.prefetch = 9
        rows3_of(8)
        ffn2_G, _, ffn2_addG = emit_ffn(rows3_of, FFNW2_dG, stats_S=rows3_of.last_S)

        out_pct = out_d.rearrange("(c p) t -> p c t", p=128)
        for i, (a, b) in enumerate(SL):
            a2 = max(a, 2)
            wt("Y", "G", ffn2_addG[i * NCH + NCH - 1])
            op("Y", lambda e, a2=a2, b=b: e.dma_start(
                out=out_pct[:, :, a2 - 2:b - 2], in_=X[:, :, a2:b]), "dY", 16)
        OUT_dY = cnt["dY"]
        wt("Y", "dY", OUT_dY)

        if stage in snaps:
            lens, cnts, (gsem, gval), src_ap = snaps[stage]
            import os as _os
            if _os.environ.get("KERNEL_DEBUG_SNAPS"):
                print("SNAP", stage, "lens", lens, "gate", gsem, gval, "cnts", cnts)
            for k in streams:
                del streams[k][lens[k]:]
            for k in cnt:
                cnt[k] = cnts[k]
            wt("G", gsem, gval)
            if src_ap == "ROWS121":
                dd = op("G", lambda e: e.dma_start(out=out_d[0:1, 0:2048], in_=srow[:, 2:2050]), "dG", 16)
                dd = op("G", lambda e: e.dma_start(out=out_d[1:2, 0:2048], in_=vrow[:, 2:2050]), "dG", 16)
                dd = op("G", lambda e: e.dma_start(out=out_d[2:3, 0:2048], in_=qrow[:, 2:2050]), "dG", 16)
            elif src_ap == "LNSL131":
                dd = op("G", lambda e: e.dma_start(out=out_d[0:128, 0:2048],
                                                   in_=LNSL[:].rearrange("p a b -> p (a b)")[:, 0:2048]), "dG", 16)
                dd = op("G", lambda e: e.dma_start(out=out_d[128:256, 0:257], in_=BCN[:]), "dG", 16)
                dd = op("G", lambda e: e.dma_start(out=out_d[256:384, 0:257], in_=BCS[:]), "dG", 16)
            elif src_ap == "XRK132":
                dd = op("G", lambda e: e.dma_start(out=out_d[0:128, 0:2048],
                                                   in_=XR2[:].rearrange("p a b c -> p (a b c)")[:, 0:2048]), "dG", 16)
                dd = op("G", lambda e: e.dma_start(out=out_d[128:256, 0:2048],
                                                   in_=XK2[:].rearrange("p a b c -> p (a b c)")[:, 0:2048]), "dG", 16)
                dd = op("G", lambda e: e.dma_start(out=out_d[256:384, 0:2048],
                                                   in_=KACT[:].rearrange("p a b -> p (a b)")), "dG", 16)
            elif src_ap == "ATTN31":
                dd = op("G", lambda e: e.dma_start(out=out_d[0:128, 0:2048], in_=K3Eb), "dG", 16)
                dd = op("G", lambda e: e.dma_start(out=out_d[128:256, 0:1024],
                                                   in_=BDQ[:].rearrange("p a b -> p (a b)")), "dG", 16)
                dd = op("G", lambda e: e.dma_start(out=out_d[256:384, 0:512],
                                                   in_=KLM[:].rearrange("p a b -> p (a b)")), "dG", 16)
                dd = op("G", lambda e: e.dma_start(out=out_d[384:512, 0:2048], in_=KG[:, 0:2048]), "dG", 16)
                dd = op("G", lambda e: e.dma_start(out=out_d[512:640, 0:256],
                                                   in_=POOLED[:].rearrange("p a b -> p (a b)")), "dG", 16)
                dd = op("G", lambda e: e.dma_start(out=out_d[640:768, 0:512],
                                                   in_=POOLEDG[:].rearrange("p a b -> p (a b)")), "dG", 16)
            elif src_ap == "LN2BF21":
                dd = op("G", lambda e: e.dma_start(
                    out=out_d.rearrange("(c p) t -> p c t", p=128),
                    in_=LN2BF[:, :, 2:T]), "dG", 16)
            elif src_ap is CCR:
                dd = op("G", lambda e: e.dma_start(out=out_d[0:128, 0:520], in_=src_ap), "dG", 16)
            else:
                dd = op("G", lambda e: e.dma_start(
                    out=out_d.rearrange("(c p) t -> p c t", p=128), in_=src_ap), "dG", 16)
            wt("G", "dG", dd)

        @block.tensor
        def _(e):
            for f in streams["T"]:
                f(e)

        @block.vector
        def _(e):
            for f in streams["V"]:
                f(e)

        @block.scalar
        def _(e):
            for f in streams["S"]:
                f(e)

        @block.gpsimd
        def _(e):
            for f in streams["G"]:
                f(e)

        @block.sync
        def _(e):
            for f in streams["Y"]:
                f(e)

    return nc


_NC_CACHE = {}


def kernel(L0_attn_x, L0_attn_w_qkv, L0_attn_w_proj, L0_ffn_W_r, L0_ffn_W_k,
           L0_ffn_W_v, L0_ffn_mu_r, L0_ffn_mu_k):
    x = np.ascontiguousarray(np.asarray(L0_attn_x, dtype=np.float32))
    w_qkv = np.ascontiguousarray(np.asarray(L0_attn_w_qkv, dtype=np.float32))
    w_proj = np.ascontiguousarray(np.asarray(L0_attn_w_proj, dtype=np.float32))
    wrt = np.ascontiguousarray(np.asarray(L0_ffn_W_r, dtype=np.float32).T)
    wkt = np.ascontiguousarray(np.asarray(L0_ffn_W_k, dtype=np.float32).T)
    wvt = np.ascontiguousarray(np.asarray(L0_ffn_W_v, dtype=np.float32).T)
    mu = np.stack([np.asarray(L0_ffn_mu_r, dtype=np.float32).reshape(8, 128),
                   np.asarray(L0_ffn_mu_k, dtype=np.float32).reshape(8, 128)], axis=0)
    mu = np.ascontiguousarray(mu)

    ident = np.eye(128, dtype=np.float32).astype(ml_dtypes.bfloat16)
    i64 = np.concatenate([np.eye(64, dtype=np.float32)] * 2, axis=0)
    ints = np.ascontiguousarray(
        np.stack([7.0 * i64, 15.0 * i64, 13.0 * i64], axis=1).astype(ml_dtypes.bfloat16))
    onescol = np.ones((128, 1), dtype=np.float32).astype(ml_dtypes.bfloat16)
    onesbd2 = np.zeros((128, 2), dtype=np.float32)
    onesbd2[0:64, 0] = 1.0
    onesbd2[64:128, 1] = 1.0
    onesbd2 = onesbd2.astype(ml_dtypes.bfloat16)
    bconst = np.zeros((1, 2, 128), dtype=np.float32)
    bconst[0, 0, :] = -1.0 / 1024.0
    bconst[0, 1, :] = 1.0
    b2 = np.zeros((2, 128), dtype=np.float32)
    b2[0, 0:64] = 1.0
    b2[1, 64:128] = 1.0

    import os
    _stage = int(os.environ.get("KERNEL_STAGE", "0"))
    if ("nc", _stage) not in _NC_CACHE:
        _NC_CACHE[("nc", _stage)] = build_nc(_stage)
    nc = _NC_CACHE[("nc", _stage)]

    in_maps = []
    for core in range(8):
        b = core // 2
        half = core % 2
        if half == 0:
            tile = np.concatenate([np.zeros((2, 1024), np.float32), x[b, 0:2048]], axis=0)
            bm = np.zeros((128, 1), np.float32)
        else:
            tile = x[b, 2046:4096]
            bm = np.ones((128, 1), np.float32)
        xtile = np.ascontiguousarray(tile.T)
        in_maps.append({
            "xt": xtile, "wqkv": w_qkv, "wproj": w_proj, "wrt": wrt, "wkt": wkt,
            "wvt": wvt, "mud": mu, "bmaskd": bm, "identbfd": ident, "intsd": ints,
            "onescold": onescol, "onesbd2d": onesbd2, "bconstd": bconst,
            "b2d": b2,
        })

    res = run_bass_kernel_spmd(nc, in_maps, core_ids=list(range(8)))
    out = np.empty((4, 4096, 1024), dtype=np.float32)
    for core in range(8):
        b = core // 2
        half = core % 2
        o = res.results[core]["out"]
        out[b, half * 2048:(half + 1) * 2048] = o.T
    return out


# version 14
# speedup vs baseline: 1.0586x; 1.0586x over previous
import numpy as np
import ml_dtypes
import concourse.bass as bass
import concourse.mybir as mybir
from concourse.bass_utils import run_bass_kernel_spmd

F32 = mybir.dt.float32
BF16 = mybir.dt.bfloat16
AF = mybir.ActivationFunctionType
ALU = mybir.AluOpType
AX = mybir.AxisListType

T = 2050
NCH = 8
EPS = 1e-5

SL = [(i * 256, (i + 1) * 256) for i in range(8)] + [(2048, 2050)]
SL5 = [(0, 512), (512, 1024), (1024, 1536), (1536, 2048), (2048, 2050)]
SLO = [(2 + i * 512, 2 + (i + 1) * 512) for i in range(4)]
CH17 = [(i * 128, i * 128 + 128) for i in range(16)] + [(2048, 2050)]
CHO16 = [(2 + i * 128, 2 + i * 128 + 128) for i in range(16)]
RG = [[0, 1], [2, 3], [4, 5], [6, 7]]


def build_nc(stage=0):
    import os
    SKIP = set(os.environ.get("KERNEL_SKIP", "").split(","))
    nc = bass.Bass("TRN2", target_bir_lowering=False)

    xt = nc.dram_tensor("xt", [1024, T], F32, kind="ExternalInput").ap()
    wqkv = nc.dram_tensor("wqkv", [1024, 3072], F32, kind="ExternalInput").ap()
    wproj_d = nc.dram_tensor("wproj", [1024, 1024], F32, kind="ExternalInput").ap()
    wrt_d = nc.dram_tensor("wrt", [1024, 1024], F32, kind="ExternalInput").ap()
    wkt_d = nc.dram_tensor("wkt", [1024, 1024], F32, kind="ExternalInput").ap()
    wvt_d = nc.dram_tensor("wvt", [1024, 1024], F32, kind="ExternalInput").ap()
    mu_d = nc.dram_tensor("mud", [2, 8, 128], F32, kind="ExternalInput").ap()
    bmask_d = nc.dram_tensor("bmaskd", [128, 1], F32, kind="ExternalInput").ap()
    identbf_d = nc.dram_tensor("identbfd", [128, 128], BF16, kind="ExternalInput").ap()
    ints_d = nc.dram_tensor("intsd", [128, 3, 64], BF16, kind="ExternalInput").ap()
    onescol_d = nc.dram_tensor("onescold", [128, 1], BF16, kind="ExternalInput").ap()
    onesbd2_d = nc.dram_tensor("onesbd2d", [128, 2], BF16, kind="ExternalInput").ap()
    bconst_d = nc.dram_tensor("bconstd", [1, 2, 128], F32, kind="ExternalInput").ap()
    b2_d = nc.dram_tensor("b2d", [2, 128], F32, kind="ExternalInput").ap()

    out_d = nc.dram_tensor("out", [1024, 2048], F32, kind="ExternalOutput").ap()

    xsp = nc.dram_tensor("xsp", [128, NCH, T], F32).ap()
    cc3in = nc.dram_tensor("cc3in", [1024, 32], BF16)
    cc3out = nc.dram_tensor("cc3out", [2, 1024, 32], BF16)
    cc4in = nc.dram_tensor("cc4in", [128, 520], F32)
    cc4out = nc.dram_tensor("cc4out", [128, 520], F32)

    streams = {e: [] for e in ("T", "V", "S", "G", "Y")}
    cnt = {"T": 0, "V": 0, "S": 0, "G": 0, "dY": 0, "dG": 0, "CC": 0}
    semh = {}

    def op(eng, fn, inc=None, n=1):
        if inc is None:
            streams[eng].append(lambda e, fn=fn: fn(e))
            return None
        cnt[inc] += n
        def thunk(e, fn=fn, inc=inc, n=n):
            fn(e).then_inc(semh[inc], n)
        streams[eng].append(thunk)
        return cnt[inc]

    def wt(eng, sem, val):
        if val is not None and val > 0:
            streams[eng].append(lambda e, sem=sem, val=val: e.wait_ge(semh[sem], val))

    from contextlib import ExitStack
    with ExitStack() as stack:
        ec = stack.enter_context
        XRb = ec(nc.sbuf_tensor("XR", [128, 16400], F32))
        WA = ec(nc.sbuf_tensor("WA", [128, 24576], BF16))
        A3 = ec(nc.sbuf_tensor("A3", [128, 16400], BF16))
        SCR = ec(nc.sbuf_tensor("SCR", [128, 4618], F32))
        GT2 = ec(nc.sbuf_tensor("GT2", [128, 4100], BF16))
        ROWS = ec(nc.sbuf_tensor("ROWS", [128, 4, 2050], F32))
        identbf = ec(nc.sbuf_tensor("identbf", [128, 128], BF16))
        ints = ec(nc.sbuf_tensor("ints", [128, 3, 64], BF16))
        onescol = ec(nc.sbuf_tensor("onescol", [128, 1], BF16))
        onesbd2 = ec(nc.sbuf_tensor("onesbd2", [128, 2], BF16))
        bconst = ec(nc.sbuf_tensor("bconst", [1, 2, 128], F32))
        b2c = ec(nc.sbuf_tensor("b2c", [2, 128], F32))
        mu_sb = ec(nc.sbuf_tensor("mu_sb", [128, 2, 8], F32))
        omu_sb = ec(nc.sbuf_tensor("omu_sb", [128, 2, 8], F32))
        bmask_sb = ec(nc.sbuf_tensor("bmask_sb", [128, 1], F32))
        sums3p = ec(nc.sbuf_tensor("sums3p", [128, 8, 4], F32))
        s3tot = ec(nc.sbuf_tensor("s3tot", [128, 8], F32))
        denr = ec(nc.sbuf_tensor("denr", [128, 8], F32))
        rc16 = ec(nc.sbuf_tensor("rc16", [128, 48], F32))
        rc8 = ec(nc.sbuf_tensor("rc8", [128, 8], F32))
        cs2 = ec(nc.sbuf_tensor("cs2", [2, 8, 64], F32))
        mx2 = ec(nc.sbuf_tensor("mx2", [2, 8], F32))
        PSA = ec(nc.psum_tensor("PSA", [128, 1024], F32))
        PSB = ec(nc.psum_tensor("PSB", [128, 1024], F32))
        PSC = ec(nc.psum_tensor("PSC", [128, 1024], F32))
        PSD = ec(nc.psum_tensor("PSD", [128, 512], F32))
        PST = ec(nc.psum_tensor("PST", [128, 1024], BF16))
        semT = ec(nc.semaphore("semT"))
        semV = ec(nc.semaphore("semV"))
        semS = ec(nc.semaphore("semS"))
        semG = ec(nc.semaphore("semG"))
        dY = ec(nc.semaphore("dY"))
        dG = ec(nc.semaphore("dG"))
        scc = ec(nc.semaphore("scc"))
        block = ec(nc.Block())
        semh.update({"T": semT, "V": semV, "S": semS, "G": semG,
                     "dY": dY, "dG": dG, "CC": scc})

        def view(base, off, shape):
            n = int(np.prod(shape))
            apx = base[:, off:off + n]
            if len(shape) == 2:
                apx = apx.rearrange("p (a b) -> p a b", b=shape[1])
            elif len(shape) == 3:
                apx = apx.rearrange("p (a b c) -> p a b c", b=shape[1], c=shape[2])
            return apx

        X = view(XRb, 0, [NCH, T])
        XRbf = XRb[:, :].bitcast(BF16)
        QALL = view(XRbf, 0, [NCH, T])
        K1T = view(XRbf, NCH * T, [NCH, T])
        PL1 = view(XRb, 8200, [NCH, 1024])

        WRT = view(WA, 0, [NCH, 1024])
        WKT = view(WA, 8192, [NCH, 1024])
        WVT = view(WA, 16384, [NCH, 1024])
        WG = [view(WA, 0, [NCH, 384]), view(WA, 3072, [NCH, 384])]
        WG4 = [view(WA, 0, [NCH, 3, 128]), view(WA, 3072, [NCH, 3, 128])]
        BDQ = view(WA, 6144, [NCH, 128])
        K2E = view(WA, 7168, [NCH, 64])
        BDKLM = view(WA, 7680, [NCH, 128])
        KLM = view(WA, 8704, [NCH, 64])
        POOLED = view(WA, 9216, [NCH, 32])
        POOLEDG = view(WA, 9472, [NCH, 64])
        KG = WA[:, 9984:9984 + T]
        VG = WA[:, 12034:12034 + T]
        K3Eb = WA[:, 14084:14084 + 2048]
        K3T = view(WA, 16132, [16, 128])
        VTOK = view(WA, 18180, [16, 128])
        E1B = [WA[:, 20228:20228 + 1024], WA[:, 21252:21252 + 1024],
               WA[:, 22276:22276 + 1024]]
        KG2 = GT2[:, 0:T]
        VG2 = GT2[:, T:2 * T]
        K3Eb2 = K3Eb
        K3T2 = K3T
        VTOK2 = VTOK
        KTBD = view(WA, 0, [NCH, 128])
        PTBD = view(WA, 1024, [NCH, 128])
        VTBD = view(WA, 2048, [NCH, 128])
        VS_ = view(WA, 3072, [NCH, 64])
        NS1 = view(WA, 3584, [NCH, 64])
        NS2 = view(WA, 4096, [NCH, 64])
        NS3 = view(WA, 4608, [NCH, 64])
        BDP = view(WA, 5120, [NCH, 128])
        K3VN = view(WA, 9984, [NCH, 64])
        WPROJ = view(WA, 10496, [NCH, 1024])

        XR2 = view(A3, 0, [NCH, 2, 256])
        XK2 = view(A3, 4096, [NCH, 2, 256])
        KACT = view(A3, 8192, [NCH, 256])
        X2SL = [view(A3, 0, [NCH, 256]), view(A3, 2048, [NCH, 256])]
        SQSL = [view(A3, 4096, [NCH, 256]), view(A3, 6144, [NCH, 256])]
        LNSLB = view(A3, 10240, [NCH, 257])
        DSLB = view(A3, 12296, [NCH, 256])
        LN2BF = view(A3, 0, [NCH, T])
        SVALL = view(A3, 0, [NCH, T])

        LNSL = view(SCR, 0, [NCH, 257])
        T1DB = view(SCR, 2056, [2, 256])
        KTMP2 = view(SCR, 2568, [2, 256])
        RCH2 = view(SCR, 3080, [2, 256])
        TCH2 = view(SCR, 3592, [2, 256])
        BCN = SCR[:, 4104:4104 + 257]
        BCS = SCR[:, 4361:4361 + 257]
        LNSCR = view(SCR, 0, [NCH, 256])
        PL2 = view(SCR, 0, [NCH, 512])
        PL3 = view(XRb, 8200, [NCH, 256])
        PL4 = view(SCR, 0, [NCH, 128])
        PL5 = view(XRb, 8200 + 4096, [NCH, 64])
        PL6 = view(SCR, 1024, [NCH, 32])
        CCS = SCR[:, 0:520]
        CCR = SCR[:, 520:1040]

        srow = ROWS[0:1, 0, :]
        qrow = ROWS[0:1, 1, :]
        mrow = ROWS[0:1, 2, :]
        vrow = ROWS[0:1, 3, :]

        op("Y", lambda e: e.dma_start(out=X[:], in_=xt.rearrange("(c p) t -> p c t", p=128)), "dY", 16)
        for src, dst in [(identbf_d, identbf), (ints_d, ints), (onescol_d, onescol),
                         (onesbd2_d, onesbd2), (bconst_d, bconst),
                         (b2_d, b2c), (bmask_d, bmask_sb)]:
            op("Y", lambda e, s=src, d=dst: e.dma_start(out=d[:], in_=s), "dY", 16)
        (op("Y", lambda e: e.dma_start(out=mu_sb[:], in_=mu_d.rearrange("i c p -> p i c"),
                                      allow_slow_non_contiguous=True), "dY", 16)
         if "mu" not in SKIP else op("Y", lambda e: e.dma_start(out=mu_sb[:, 0, 0:1], in_=mu_d[0:1, 0, 0:1].transpose([1, 0]) if False else mu_d[0, 0:1, 0:1]), "dY", 16))
        PRO_dY = cnt["dY"]

        def wg_load_thunks(g, buf):
            outs = []
            for t3 in range(3):
                col = t3 * 1024 + g * 128
                outs.append(lambda e, t3=t3, col=col, buf=buf: e.dma_start(
                    out=view(WA, buf * 3072 + t3 * 128, [NCH, 128]) if False else WG4[buf][:, :, t3, :],
                    in_=wqkv[:, col:col + 128].rearrange("(c p) d -> p c d", p=128)))
            return outs

        wt("V", "dY", PRO_dY)
        op("V", lambda e: e.tensor_scalar(out=omu_sb[:], in0=mu_sb[:], scalar1=-1.0,
                                          scalar2=1.0, op0=ALU.mult, op1=ALU.add))

        SCRbf = SCR[:, :].bitcast(BF16)
        X2SLB = [GT2[:, 0:2048].rearrange("p (a b) -> p a b", b=256),
                 GT2[:, 2048:4096].rearrange("p (a b) -> p a b", b=256)]
        SQSLB = [SCRbf[:, 4112:4112 + 2048].rearrange("p (a b) -> p a b", b=256),
                 SCRbf[:, 6160:6160 + 2048].rearrange("p (a b) -> p a b", b=256)]

        def make_rows(src_dram, gate_sem, gate_val, slice_gates=None, slots=None):
            SX, SQ = slots if slots is not None else (X2SL, SQSL)
            """Lazy per-slice LN stats. Returns rows_of(i) -> (sem, tick): emits
            stats slices up to i+2 and per-slice finishes up to i on demand, so
            the T stream interleaves stat matmuls with consuming bcasts."""
            sum_T = {}
            sq_T = {}
            ev_sum_S = {}
            ev_sq_S = {}
            castG = {}
            sqV = {}
            fin_tick = {}

            def emit_stats(i):
                a, b = SL[i]
                w = b - a
                pg = i % 2
                if slice_gates is not None:
                    wt("G", slice_gates[i][0], slice_gates[i][1])
                elif gate_sem is not None and i == 0:
                    wt("G", gate_sem, gate_val)
                if i >= 2:
                    wt("G", "T", sum_T[i - 2])
                    wt("G", "S", sqV[i - 2])
                if src_dram is not None:
                    castG[i] = op("G", lambda e, a=a, w=w, pg=pg: e.dma_start(
                        out=SX[pg][:, :, 0:w],
                        in_=src_dram.rearrange("(c p) t -> p c t", p=128)[:, :, a:a + w]), "dG", 16)
                else:
                    castG[i] = op("G", lambda e, a=a, w=w, pg=pg: e.dma_start(
                        out=SX[pg][:, :, 0:w], in_=X[:, :, a:a + w]), "dG", 16)
                wt("S", "dG", castG[i])
                if i >= 2:
                    wt("S", "T", sq_T[i - 2])
                sqV[i] = op("S", lambda e, w=w, pg=pg: e.activation(
                    out=SQ[pg][:, :, 0:w], in_=SX[pg][:, :, 0:w],
                    func=AF.Square), "S")
                wt("T", "dG", castG[i])
                if i == 0:
                    wt("T", "dY", PRO_dY)
                if i >= 2:
                    wt("T", "S", ev_sum_S[i - 2])
                def mm_sum(e, w=w, pg=pg):
                    ins = None
                    for kc in range(NCH):
                        ins = e.matmul(PSC[0:1, pg * 512:pg * 512 + w], onescol[:],
                                       SX[pg][:, kc, 0:w], start=(kc == 0), stop=(kc == NCH - 1))
                    return ins
                sum_T[i] = op("T", mm_sum, "T")
                wt("T", "S", sqV[i])
                if i >= 2:
                    wt("T", "S", ev_sq_S[i - 2])
                def mm_sq(e, w=w, pg=pg):
                    ins = None
                    for kc in range(NCH):
                        ins = e.matmul(PSB[0:1, pg * 512:pg * 512 + w], onescol[:],
                                       SQ[pg][:, kc, 0:w], start=(kc == 0), stop=(kc == NCH - 1))
                    return ins
                sq_T[i] = op("T", mm_sq, "T")
                wt("S", "T", sum_T[i])
                ev_sum_S[i] = op("S", lambda e, a=a, w=w, pg=pg: e.copy(
                    out=srow[:, a:a + w], in_=PSC[0:1, pg * 512:pg * 512 + w]), "S")
                wt("S", "T", sq_T[i])
                ev_sq_S[i] = op("S", lambda e, a=a, w=w, pg=pg: e.copy(
                    out=qrow[:, a:a + w], in_=PSB[0:1, pg * 512:pg * 512 + w]), "S")

            def emit_finish(k):
                a = SL[k][0]
                b = 2050 if k == 7 else SL[k][1]
                wt("V", "S", ev_sq_S[8 if k == 7 else k])
                op("V", lambda e, a=a, b=b: e.tensor_tensor(
                    out=vrow[:, a:b], in0=srow[:, a:b], in1=srow[:, a:b], op=ALU.mult))
                op("V", lambda e, a=a, b=b: e.scalar_tensor_tensor(
                    out=vrow[:, a:b], in0=vrow[:, a:b], scalar=1.0 / 1024.0,
                    in1=qrow[:, a:b], op0=ALU.mult, op1=ALU.subtract))
                epsV = op("V", lambda e, a=a, b=b: e.tensor_scalar(
                    out=vrow[:, a:b], in0=vrow[:, a:b], scalar1=-1.0 / 1024.0,
                    scalar2=EPS, op0=ALU.mult, op1=ALU.add), "V")
                wt("S", "V", epsV)
                sdS = op("S", lambda e, a=a, b=b: e.activation(
                    out=vrow[:, a:b], in_=vrow[:, a:b], func=AF.Sqrt), "S")
                wt("V", "S", sdS)
                fin_tick[k] = op("V", lambda e, a=a, b=b: e.reciprocal(
                    out=vrow[:, a:b], in_=vrow[:, a:b]), "V")

            def rows_of(i):
                k = min(i, 7)
                for j in range(len(castG), min(i + 1 + rows_of.prefetch, 9)):
                    emit_stats(j)
                for j in range(len(fin_tick), k + 1):
                    if j == 7 and 8 not in castG:
                        emit_stats(8)
                    emit_finish(j)
                return ("V", fin_tick[k])

            rows_of.prefetch = 2
            rows_of.last_S = lambda: ev_sq_S[8]
            return rows_of

        def emit_bcast_slice(i, a, b, rows_of, prev_user_V, prev_user_G=None):
            w = b - a
            tick = rows_of(i)
            wt("T", tick[0], tick[1])
            if prev_user_V is not None:
                wt("T", "V", prev_user_V)
            if prev_user_G is not None:
                wt("T", "G", prev_user_G)
            def mm_bc(e, a=a, w=w):
                e.matmul(PSD[:, 0:w], bconst[0:1, 0, :], srow[:, a:a + w], start=True, stop=True)
                return e.matmul(PSD[:, 256:256 + w], bconst[0:1, 1, :], vrow[:, a:a + w], start=True, stop=True)
            t2 = op("T", mm_bc, "T")
            wt("V", "T", t2)
            op("V", lambda e, w=w: e.tensor_copy(out=BCN[:, 1:1 + w], in_=PSD[:, 0:w]))
            vv = op("V", lambda e, w=w: e.tensor_copy(out=BCS[:, 1:1 + w], in_=PSD[:, 256:256 + w]), "V")
            if w < 64:
                op("V", lambda e: e.drain())
            return vv

        def emit_ffn(rows_V, w_dG, snap_cb=None, stats_S=None):
            snap_cb2 = [snap_cb]
            bc_user = None
            reluS = {}
            kactV = {}
            sigS = {}
            rvV = {}
            addG = {}
            mixV = {}
            t1hist = []
            mix_hist = []
            vT = {}
            NRV = 0
            def stageA(i):
                nonlocal bc_user
                a, b = SL[i]
                w = b - a
                bc_user = emit_bcast_slice(i, a, b, rows_V, bc_user)
                if i >= 2:
                    wt("V", "T", vT[(i - 2, NCH - 1)])
                if i == 0:
                    op("V", lambda e: e.memset(LNSLB[:, :, 0:1], 0.0))
                else:
                    op("V", lambda e: e.tensor_copy(out=LNSLB[:, :, 0:1], in_=LNSLB[:, :, 256:257]))
                if w < 64:
                    op("V", lambda e: e.drain())
                op("V", lambda e, a=a, w=w: e.tensor_tensor(
                    out=LNSL[:, :, 1:1 + w], in0=X[:, :, a:a + w],
                    in1=BCN[:, 1:1 + w].unsqueeze(1).broadcast_to([128, NCH, w]),
                    op=ALU.add))
                if w < 64:
                    op("V", lambda e: e.drain())
                lnV = op("V", lambda e, w=w: e.tensor_tensor(
                    out=LNSLB[:, :, 1:1 + w], in0=LNSL[:, :, 1:1 + w],
                    in1=BCS[:, 1:1 + w].unsqueeze(1).broadcast_to([128, NCH, w]),
                    op=ALU.mult), "V")
                if w < 64:
                    op("V", lambda e: e.drain())
                if i == len(SL) - 1 and 15 not in snaps:
                    snaps[15] = ({k: len(v) for k, v in streams.items()}, dict(cnt),
                                 ("V", lnV), "LNSL131")
                pg = i % 2
                op("V", lambda e, w=w: e.tensor_tensor(
                    out=DSLB[:, :, 0:w], in0=LNSLB[:, :, 1:1 + w],
                    in1=LNSLB[:, :, 0:w], op=ALU.subtract))
                for mi, dst2 in [(0, XR2), (1, XK2)]:
                    for c in range(NCH):
                        op("V", lambda e, c=c, w=w, mi=mi, dst2=dst2, pg=pg:
                           e.scalar_tensor_tensor(
                               out=dst2[:, c, pg, 0:w], in0=DSLB[:, c, 0:w],
                               scalar=mu_sb[:, mi, c:c + 1], in1=LNSLB[:, c, 0:w],
                               op0=ALU.mult, op1=ALU.add), "V")
                mixV[i] = cnt["V"]

            def stageB(i):
                a, b = SL[i]
                w = b - a
                pg = i % 2
                wt("T", "V", mixV[i])
                if i == 0:
                    wt("T", "dG", w_dG[0])
                for dp in range(NCH):
                    half = (dp % 2) * 512
                    prev_relu = reluS.get((i, dp - 2)) if dp >= 2 else reluS.get((i - 1, dp + 6))
                    if prev_relu is not None:
                        wt("T", "S", prev_relu)
                    def mm_k(e, w=w, pg=pg, dp=dp, half=half):
                        ins = None
                        for c in range(NCH):
                            ins = e.matmul(PSA[:, half:half + w], WKT[:, c, dp * 128:dp * 128 + 128],
                                           XK2[:, c, pg, 0:w], start=(c == 0), stop=(c == NCH - 1))
                        return ins
                    kTt = op("T", mm_k, "T")
                    wt("S", "T", kTt)
                    op("S", lambda e, w=w, dp=dp, half=half: e.activation(
                        out=KTMP2[:, dp % 2, 0:w], in_=PSA[:, half:half + w], func=AF.Relu))
                    reluS[(i, dp)] = op("S", lambda e, w=w, dp=dp: e.activation(
                        out=KACT[:, dp, 0:w], in_=KTMP2[:, dp % 2, 0:w], func=AF.Square), "S")
                    kactV[(i, dp)] = reluS[(i, dp)]

                if i == 0:
                    wt("T", "dG", w_dG[2])
                if i == 0 and stats_S is not None:
                    wt("T", "S", stats_S())
                for dp in range(NCH):
                    half = (dp % 2) * 512
                    n = i * NCH + dp
                    prev_sig = sigS.get(n - 2)
                    if prev_sig is not None:
                        wt("T", "S", prev_sig)
                    def mm_r(e, w=w, pg=pg, dp=dp, half=half):
                        ins = None
                        for c in range(NCH):
                            ins = e.matmul(PSB[:, half:half + w], WRT[:, c, dp * 128:dp * 128 + 128],
                                           XR2[:, c, pg, 0:w], start=(c == 0), stop=(c == NCH - 1))
                        return ins
                    rTt = op("T", mm_r, "T")
                    if dp == 0:
                        wt("T", "S", kactV[(i, NCH - 1)])
                    prev_rv = rvV.get(n - 2)
                    if prev_rv is not None:
                        wt("T", "V", prev_rv)
                    def mm_v(e, w=w, dp=dp, half=half):
                        ins = None
                        for c in range(NCH):
                            ins = e.matmul(PSC[:, half:half + w], WVT[:, c, dp * 128:dp * 128 + 128],
                                           KACT[:, c, 0:w], start=(c == 0), stop=(c == NCH - 1))
                        return ins
                    vT[(i, dp)] = op("T", mm_v, "T")
                    wt("S", "T", rTt)
                    prev_rv1 = rvV.get(n - 1)
                    if prev_rv1 is not None:
                        wt("S", "V", prev_rv1)
                    sigS[n] = op("S", lambda e, w=w, dp=dp, half=half: e.activation(
                        out=RCH2[:, dp % 2, 0:w], in_=PSB[:, half:half + w], func=AF.Sigmoid), "S")
                    wt("V", "S", sigS[n])
                    wt("V", "T", vT[(i, dp)])
                    prev_add = addG.get(n - 2)
                    if prev_add is not None:
                        wt("V", "G", prev_add)
                    rvV[n] = op("V", lambda e, w=w, dp=dp, half=half: e.scalar_tensor_tensor(
                        out=TCH2[:, dp % 2, 0:w], in0=PSC[:, half:half + w], scalar=0.5,
                        in1=RCH2[:, dp % 2, 0:w], op0=ALU.mult, op1=ALU.mult), "V")
                    wt("G", "V", rvV[n])
                    addG[n] = op("G", lambda e, a=a, b=b, w=w, dp=dp: e.tensor_tensor(
                        out=X[:, dp, a:b], in0=X[:, dp, a:b], in1=TCH2[:, dp % 2, 0:w],
                        op=ALU.add), "G")
                if i == 0 and snap_cb is not None:
                    snap_cb2[0] = None

            for i in range(len(SL) + 1):
                if i < len(SL):
                    stageA(i)
                if i >= 1:
                    stageB(i - 1)
                    if i - 1 == 0 and snap_cb is not None:
                        snap_cb(cnt["G"])
            return cnt["G"], bc_user, addG

        snaps = {}
        def snap(n, gate, src_ap):
            snaps[n] = ({k: len(v) for k, v in streams.items()}, dict(cnt), gate, src_ap)

        rows1_of = make_rows(xt, None, None)
        rows1_of.prefetch = 9
        rows1_t0 = rows1_of(8)
        if "wcast" not in SKIP:
            op("G", lambda e: e.dma_start(out=WKT, in_=wkt_d.rearrange("(c p) d -> p c d", p=128)), "dG", 16)
        FFNWK_dG = cnt["dG"]
        if "wcast" not in SKIP:
            op("G", lambda e: e.dma_start(out=WRT, in_=wrt_d.rearrange("(c p) d -> p c d", p=128)), "dG", 16)
        FFNWR_dG = cnt["dG"]
        if "wcast" not in SKIP:
            op("G", lambda e: e.dma_start(out=WVT, in_=wvt_d.rearrange("(c p) d -> p c d", p=128)), "dG", 16)
        FFNW_dG = cnt["dG"]
        snap(11, ("dG", FFNW_dG), X[:, :, 2:T])
        if stage in (12, 121):
            rows1_full = rows1_of(8)
        else:
            rows1_full = rows1_t0
        snap(12, rows1_full, X[:, :, 2:T])
        snaps[121] = (snaps[12][0], snaps[12][1], rows1_full, "ROWS121")
        def _snap13(g_tick):
            snap(13, ("G", g_tick), X[:, :, 2:T])
            snaps[131] = (snaps[13][0], snaps[13][1], ("G", g_tick), "LNSL131")
            snaps[132] = (snaps[13][0], snaps[13][1], ("G", g_tick), "XRK132")
        ffn1_G, _, ffn1_addG = emit_ffn(rows1_of, (FFNWK_dG, FFNWR_dG, FFNW_dG),
                                        snap_cb=_snap13, stats_S=rows1_of.last_S)
        ffn1_T = cnt["T"]
        snaps[14] = ({k: len(v) for k, v in streams.items()}, dict(cnt), ("G", ffn1_G), "LNSL131")
        snaps[141] = ({k: len(v) for k, v in streams.items()}, dict(cnt), ("G", ffn1_G), "XRK132")

        snap(1, ("G", ffn1_G), X[:, :, 2:T])
        rows2_of = make_rows(None, "G", ffn1_G,
                             slice_gates=[("G", ffn1_addG[i * NCH + NCH - 1]) for i in range(len(SL))],
                             slots=(X2SLB, SQSLB))

        wt("Y", "G", ffn1_G)
        XSPILL_dY = op("Y", lambda e: e.dma_start(out=xsp, in_=X[:]), "dY", 16)

        bc_user = None
        ln2V = {}
        for i, (a, b) in enumerate(SL):
            w = b - a
            bc_user = emit_bcast_slice(i, a, b, rows2_of, bc_user)
            op("V", lambda e, a=a, w=w: e.tensor_tensor(
                out=LNSCR[:, :, 0:w], in0=X[:, :, a:a + w],
                in1=BCN[:, 1:1 + w].unsqueeze(1).broadcast_to([128, NCH, w]), op=ALU.add))
            if w < 64:
                op("V", lambda e: e.drain())
            bc_user = op("V", lambda e, a=a, w=w: e.tensor_tensor(
                out=LN2BF[:, :, a:a + w], in0=LNSCR[:, :, 0:w],
                in1=BCS[:, 1:1 + w].unsqueeze(1).broadcast_to([128, NCH, w]), op=ALU.mult), "V")
            if w < 64:
                op("V", lambda e: e.drain())
            ln2V[i] = bc_user
        LN2_V = cnt["V"]

        snaps[21] = ({k: len(v) for k, v in streams.items()}, dict(cnt),
                     ("V", LN2_V), "LN2BF21")
        snaps[22] = ({k: len(v) for k, v in streams.items()}, dict(cnt),
                     ("V", LN2_V), "ROWS121")
        wt("V", "dY", XSPILL_dY)
        op("V", lambda e: e.tensor_tensor(out=PL1[:], in0=LN2BF[:, :, 2:2049:2],
                                          in1=LN2BF[:, :, 3:2050:2], op=ALU.add))
        op("V", lambda e: e.tensor_tensor(out=PL2[:], in0=PL1[:, :, 0:1024:2],
                                          in1=PL1[:, :, 1:1024:2], op=ALU.add))
        op("V", lambda e: e.tensor_tensor(out=PL3[:], in0=PL2[:, :, 0:512:2],
                                          in1=PL2[:, :, 1:512:2], op=ALU.add))
        op("V", lambda e: e.tensor_tensor(out=PL4[:], in0=PL3[:, :, 0:256:2],
                                          in1=PL3[:, :, 1:256:2], op=ALU.add))
        op("V", lambda e: e.tensor_tensor(out=PL5[:], in0=PL4[:, :, 0:128:2],
                                          in1=PL4[:, :, 1:128:2], op=ALU.add))
        op("V", lambda e: e.tensor_tensor(out=PL6[:], in0=PL5[:, :, 0:64:2],
                                          in1=PL5[:, :, 1:64:2], op=ALU.add))
        POOL_V = op("V", lambda e: e.tensor_scalar(out=POOLED[:], in0=PL6[:], scalar1=1.0 / 64.0,
                                                   scalar2=None, op0=ALU.mult), "V")

        wt("G", "T", ffn1_T)
        WGL = {}
        for th in wg_load_thunks(0, 0):
            WGL[0] = op("G", th, "dG", 16)
        for th in wg_load_thunks(1, 1):
            WGL[1] = op("G", th, "dG", 16)

        wt("G", "V", POOL_V)
        op("G", lambda e: e.dma_start(out=cc3in.ap().rearrange("(c p) s -> p c s", p=128),
                                      in_=POOLED[:]), "dG", 16)
        wt("G", "dG", cnt["dG"])
        CC3 = op("G", lambda e: e.collective_compute(
            "AllGather", ALU.bypass, replica_groups=RG,
            ins=[cc3in.ap().opt()], outs=[cc3out.ap().opt()]), "CC", 1)
        wt("G", "CC", CC3)
        op("G", lambda e: e.dma_start(
            out=POOLEDG[:, :, 0:32],
            in_=cc3out.ap()[0].rearrange("(c p) s -> p c s", p=128)), "dG", 16)
        PGL_dG = op("G", lambda e: e.dma_start(
            out=POOLEDG[:, :, 32:64],
            in_=cc3out.ap()[1].rearrange("(c p) s -> p c s", p=128)), "dG", 16)

        snap(2, ("dG", PGL_dG), X[:, :, 2:T])
        wt("V", "T", ffn1_T)
        op("V", lambda e: e.memset(BDQ[:], 0.0))
        BDZERO_V = op("V", lambda e: e.memset(BDKLM[:], 0.0), "V")

        lastWG_T = {}
        qlmV = {}
        klmT = {}
        l2T = {}
        k3eS = {}
        k3tV = {}
        vtokS = {}
        k3vT = {}
        kevV = {}
        vevS = {}
        KGP = [KG, KG2]
        VGP = [VG, VG2]
        K3EP = [K3Eb, K3Eb2]
        K3TP = [K3T, K3T2]
        VTOKP = [VTOK, VTOK2]
        evhist = []

        def stage_Q(g):
            wg = WG[g % 2]
            pq = g % 2
            wt("T", "dG", WGL[g])
            for t3 in range(3):
                for si, (a, b) in enumerate(SL5):
                    w = b - a
                    half = ((t3 * 5 + si) % 2) * 512
                    if len(evhist) >= 2:
                        wt("T", evhist[-2][0], evhist[-2][1])
                    if g == 0:
                        wt("T", "V", ln2V[min(len(SL) - 1, (a + w) // 256)])
                    def mm_qkv(e, wg=wg, t3=t3, a=a, w=w, half=half):
                        ins = None
                        for kc in range(NCH):
                            ins = e.matmul(PSA[:, half:half + w], wg[:, kc, t3 * 128:t3 * 128 + 128],
                                           LN2BF[:, kc, a:a + w], start=(kc == 0), stop=(kc == NCH - 1))
                        return ins
                    tt = op("T", mm_qkv, "T")
                    if t3 == 0:
                        if g == 0 and si == 0:
                            wt("S", "dY", XSPILL_dY)
                        wt("S", "T", tt)
                        sv = op("S", lambda e, g=g, a=a, w=w, half=half: e.copy(
                            out=QALL[:, g, a:a + w], in_=PSA[:, half:half + w]), "S")
                        evhist.append(("S", sv))
                    elif t3 == 1:
                        wt("V", "T", tt)
                        if si == 0 and g >= 2:
                            wt("V", "T", k3vT[g - 2])
                        sv = op("V", lambda e, a=a, w=w, half=half, pq=pq: e.tensor_copy(
                            out=KGP[pq][:, a:a + w], in_=PSA[:, half:half + w]), "V")
                        evhist.append(("V", sv))
                    else:
                        wt("S", "T", tt)
                        if si == 0 and g >= 2:
                            wt("S", "T", k3vT[g - 2])
                        sv = op("S", lambda e, a=a, w=w, half=half, pq=pq: e.copy(
                            out=VGP[pq][:, a:a + w], in_=PSA[:, half:half + w]), "S")
                        evhist.append(("S", sv))
            kevV[g] = evhist[-6][1]
            vevS[g] = evhist[-1][1]
            lastWG_T[g] = cnt["T"]

        def stage_R(g):
            wg = WG[g % 2]
            pq = g % 2
            wt("T", "dG", PGL_dG)
            if g >= 1:
                wt("T", "V", qlmV[g - 1])
                wt("T", "S", k3eS[(g - 1, 3)])
            def mm_qlm(e, wg=wg):
                ins = None
                for kc in range(NCH):
                    ins = e.matmul(PSB[:, 512:576], wg[:, kc, 0:128], POOLEDG[:, kc, :],
                                   start=(kc == 0), stop=(kc == NCH - 1))
                return ins
            t_qlm = op("T", mm_qlm, "T")
            def mm_klm(e, wg=wg):
                ins = None
                for kc in range(NCH):
                    ins = e.matmul(PSB[:, 576:640], wg[:, kc, 128:256], POOLEDG[:, kc, :],
                                   start=(kc == 0), stop=(kc == NCH - 1))
                return ins
            klmT[g] = op("T", mm_klm, "T")
            if g + 2 < 8:
                wt("G", "T", klmT[g])
                for th in wg_load_thunks(g + 2, g % 2):
                    WGL[g + 2] = op("G", th, "dG", 16)
            wt("V", "T", klmT[g])
            wt("V", "V", BDZERO_V)
            op("V", lambda e, g=g: e.tensor_scalar(out=BDQ[0:64, g, 0:64], in0=PSB[0:64, 512:576],
                                                   scalar1=0.125, scalar2=None, op0=ALU.mult))
            op("V", lambda e, g=g: e.tensor_scalar(out=BDQ[64:128, g, 64:128], in0=PSB[64:128, 512:576],
                                                   scalar1=0.125, scalar2=None, op0=ALU.mult))
            qlmV[g] = op("V", lambda e, g=g: e.tensor_copy(out=KLM[:, g, :], in_=PSB[:, 576:640]), "V")
            wt("T", "V", qlmV[g])
            l2T[g] = op("T", lambda e, g=g: e.matmul(PSD[:, g * 64:g * 64 + 64], BDQ[:, g, :],
                                                     KLM[:, g, :], start=True, stop=True), "T")

            for si, (a, b) in enumerate(SLO):
                wt("T", "V", kevV[g])
                prev = k3eS.get((g, si - 2)) if si >= 2 else k3eS.get((g - 1, si + 2))
                if prev is not None:
                    wt("T", "S", prev)
                t3t = op("T", lambda e, g=g, a=a, si=si, pq=pq: e.matmul(
                    PSB[:, (si % 2) * 512:(si % 2) * 512 + 512], BDQ[:, g, :],
                    KGP[pq][:, a:a + 512], start=True, stop=True), "T")
                wt("S", "T", t3t)
                k3eS[(g, si)] = op("S", lambda e, g=g, si=si, pq=pq: e.activation(
                    out=K3EP[pq][:, si * 512:si * 512 + 512],
                    in_=PSB[:, (si % 2) * 512:(si % 2) * 512 + 512],
                    func=AF.Exp, accum_out=sums3p[:, g, si:si + 1]), "S")

            for half in range(2):
                base = half * 8
                wt("T", "S", k3eS[(g, 2 * half + 1)])
                prev_ev = k3tV.get((g, half - 1)) if half >= 1 else k3tV.get((g - 1, 1))
                if prev_ev is not None:
                    wt("T", "V", prev_ev)
                def mm_tr3(e, base=base, pq=pq):
                    ins = None
                    for j in range(8):
                        oc = base + j
                        ins = e.transpose(PST[:, j * 128:j * 128 + 128],
                                          K3EP[pq][:, oc * 128:oc * 128 + 128], identbf[:])
                    return ins
                tr = op("T", mm_tr3, "T")
                wt("V", "T", tr)
                if g >= 2 and half == 0:
                    wt("V", "T", k3vT[g - 2])
                k3tV[(g, half)] = op("V", lambda e, base=base, pq=pq: e.tensor_copy(
                    out=K3TP[pq][:, base:base + 8, :],
                    in_=PST[:].rearrange("p (a b) -> p a b", b=128)), "V")
            for half in range(2):
                base = half * 8
                wt("T", "S", vevS[g])
                if half == 0:
                    wt("T", "V", k3tV[(g, 1)])
                    prev_ev = vtokS.get((g - 1, 1))
                else:
                    prev_ev = vtokS.get((g, 0))
                if prev_ev is not None:
                    wt("T", "S", prev_ev)
                def mm_trv(e, base=base, pq=pq):
                    ins = None
                    for j in range(8):
                        a0 = CHO16[base + j][0]
                        ins = e.transpose(PST[:, j * 128:j * 128 + 128],
                                          VGP[pq][:, a0:a0 + 128], identbf[:])
                    return ins
                trv = op("T", mm_trv, "T")
                wt("S", "T", trv)
                if g >= 2 and half == 0:
                    wt("S", "T", k3vT[g - 2])
                vtokS[(g, half)] = op("S", lambda e, base=base, pq=pq: e.copy(
                    out=VTOKP[pq][:, base:base + 8, :],
                    in_=PST[:].rearrange("p (a b) -> p a b", b=128)), "S")

            wt("T", "V", k3tV[(g, 1)])
            wt("T", "S", vtokS[(g, 1)])
            def mm_k3v(e, g=g, pq=pq):
                ins = None
                for oc in range(16):
                    ins = e.matmul(PSC[:, g * 128:g * 128 + 128], K3TP[pq][:, oc, :],
                                   VTOKP[pq][:, oc, :], start=(oc == 0), stop=(oc == 15))
                return ins
            k3vT[g] = op("T", mm_k3v, "T")

        stage_Q(0)
        stage_Q(1)
        for g in range(8):
            stage_R(g)
            if g + 2 < 8:
                stage_Q(g + 2)

        QALL_S = cnt["S"]

        wt("V", "T", k3vT[7])
        op("V", lambda e: e.tensor_copy(
            out=CCS[0:64, 0:512].rearrange("p (g x) -> p g x", x=64),
            in_=PSC[0:64, :].rearrange("p (g x) -> p g x", x=128)[:, :, 0:64]))
        op("V", lambda e: e.tensor_copy(
            out=CCS[64:128, 0:512].rearrange("p (g x) -> p g x", x=64),
            in_=PSC[64:128, :].rearrange("p (g x) -> p g x", x=128)[:, :, 64:128]))
        wt("V", "S", k3eS[(7, 3)])
        CCS_V = op("V", lambda e: e.tensor_reduce(
            out=CCS[:, 512:520], in_=sums3p[:], axis=AX.X, op=ALU.add), "V")
        wt("G", "V", CCS_V)
        op("G", lambda e: e.dma_start(out=cc4in.ap(), in_=CCS[:, 0:520]), "dG", 16)
        wt("G", "dG", cnt["dG"])
        CC4 = op("G", lambda e: e.collective_compute(
            "AllReduce", ALU.add, replica_groups=RG,
            ins=[cc4in.ap().opt()], outs=[cc4out.ap().opt()]), "CC", 1)
        wt("G", "CC", CC4)
        CCR_dG = op("G", lambda e: e.dma_start(out=CCR[:, 0:520], in_=cc4out.ap()), "dG", 16)

        snap(3, ("dG", CCR_dG), CCR)
        wt("G", "T", k3vT[7])
        WPROJ_dG = op("G", lambda e: e.dma_start(
            out=WPROJ[:], in_=wproj_d.rearrange("(c p) d -> p c d", p=128)), "dG", 16)

        wt("S", "T", l2T[7])
        wt("S", "T", k3vT[7])
        K2e_S = op("S", lambda e: e.activation(out=K2E[:], in_=PSD[:].rearrange("p (g x) -> p g x", x=64),
                                               func=AF.Exp), "S")
        wt("V", "S", K2e_S)
        op("V", lambda e: e.tensor_reduce(out=rc8[:], in_=K2E[:], axis=AX.X, op=ALU.add))
        op("V", lambda e: e.drain())
        op("V", lambda e: e.reciprocal(out=rc8[:], in_=rc8[:]))
        op("V", lambda e: e.drain())
        K2n_V = op("V", lambda e: e.tensor_tensor(out=K2E[:], in0=K2E[:],
                                                  in1=rc8[:].unsqueeze(2).broadcast_to([128, 8, 64]),
                                                  op=ALU.mult), "V")
        wt("T", "V", K2n_V)
        def mm_cs(e):
            ins = None
            for g in range(8):
                ins = e.matmul(PSB[0:2, g * 64:g * 64 + 64], onesbd2[:], K2E[:, g, :],
                               start=True, stop=True)
            return ins
        cs_T = op("T", mm_cs, "T")
        wt("V", "T", cs_T)
        op("V", lambda e: e.tensor_copy(out=cs2[:], in_=PSB[0:2, 0:512].rearrange("p (g x) -> p g x", x=64)))
        op("V", lambda e: e.tensor_reduce(out=mx2[:], in_=cs2[:], axis=AX.X, op=ALU.max))
        op("V", lambda e: e.drain())
        mx_V = op("V", lambda e: e.reciprocal(out=mx2[:], in_=mx2[:]), "V")
        op("V", lambda e: e.drain())
        wt("T", "V", mx_V)
        dn_T = op("T", lambda e: e.matmul(PSB[:, 512:520], b2c[:], mx2[:, :], start=True, stop=True), "T")
        wt("V", "T", dn_T)
        dn_V = op("V", lambda e: e.tensor_copy(out=denr[:], in_=PSB[:, 512:520]), "V")
        op("V", lambda e: e.drain())

        op("V", lambda e: e.memset(BDP[:], 0.0))
        op("V", lambda e: e.tensor_copy(out=BDP[0:64, :, 0:64], in_=K2E[0:64, :, :]))
        bdk_V = op("V", lambda e: e.tensor_copy(out=BDP[64:128, :, 64:128], in_=K2E[64:128, :, :]), "V")
        wt("T", "V", bdk_V)
        def mm_trk(e):
            ins = None
            for g in range(8):
                ins = e.transpose(PST[:, g * 128:g * 128 + 128], BDP[:, g, :], identbf[:])
            return ins
        trk_T = op("T", mm_trk, "T")
        wt("V", "T", trk_T)
        op("V", lambda e: e.tensor_copy(out=KTBD[:], in_=PST[:].rearrange("p (a b) -> p a b", b=128)))
        wt("V", "V", dn_V)
        op("V", lambda e: e.tensor_tensor(out=VS_[0:64, :, :], in0=KTBD[0:64, :, 0:64],
                                          in1=denr[0:64, :].unsqueeze(2).broadcast_to([64, 8, 64]), op=ALU.mult))
        op("V", lambda e: e.tensor_tensor(out=VS_[64:128, :, :], in0=KTBD[64:128, :, 64:128],
                                          in1=denr[64:128, :].unsqueeze(2).broadcast_to([64, 8, 64]), op=ALU.mult))
        itV = op("V", lambda e: e.tensor_tensor(out=VTBD[:], in0=BDP[:],
                                                in1=denr[:].unsqueeze(2).broadcast_to([128, 8, 128]), op=ALU.mult), "V")

        pst_user = [None]

        def newton_iter(it, state):
            itV = state["itV"]
            wt("T", "V", itV)
            if it == 0:
                wt("T", "V", CCS_V)
            def mm_P(e):
                ins = None
                for g in range(8):
                    ins = e.matmul(PSC[:, g * 64:g * 64 + 64], KTBD[:, g, :], VS_[:, g, :],
                                   start=True, stop=True)
                return ins
            P_T = op("T", mm_P, "T")
            wt("V", "T", P_T)
            ns1_V = op("V", lambda e: e.scalar_tensor_tensor(
                out=NS1[:], in0=PSC[:, 0:512].rearrange("p (g x) -> p g x", x=64),
                scalar=-1.0, in1=ints[:, 0, :].unsqueeze(1).broadcast_to([128, 8, 64]),
                op0=ALU.mult, op1=ALU.add), "V")
            wt("S", "T", P_T)
            op("S", lambda e: e.copy(out=BDP[0:64, :, 0:64],
                                     in_=PSC[0:64, 0:512].rearrange("p (g x) -> p g x", x=64)))
            bdp_V = op("S", lambda e: e.copy(out=BDP[64:128, :, 64:128],
                                             in_=PSC[64:128, 0:512].rearrange("p (g x) -> p g x", x=64)), "S")
            wt("T", "S", bdp_V)
            if pst_user[0] is not None:
                wt("T", pst_user[0][0], pst_user[0][1])
            def mm_trp(e):
                ins = None
                for g in range(8):
                    ins = e.transpose(PST[:, g * 128:g * 128 + 128], BDP[:, g, :], identbf[:])
                return ins
            trp_T = op("T", mm_trp, "T")
            wt("S", "T", trp_T)
            ptbd_V = op("S", lambda e: e.copy(out=PTBD[:], in_=PST[:].rearrange("p (a b) -> p a b", b=128)), "S")
            pst_user[0] = ("S", ptbd_V)
            wt("T", "S", ptbd_V)
            wt("T", "V", ns1_V)
            def mm_T2(e):
                ins = None
                for g in range(8):
                    ins = e.matmul(PSC[:, 512 + g * 64:512 + g * 64 + 64], PTBD[:, g, :], NS1[:, g, :],
                                   start=True, stop=True)
                return ins
            T2_T = op("T", mm_T2, "T")
            wt("V", "T", T2_T)
            s2_V = op("V", lambda e: e.scalar_tensor_tensor(
                out=NS2[:], in0=PSC[:, 512:1024].rearrange("p (g x) -> p g x", x=64),
                scalar=-1.0, in1=ints[:, 1, :].unsqueeze(1).broadcast_to([128, 8, 64]),
                op0=ALU.mult, op1=ALU.add), "V")
            wt("T", "V", s2_V)
            wt("T", "V", ns1_V)
            wt("T", "S", bdp_V)
            def mm_T3(e):
                ins = None
                for g in range(8):
                    ins = e.matmul(PSC[:, g * 64:g * 64 + 64], PTBD[:, g, :], NS2[:, g, :],
                                   start=True, stop=True)
                return ins
            T3_T = op("T", mm_T3, "T")
            wt("V", "T", T3_T)
            s3_V = op("V", lambda e: e.scalar_tensor_tensor(
                out=NS3[:], in0=PSC[:, 0:512].rearrange("p (g x) -> p g x", x=64),
                scalar=-1.0, in1=ints[:, 2, :].unsqueeze(1).broadcast_to([128, 8, 64]),
                op0=ALU.mult, op1=ALU.add), "V")
            wt("T", "V", s3_V)
            def mm_Vn(e):
                ins = None
                for g in range(8):
                    ins = e.matmul(PSC[:, 512 + g * 64:512 + g * 64 + 64], VTBD[:, g, :], NS3[:, g, :],
                                   start=True, stop=True)
                return ins
            Vn_T = op("T", mm_Vn, "T")
            wt("V", "T", Vn_T)
            op("V", lambda e: e.tensor_scalar(out=VS_[:], in0=PSC[:, 512:1024].rearrange("p (g x) -> p g x", x=64),
                                              scalar1=0.25, scalar2=None, op0=ALU.mult))
            op("V", lambda e: e.tensor_scalar(out=BDP[0:64, :, 0:64],
                                              in0=PSC[0:64, 512:1024].rearrange("p (g x) -> p g x", x=64),
                                              scalar1=0.25, scalar2=None, op0=ALU.mult))
            bdv_V = op("V", lambda e: e.tensor_scalar(out=BDP[64:128, :, 64:128],
                                                      in0=PSC[64:128, 512:1024].rearrange("p (g x) -> p g x", x=64),
                                                      scalar1=0.25, scalar2=None, op0=ALU.mult), "V")
            wt("T", "V", bdv_V)
            if pst_user[0] is not None:
                wt("T", pst_user[0][0], pst_user[0][1])
            def mm_trv2(e):
                ins = None
                for g in range(8):
                    ins = e.transpose(PST[:, g * 128:g * 128 + 128], BDP[:, g, :], identbf[:])
                return ins
            trv2_T = op("T", mm_trv2, "T")
            wt("S", "T", trv2_T)
            itS = op("S", lambda e: e.copy(out=VTBD[:], in_=PST[:].rearrange("p (a b) -> p a b", b=128)), "S")
            pst_user[0] = ("S", itS)
            wt("V", "S", itS)
            state["itV"] = op("V", lambda e: e.engine_nop() if False else e.nop(), "V")

        wt("V", "V", qlmV[7])
        op("V", lambda e: e.tensor_copy(out=BDKLM[0:64, :, 0:64], in_=KLM[0:64, :, :]))
        bdklm_V = op("V", lambda e: e.tensor_copy(out=BDKLM[64:128, :, 64:128], in_=KLM[64:128, :, :]), "V")
        e1S = {}
        e1V = {}
        trS = {}

        tr1T = {}

        def l1_mm(ci):
            c0, c1 = CH17[ci]
            wc = c1 - c0
            pg = ci % 2
            s3 = ci % 3
            PSL = PSA if pg == 0 else PSB
            wt("T", "V", bdklm_V)
            if ci == 0:
                wt("T", "S", QALL_S)
                wt("T", "V", kevV[7])
            if ci == 1:
                wt("T", "S", k3eS[(7, 3)])
                wt("T", "V", dn_V)
            prev = e1S.get(ci - 2)
            if prev is not None:
                wt("T", "S", prev)
            def mm_l1(e, c0=c0, wc=wc, PSL=PSL):
                ins = None
                for g in range(8):
                    ins = e.matmul(PSL[0:wc, g * 128:g * 128 + 128], QALL[:, g, c0:c0 + wc],
                                   BDKLM[:, g, :], start=True, stop=True)
                return ins
            e1T = op("T", mm_l1, "T")
            wt("S", "T", e1T)
            prevv = e1V.get(ci - 3)
            if prevv is not None:
                wt("S", "V", prevv)
            if ci - 3 in tr1T:
                wt("S", "T", tr1T[ci - 3])
            e1S[ci] = op("S", lambda e, wc=wc, s3=s3, PSL=PSL: e.activation(
                out=E1B[s3][0:wc, :], in_=PSL[0:wc, :], func=AF.Exp, scale=0.125), "S")
            wt("V", "S", e1S[ci])
            op("V", lambda e, wc=wc, s3=s3: e.tensor_reduce(
                out=rc16[0:wc, s3 * 16:s3 * 16 + 16],
                in_=E1B[s3][0:wc, :].rearrange("p (h x) -> p h x", x=64),
                axis=AX.X, op=ALU.add))

        def l1_recip(ci, guard=False):
            wc = CH17[ci][1] - CH17[ci][0]
            s3 = ci % 3
            if guard:
                op("V", lambda e: e.drain())
            op("V", lambda e, wc=wc, s3=s3: e.reciprocal(
                out=rc16[0:wc, s3 * 16:s3 * 16 + 16], in_=rc16[0:wc, s3 * 16:s3 * 16 + 16]))

        def l1_fin(ci, guard=False):
            c0, c1 = CH17[ci]
            wc = c1 - c0
            s3 = ci % 3
            if guard:
                op("V", lambda e: e.drain())
            e1V[ci] = op("V", lambda e, wc=wc, s3=s3: e.tensor_tensor(
                out=E1B[s3][0:wc, :].rearrange("p (h x) -> p h x", x=64),
                in0=E1B[s3][0:wc, :].rearrange("p (h x) -> p h x", x=64),
                in1=rc16[0:wc, s3 * 16:s3 * 16 + 16].unsqueeze(2).broadcast_to([wc, 16, 64]),
                op=ALU.mult), "V")
            wt("T", "V", e1V[ci])
            if pst_user[0] is not None:
                wt("T", pst_user[0][0], pst_user[0][1])
            def mm_tr1(e, wc=wc, s3=s3):
                ins = None
                for g in range(8):
                    ins = e.transpose(PST[:, g * wc:g * wc + wc],
                                      E1B[s3][0:wc, g * 128:g * 128 + 128], identbf[0:wc, 0:wc])
                return ins
            tr1T[ci] = op("T", mm_tr1, "T")
            wt("S", "T", tr1T[ci])
            trS[ci] = op("S", lambda e, c0=c0, wc=wc: e.copy(
                out=K1T[:, :, c0:c0 + wc],
                in_=PST[:, 0:8 * wc].rearrange("p (g x) -> p g x", x=wc)), "S")
            pst_user[0] = ("S", trS[ci])

        nw_state = {"itV": itV}
        l1_sched = [[0, 1], [2, 3], [4, 5], [6, 7], [8, 9], [10, 11, 12], [13, 14, 15, 16]]
        for step in range(7):
            for ci in l1_sched[step]:
                l1_mm(ci)
                if ci - 1 >= 0:
                    l1_recip(ci - 1)
                if ci - 2 >= 0:
                    l1_fin(ci - 2)
            if step < 6:
                newton_iter(step, nw_state)
        l1_fin(15)
        l1_recip(16)
        l1_fin(16, guard=True)
        NEWT_V = nw_state["itV"]
        K1T_S = cnt["S"]
        snap(4, ("S", K1T_S), K1T[:, :, 0:2048])

        wt("V", "dG", CCR_dG)
        op("V", lambda e: e.reciprocal(out=s3tot[:], in_=CCR[:, 512:520]))
        op("V", lambda e: e.drain())
        k3vn_V = op("V", lambda e: e.tensor_tensor(
            out=K3VN[:], in0=CCR[:, 0:512].rearrange("p (g x) -> p g x", x=64),
            in1=s3tot[:].unsqueeze(2).broadcast_to([128, 8, 64]), op=ALU.mult), "V")
        wt("T", "V", k3vn_V)
        wt("T", "V", NEWT_V)
        def mm_Z(e):
            ins = None
            for g in range(8):
                ins = e.matmul(PSD[:, g * 64:g * 64 + 64], VTBD[:, g, :], K3VN[:, g, :],
                               start=True, stop=True)
            return ins
        Z_T = op("T", mm_Z, "T")
        wt("V", "T", Z_T)
        op("V", lambda e: e.tensor_copy(out=BDQ[0:64, :, 0:64],
                                        in_=PSD[0:64, :].rearrange("p (g x) -> p g x", x=64)))
        bdz_V = op("V", lambda e: e.tensor_copy(out=BDQ[64:128, :, 64:128],
                                                in_=PSD[64:128, :].rearrange("p (g x) -> p g x", x=64)), "V")

        svS = {}
        flat = []
        for g in range(8):
            for si, (a, b) in enumerate(SL5):
                flat.append((g, si, a, b))
        for n, (g, si, a, b) in enumerate(flat):
            w = b - a
            half = (n % 2) * 512
            wt("T", "V", bdz_V)
            wt("T", "S", K1T_S)
            if n >= 2:
                wt("T", "S", svS[n - 2])
            def mm_sv(e, g=g, a=a, w=w, half=half):
                return e.matmul(PSA[:, half:half + w], BDQ[:, g, :], K1T[:, g, a:a + w],
                                start=True, stop=True)
            svT = op("T", mm_sv, "T")
            wt("S", "T", svT)
            svS[n] = op("S", lambda e, g=g, a=a, w=w, half=half: e.copy(
                out=SVALL[:, g, a:a + w], in_=PSA[:, half:half + w]), "S")
        SV_S = cnt["S"]

        wt("Y", "S", K1T_S)
        XRL1_dY = op("Y", lambda e: e.dma_start(out=X[:, 0:4, :], in_=xsp[:, 0:4, :]), "dY", 16)
        wt("Y", "S", SV_S)
        XRL2_dY = op("Y", lambda e: e.dma_start(out=X[:, 4:8, :], in_=xsp[:, 4:8, :]), "dY", 16)

        projV = {}
        flat = []
        for dp in range(NCH):
            for si, (a, b) in enumerate(SL5):
                flat.append((dp, si, a, b))
        for n, (dp, si, a, b) in enumerate(flat):
            w = b - a
            half = (n % 2) * 512
            if n == 0:
                wt("T", "S", SV_S)
                wt("T", "dG", WPROJ_dG)
            if n >= 2:
                wt("T", "V", projV[n - 2])
            def mm_pj(e, dp=dp, a=a, w=w, half=half):
                ins = None
                for c in range(NCH):
                    ins = e.matmul(PSB[:, half:half + w], WPROJ[:, c, dp * 128:dp * 128 + 128],
                                   SVALL[:, c, a:a + w], start=(c == 0), stop=(c == NCH - 1))
                return ins
            pjT = op("T", mm_pj, "T")
            wt("V", "T", pjT)
            if n == 0:
                wt("V", "dY", XRL1_dY)
            if dp == 4 and si == 0:
                wt("V", "dY", XRL2_dY)
            projV[n] = op("V", lambda e, dp=dp, a=a, w=w, half=half: e.tensor_tensor(
                out=X[:, dp, a:a + w], in0=X[:, dp, a:a + w], in1=PSB[:, half:half + w],
                op=ALU.add), "V")
        MASK_V = op("V", lambda e: e.tensor_scalar(out=X[:, :, 0:2], in0=X[:, :, 0:2],
                                                   scalar1=bmask_sb[:, 0:1], scalar2=None,
                                                   op0=ALU.mult), "V")
        PROJ_T = cnt["T"]
        snap(5, ("V", MASK_V), X[:, :, 2:T])

        rows3_of = make_rows(None, "V", MASK_V,
                             slice_gates=[("V", MASK_V if i == 0 else projV[35 + min(4, i // 2)])
                                          for i in range(len(SL))])
        rows3_of.prefetch = 9
        rows3_of(8)
        wt("G", "T", PROJ_T)
        wt("G", "V", MASK_V)
        op("G", lambda e: e.dma_start(out=WKT, in_=wkt_d.rearrange("(c p) d -> p c d", p=128)), "dG", 16)
        FFNW2K_dG = cnt["dG"]
        op("G", lambda e: e.dma_start(out=WRT, in_=wrt_d.rearrange("(c p) d -> p c d", p=128)), "dG", 16)
        FFNW2R_dG = cnt["dG"]
        op("G", lambda e: e.dma_start(out=WVT, in_=wvt_d.rearrange("(c p) d -> p c d", p=128)), "dG", 16)
        FFNW2_dG = cnt["dG"]
        ffn2_G, _, ffn2_addG = emit_ffn(rows3_of, (FFNW2K_dG, FFNW2R_dG, FFNW2_dG),
                                        stats_S=rows3_of.last_S)

        out_pct = out_d.rearrange("(c p) t -> p c t", p=128)
        for i, (a, b) in enumerate(SL):
            a2 = max(a, 2)
            wt("Y", "G", ffn2_addG[i * NCH + NCH - 1])
            op("Y", lambda e, a2=a2, b=b: e.dma_start(
                out=out_pct[:, :, a2 - 2:b - 2], in_=X[:, :, a2:b]), "dY", 16)
        OUT_dY = cnt["dY"]
        wt("Y", "dY", OUT_dY)

        if stage in snaps:
            lens, cnts, (gsem, gval), src_ap = snaps[stage]
            import os as _os
            if _os.environ.get("KERNEL_DEBUG_SNAPS"):
                print("SNAP", stage, "lens", lens, "gate", gsem, gval, "cnts", cnts)
            for k in streams:
                del streams[k][lens[k]:]
            for k in cnt:
                cnt[k] = cnts[k]
            wt("G", gsem, gval)
            if src_ap == "ROWS121":
                dd = op("G", lambda e: e.dma_start(out=out_d[0:1, 0:2048], in_=srow[:, 2:2050]), "dG", 16)
                dd = op("G", lambda e: e.dma_start(out=out_d[1:2, 0:2048], in_=vrow[:, 2:2050]), "dG", 16)
                dd = op("G", lambda e: e.dma_start(out=out_d[2:3, 0:2048], in_=qrow[:, 2:2050]), "dG", 16)
            elif src_ap == "LNSL131":
                dd = op("G", lambda e: e.dma_start(out=out_d[0:128, 0:2048],
                                                   in_=LNSL[:].rearrange("p a b -> p (a b)")[:, 0:2048]), "dG", 16)
                dd = op("G", lambda e: e.dma_start(out=out_d[128:256, 0:257], in_=BCN[:]), "dG", 16)
                dd = op("G", lambda e: e.dma_start(out=out_d[256:384, 0:257], in_=BCS[:]), "dG", 16)
            elif src_ap == "XRK132":
                dd = op("G", lambda e: e.dma_start(out=out_d[0:128, 0:2048],
                                                   in_=XR2[:].rearrange("p a b c -> p (a b c)")[:, 0:2048]), "dG", 16)
                dd = op("G", lambda e: e.dma_start(out=out_d[128:256, 0:2048],
                                                   in_=XK2[:].rearrange("p a b c -> p (a b c)")[:, 0:2048]), "dG", 16)
                dd = op("G", lambda e: e.dma_start(out=out_d[256:384, 0:2048],
                                                   in_=KACT[:].rearrange("p a b -> p (a b)")), "dG", 16)
            elif src_ap == "ATTN31":
                dd = op("G", lambda e: e.dma_start(out=out_d[0:128, 0:2048], in_=K3Eb), "dG", 16)
                dd = op("G", lambda e: e.dma_start(out=out_d[128:256, 0:1024],
                                                   in_=BDQ[:].rearrange("p a b -> p (a b)")), "dG", 16)
                dd = op("G", lambda e: e.dma_start(out=out_d[256:384, 0:512],
                                                   in_=KLM[:].rearrange("p a b -> p (a b)")), "dG", 16)
                dd = op("G", lambda e: e.dma_start(out=out_d[384:512, 0:2048], in_=KG[:, 0:2048]), "dG", 16)
                dd = op("G", lambda e: e.dma_start(out=out_d[512:640, 0:256],
                                                   in_=POOLED[:].rearrange("p a b -> p (a b)")), "dG", 16)
                dd = op("G", lambda e: e.dma_start(out=out_d[640:768, 0:512],
                                                   in_=POOLEDG[:].rearrange("p a b -> p (a b)")), "dG", 16)
            elif src_ap == "LN2BF21":
                dd = op("G", lambda e: e.dma_start(
                    out=out_d.rearrange("(c p) t -> p c t", p=128),
                    in_=LN2BF[:, :, 2:T]), "dG", 16)
            elif src_ap is CCR:
                dd = op("G", lambda e: e.dma_start(out=out_d[0:128, 0:520], in_=src_ap), "dG", 16)
            else:
                dd = op("G", lambda e: e.dma_start(
                    out=out_d.rearrange("(c p) t -> p c t", p=128), in_=src_ap), "dG", 16)
            wt("G", "dG", dd)

        @block.tensor
        def _(e):
            for f in streams["T"]:
                f(e)

        @block.vector
        def _(e):
            for f in streams["V"]:
                f(e)

        @block.scalar
        def _(e):
            for f in streams["S"]:
                f(e)

        @block.gpsimd
        def _(e):
            for f in streams["G"]:
                f(e)

        @block.sync
        def _(e):
            for f in streams["Y"]:
                f(e)

    return nc


_NC_CACHE = {}


def kernel(L0_attn_x, L0_attn_w_qkv, L0_attn_w_proj, L0_ffn_W_r, L0_ffn_W_k,
           L0_ffn_W_v, L0_ffn_mu_r, L0_ffn_mu_k):
    x = np.ascontiguousarray(np.asarray(L0_attn_x, dtype=np.float32))
    w_qkv = np.ascontiguousarray(np.asarray(L0_attn_w_qkv, dtype=np.float32))
    w_proj = np.ascontiguousarray(np.asarray(L0_attn_w_proj, dtype=np.float32))
    wrt = np.ascontiguousarray(np.asarray(L0_ffn_W_r, dtype=np.float32).T)
    wkt = np.ascontiguousarray(np.asarray(L0_ffn_W_k, dtype=np.float32).T)
    wvt = np.ascontiguousarray(np.asarray(L0_ffn_W_v, dtype=np.float32).T)
    mu = np.stack([np.asarray(L0_ffn_mu_r, dtype=np.float32).reshape(8, 128),
                   np.asarray(L0_ffn_mu_k, dtype=np.float32).reshape(8, 128)], axis=0)
    mu = np.ascontiguousarray(mu)

    ident = np.eye(128, dtype=np.float32).astype(ml_dtypes.bfloat16)
    i64 = np.concatenate([np.eye(64, dtype=np.float32)] * 2, axis=0)
    ints = np.ascontiguousarray(
        np.stack([7.0 * i64, 15.0 * i64, 13.0 * i64], axis=1).astype(ml_dtypes.bfloat16))
    onescol = np.ones((128, 1), dtype=np.float32).astype(ml_dtypes.bfloat16)
    onesbd2 = np.zeros((128, 2), dtype=np.float32)
    onesbd2[0:64, 0] = 1.0
    onesbd2[64:128, 1] = 1.0
    onesbd2 = onesbd2.astype(ml_dtypes.bfloat16)
    bconst = np.zeros((1, 2, 128), dtype=np.float32)
    bconst[0, 0, :] = -1.0 / 1024.0
    bconst[0, 1, :] = 1.0
    b2 = np.zeros((2, 128), dtype=np.float32)
    b2[0, 0:64] = 1.0
    b2[1, 64:128] = 1.0

    import os
    _stage = int(os.environ.get("KERNEL_STAGE", "0"))
    if ("nc", _stage) not in _NC_CACHE:
        _NC_CACHE[("nc", _stage)] = build_nc(_stage)
    nc = _NC_CACHE[("nc", _stage)]

    in_maps = []
    for core in range(8):
        b = core // 2
        half = core % 2
        if half == 0:
            tile = np.concatenate([np.zeros((2, 1024), np.float32), x[b, 0:2048]], axis=0)
            bm = np.zeros((128, 1), np.float32)
        else:
            tile = x[b, 2046:4096]
            bm = np.ones((128, 1), np.float32)
        xtile = np.ascontiguousarray(tile.T)
        in_maps.append({
            "xt": xtile, "wqkv": w_qkv, "wproj": w_proj, "wrt": wrt, "wkt": wkt,
            "wvt": wvt, "mud": mu, "bmaskd": bm, "identbfd": ident, "intsd": ints,
            "onescold": onescol, "onesbd2d": onesbd2, "bconstd": bconst,
            "b2d": b2,
        })

    res = run_bass_kernel_spmd(nc, in_maps, core_ids=list(range(8)))
    out = np.empty((4, 4096, 1024), dtype=np.float32)
    for core in range(8):
        b = core // 2
        half = core % 2
        o = res.results[core]["out"]
        out[b, half * 2048:(half + 1) * 2048] = o.T
    return out


# version 15
# speedup vs baseline: 1.0845x; 1.0245x over previous
import numpy as np
import ml_dtypes
import concourse.bass as bass
import concourse.mybir as mybir
from concourse.bass_utils import run_bass_kernel_spmd

F32 = mybir.dt.float32
BF16 = mybir.dt.bfloat16
AF = mybir.ActivationFunctionType
ALU = mybir.AluOpType
AX = mybir.AxisListType

T = 2050
NCH = 8
EPS = 1e-5

SL = [(i * 256, (i + 1) * 256) for i in range(8)] + [(2048, 2050)]
SL5 = [(0, 512), (512, 1024), (1024, 1536), (1536, 2048), (2048, 2050)]
SLO = [(2 + i * 512, 2 + (i + 1) * 512) for i in range(4)]
CH17 = [(i * 128, i * 128 + 128) for i in range(16)] + [(2048, 2050)]
CHO16 = [(2 + i * 128, 2 + i * 128 + 128) for i in range(16)]
RG = [[0, 1], [2, 3], [4, 5], [6, 7]]


def build_nc(stage=0):
    import os
    SKIP = set(os.environ.get("KERNEL_SKIP", "").split(","))
    nc = bass.Bass("TRN2", target_bir_lowering=False)

    xt = nc.dram_tensor("xt", [1024, T], F32, kind="ExternalInput").ap()
    wqkv = nc.dram_tensor("wqkv", [1024, 3072], F32, kind="ExternalInput").ap()
    wproj_d = nc.dram_tensor("wproj", [1024, 1024], F32, kind="ExternalInput").ap()
    wrt_d = nc.dram_tensor("wrt", [1024, 1024], F32, kind="ExternalInput").ap()
    wkt_d = nc.dram_tensor("wkt", [1024, 1024], F32, kind="ExternalInput").ap()
    wvt_d = nc.dram_tensor("wvt", [1024, 1024], F32, kind="ExternalInput").ap()
    mu_d = nc.dram_tensor("mud", [2, 8, 128], F32, kind="ExternalInput").ap()
    bmask_d = nc.dram_tensor("bmaskd", [128, 1], F32, kind="ExternalInput").ap()
    identbf_d = nc.dram_tensor("identbfd", [128, 128], BF16, kind="ExternalInput").ap()
    ints_d = nc.dram_tensor("intsd", [128, 3, 64], BF16, kind="ExternalInput").ap()
    onescol_d = nc.dram_tensor("onescold", [128, 1], BF16, kind="ExternalInput").ap()
    onesbd2_d = nc.dram_tensor("onesbd2d", [128, 2], BF16, kind="ExternalInput").ap()
    bconst_d = nc.dram_tensor("bconstd", [1, 2, 128], F32, kind="ExternalInput").ap()
    b2_d = nc.dram_tensor("b2d", [2, 128], F32, kind="ExternalInput").ap()

    out_d = nc.dram_tensor("out", [1024, 2048], F32, kind="ExternalOutput").ap()

    xsp = nc.dram_tensor("xsp", [128, NCH, T], F32).ap()
    cc3in = nc.dram_tensor("cc3in", [1024, 32], BF16)
    cc3out = nc.dram_tensor("cc3out", [2, 1024, 32], BF16)
    cc4in = nc.dram_tensor("cc4in", [128, 520], F32)
    cc4out = nc.dram_tensor("cc4out", [128, 520], F32)

    streams = {e: [] for e in ("T", "V", "S", "G", "Y")}
    cnt = {"T": 0, "V": 0, "S": 0, "G": 0, "dY": 0, "dG": 0, "CC": 0}
    semh = {}

    def op(eng, fn, inc=None, n=1):
        if inc is None:
            streams[eng].append(lambda e, fn=fn: fn(e))
            return None
        cnt[inc] += n
        def thunk(e, fn=fn, inc=inc, n=n):
            fn(e).then_inc(semh[inc], n)
        streams[eng].append(thunk)
        return cnt[inc]

    def wt(eng, sem, val):
        if val is not None and val > 0:
            streams[eng].append(lambda e, sem=sem, val=val: e.wait_ge(semh[sem], val))

    from contextlib import ExitStack
    with ExitStack() as stack:
        ec = stack.enter_context
        XRb = ec(nc.sbuf_tensor("XR", [128, 16400], F32))
        WA = ec(nc.sbuf_tensor("WA", [128, 24576], BF16))
        A3 = ec(nc.sbuf_tensor("A3", [128, 16400], BF16))
        SCR = ec(nc.sbuf_tensor("SCR", [128, 4618], F32))
        GT2 = ec(nc.sbuf_tensor("GT2", [128, 4100], BF16))
        ROWS = ec(nc.sbuf_tensor("ROWS", [128, 4, 2050], F32))
        identbf = ec(nc.sbuf_tensor("identbf", [128, 128], BF16))
        ints = ec(nc.sbuf_tensor("ints", [128, 3, 64], BF16))
        onescol = ec(nc.sbuf_tensor("onescol", [128, 1], BF16))
        onesbd2 = ec(nc.sbuf_tensor("onesbd2", [128, 2], BF16))
        bconst = ec(nc.sbuf_tensor("bconst", [1, 2, 128], F32))
        b2c = ec(nc.sbuf_tensor("b2c", [2, 128], F32))
        mu_sb = ec(nc.sbuf_tensor("mu_sb", [128, 2, 8], F32))
        omu_sb = ec(nc.sbuf_tensor("omu_sb", [128, 2, 8], F32))
        bmask_sb = ec(nc.sbuf_tensor("bmask_sb", [128, 1], F32))
        sums3p = ec(nc.sbuf_tensor("sums3p", [128, 8, 4], F32))
        s3tot = ec(nc.sbuf_tensor("s3tot", [128, 8], F32))
        denr = ec(nc.sbuf_tensor("denr", [128, 8], F32))
        rc16 = ec(nc.sbuf_tensor("rc16", [128, 48], F32))
        rc8 = ec(nc.sbuf_tensor("rc8", [128, 8], F32))
        cs2 = ec(nc.sbuf_tensor("cs2", [2, 8, 64], F32))
        mx2 = ec(nc.sbuf_tensor("mx2", [2, 8], F32))
        PSA = ec(nc.psum_tensor("PSA", [128, 1024], F32))
        PSB = ec(nc.psum_tensor("PSB", [128, 1024], F32))
        PSC = ec(nc.psum_tensor("PSC", [128, 1024], F32))
        PSD = ec(nc.psum_tensor("PSD", [128, 512], F32))
        PST = ec(nc.psum_tensor("PST", [128, 1024], BF16))
        semT = ec(nc.semaphore("semT"))
        semV = ec(nc.semaphore("semV"))
        semS = ec(nc.semaphore("semS"))
        semG = ec(nc.semaphore("semG"))
        dY = ec(nc.semaphore("dY"))
        dG = ec(nc.semaphore("dG"))
        scc = ec(nc.semaphore("scc"))
        block = ec(nc.Block())
        semh.update({"T": semT, "V": semV, "S": semS, "G": semG,
                     "dY": dY, "dG": dG, "CC": scc})

        def view(base, off, shape):
            n = int(np.prod(shape))
            apx = base[:, off:off + n]
            if len(shape) == 2:
                apx = apx.rearrange("p (a b) -> p a b", b=shape[1])
            elif len(shape) == 3:
                apx = apx.rearrange("p (a b c) -> p a b c", b=shape[1], c=shape[2])
            return apx

        X = view(XRb, 0, [NCH, T])
        XRbf = XRb[:, :].bitcast(BF16)
        QALL = view(XRbf, 0, [NCH, T])
        K1T = view(XRbf, NCH * T, [NCH, T])
        PL1 = view(XRb, 8200, [NCH, 1024])

        WRT = view(WA, 0, [NCH, 1024])
        WKT = view(WA, 8192, [NCH, 1024])
        WVT = view(WA, 16384, [NCH, 1024])
        WG = [view(WA, 0, [NCH, 384]), view(WA, 3072, [NCH, 384])]
        WG4 = [view(WA, 0, [NCH, 3, 128]), view(WA, 3072, [NCH, 3, 128])]
        BDQ = view(WA, 6144, [NCH, 128])
        K2E = view(WA, 7168, [NCH, 64])
        BDKLM = view(WA, 7680, [NCH, 128])
        KLM = view(WA, 8704, [NCH, 64])
        POOLED = view(WA, 9216, [NCH, 32])
        POOLEDG = view(WA, 9472, [NCH, 64])
        KG = WA[:, 9984:9984 + T]
        VG = WA[:, 12034:12034 + T]
        K3Eb = WA[:, 14084:14084 + 2048]
        K3T = view(WA, 16132, [16, 128])
        VTOK = view(WA, 18180, [16, 128])
        E1B = [WA[:, 20228:20228 + 1024], WA[:, 21252:21252 + 1024],
               WA[:, 22276:22276 + 1024]]
        KG2 = GT2[:, 0:T]
        VG2 = GT2[:, T:2 * T]
        K3Eb2 = K3Eb
        K3T2 = K3T
        VTOK2 = VTOK
        KTBD = view(WA, 0, [NCH, 128])
        PTBD = view(WA, 1024, [NCH, 128])
        VTBD = view(WA, 2048, [NCH, 128])
        VS_ = view(WA, 3072, [NCH, 64])
        NS1 = view(WA, 3584, [NCH, 64])
        NS2 = view(WA, 4096, [NCH, 64])
        NS3 = view(WA, 4608, [NCH, 64])
        BDP = view(WA, 5120, [NCH, 128])
        K3VN = view(WA, 9984, [NCH, 64])
        WPROJ = view(WA, 10496, [NCH, 1024])

        XR2 = view(A3, 0, [NCH, 2, 256])
        XK2 = view(A3, 4096, [NCH, 2, 256])
        KACT = view(A3, 8192, [NCH, 256])
        X2SL = [view(A3, 0, [NCH, 256]), view(A3, 2048, [NCH, 256])]
        SQSL = [view(A3, 4096, [NCH, 256]), view(A3, 6144, [NCH, 256])]
        LNSLB = view(A3, 10240, [NCH, 257])
        DSLB = view(A3, 12296, [NCH, 256])
        LN2BF = view(A3, 0, [NCH, T])
        SVALL = view(A3, 0, [NCH, T])

        LNSL = view(SCR, 0, [NCH, 257])
        T1DB = view(SCR, 2056, [2, 256])
        KTMP2 = view(SCR, 2568, [2, 256])
        RCH2 = view(SCR, 3080, [2, 256])
        TCH2 = view(SCR, 3592, [2, 256])
        BCN = SCR[:, 4104:4104 + 257]
        BCS = SCR[:, 4361:4361 + 257]
        LNSCR = view(SCR, 0, [NCH, 256])
        PL2 = view(SCR, 0, [NCH, 512])
        PL3 = view(XRb, 8200, [NCH, 256])
        PL4 = view(SCR, 0, [NCH, 128])
        PL5 = view(XRb, 8200 + 4096, [NCH, 64])
        PL6 = view(SCR, 1024, [NCH, 32])
        CCS = SCR[:, 0:520]
        CCR = SCR[:, 520:1040]

        srow = ROWS[0:1, 0, :]
        qrow = ROWS[0:1, 1, :]
        mrow = ROWS[0:1, 2, :]
        vrow = ROWS[0:1, 3, :]

        for src, dst in [(identbf_d, identbf), (ints_d, ints), (onescol_d, onescol),
                         (onesbd2_d, onesbd2), (bconst_d, bconst),
                         (b2_d, b2c), (bmask_d, bmask_sb)]:
            op("Y", lambda e, s=src, d=dst: e.dma_start(out=d[:], in_=s), "dY", 16)
        (op("Y", lambda e: e.dma_start(out=mu_sb[:], in_=mu_d.rearrange("i c p -> p i c"),
                                      allow_slow_non_contiguous=True), "dY", 16)
         if "mu" not in SKIP else op("Y", lambda e: e.dma_start(out=mu_sb[:, 0, 0:1], in_=mu_d[0:1, 0, 0:1].transpose([1, 0]) if False else mu_d[0, 0:1, 0:1]), "dY", 16))
        PRO_dY = cnt["dY"]
        XL = {}
        xt_pct = xt.rearrange("(c p) t -> p c t", p=128)
        for i, (a, b) in enumerate(SL):
            XL[i] = op("Y", lambda e, a=a, b=b: e.dma_start(
                out=X[:, :, a:b], in_=xt_pct[:, :, a:b]), "dY", 16)

        def wg_load_thunks(g, buf):
            outs = []
            for t3 in range(3):
                col = t3 * 1024 + g * 128
                outs.append(lambda e, t3=t3, col=col, buf=buf: e.dma_start(
                    out=view(WA, buf * 3072 + t3 * 128, [NCH, 128]) if False else WG4[buf][:, :, t3, :],
                    in_=wqkv[:, col:col + 128].rearrange("(c p) d -> p c d", p=128)))
            return outs

        wt("V", "dY", PRO_dY)
        op("V", lambda e: e.tensor_scalar(out=omu_sb[:], in0=mu_sb[:], scalar1=-1.0,
                                          scalar2=1.0, op0=ALU.mult, op1=ALU.add))

        SCRbf = SCR[:, :].bitcast(BF16)
        X2SLB = [GT2[:, 0:2048].rearrange("p (a b) -> p a b", b=256),
                 GT2[:, 2048:4096].rearrange("p (a b) -> p a b", b=256)]
        SQSLB = [SCRbf[:, 4112:4112 + 2048].rearrange("p (a b) -> p a b", b=256),
                 SCRbf[:, 6160:6160 + 2048].rearrange("p (a b) -> p a b", b=256)]

        def make_rows(src_dram, gate_sem, gate_val, slice_gates=None, slots=None):
            SX, SQ = slots if slots is not None else (X2SL, SQSL)
            """Lazy per-slice LN stats. Returns rows_of(i) -> (sem, tick): emits
            stats slices up to i+2 and per-slice finishes up to i on demand, so
            the T stream interleaves stat matmuls with consuming bcasts."""
            sum_T = {}
            sq_T = {}
            ev_sum_S = {}
            ev_sq_S = {}
            castG = {}
            sqV = {}
            fin_tick = {}

            def emit_stats(i):
                a, b = SL[i]
                w = b - a
                pg = i % 2
                if slice_gates is not None:
                    wt("G", slice_gates[i][0], slice_gates[i][1])
                elif gate_sem is not None and i == 0:
                    wt("G", gate_sem, gate_val)
                if i >= 2:
                    wt("G", "T", sum_T[i - 2])
                    wt("G", "S", sqV[i - 2])
                if src_dram is not None:
                    castG[i] = op("G", lambda e, a=a, w=w, pg=pg: e.dma_start(
                        out=SX[pg][:, :, 0:w],
                        in_=src_dram.rearrange("(c p) t -> p c t", p=128)[:, :, a:a + w]), "dG", 16)
                else:
                    castG[i] = op("G", lambda e, a=a, w=w, pg=pg: e.dma_start(
                        out=SX[pg][:, :, 0:w], in_=X[:, :, a:a + w]), "dG", 16)
                wt("S", "dG", castG[i])
                if i >= 2:
                    wt("S", "T", sq_T[i - 2])
                sqV[i] = op("S", lambda e, w=w, pg=pg: e.activation(
                    out=SQ[pg][:, :, 0:w], in_=SX[pg][:, :, 0:w],
                    func=AF.Square), "S")
                wt("T", "dG", castG[i])
                if i == 0:
                    wt("T", "dY", PRO_dY)
                if i >= 2:
                    wt("T", "S", ev_sum_S[i - 2])
                def mm_sum(e, w=w, pg=pg):
                    ins = None
                    for kc in range(NCH):
                        ins = e.matmul(PSC[0:1, pg * 512:pg * 512 + w], onescol[:],
                                       SX[pg][:, kc, 0:w], start=(kc == 0), stop=(kc == NCH - 1))
                    return ins
                sum_T[i] = op("T", mm_sum, "T")
                wt("T", "S", sqV[i])
                if i >= 2:
                    wt("T", "S", ev_sq_S[i - 2])
                def mm_sq(e, w=w, pg=pg):
                    ins = None
                    for kc in range(NCH):
                        ins = e.matmul(PSB[0:1, pg * 512:pg * 512 + w], onescol[:],
                                       SQ[pg][:, kc, 0:w], start=(kc == 0), stop=(kc == NCH - 1))
                    return ins
                sq_T[i] = op("T", mm_sq, "T")
                wt("S", "T", sum_T[i])
                ev_sum_S[i] = op("S", lambda e, a=a, w=w, pg=pg: e.copy(
                    out=srow[:, a:a + w], in_=PSC[0:1, pg * 512:pg * 512 + w]), "S")
                wt("S", "T", sq_T[i])
                ev_sq_S[i] = op("S", lambda e, a=a, w=w, pg=pg: e.copy(
                    out=qrow[:, a:a + w], in_=PSB[0:1, pg * 512:pg * 512 + w]), "S")

            def emit_finish(k):
                a = SL[k][0]
                b = 2050 if k == 7 else SL[k][1]
                wt("V", "S", ev_sq_S[8 if k == 7 else k])
                op("V", lambda e, a=a, b=b: e.tensor_tensor(
                    out=vrow[:, a:b], in0=srow[:, a:b], in1=srow[:, a:b], op=ALU.mult))
                op("V", lambda e, a=a, b=b: e.scalar_tensor_tensor(
                    out=vrow[:, a:b], in0=vrow[:, a:b], scalar=1.0 / 1024.0,
                    in1=qrow[:, a:b], op0=ALU.mult, op1=ALU.subtract))
                epsV = op("V", lambda e, a=a, b=b: e.tensor_scalar(
                    out=vrow[:, a:b], in0=vrow[:, a:b], scalar1=-1.0 / 1024.0,
                    scalar2=EPS, op0=ALU.mult, op1=ALU.add), "V")
                wt("S", "V", epsV)
                sdS = op("S", lambda e, a=a, b=b: e.activation(
                    out=vrow[:, a:b], in_=vrow[:, a:b], func=AF.Sqrt), "S")
                wt("V", "S", sdS)
                fin_tick[k] = op("V", lambda e, a=a, b=b: e.reciprocal(
                    out=vrow[:, a:b], in_=vrow[:, a:b]), "V")

            def rows_of(i):
                k = min(i, 7)
                for j in range(len(castG), min(i + 1 + rows_of.prefetch, 9)):
                    emit_stats(j)
                for j in range(len(fin_tick), k + 1):
                    if j == 7 and 8 not in castG:
                        emit_stats(8)
                    emit_finish(j)
                return ("V", fin_tick[k])

            rows_of.prefetch = 2
            rows_of.last_S = lambda: ev_sq_S[8]
            return rows_of

        def emit_bcast_slice(i, a, b, rows_of, prev_user_V, prev_user_G=None):
            w = b - a
            tick = rows_of(i)
            wt("T", tick[0], tick[1])
            if prev_user_V is not None:
                wt("T", "V", prev_user_V)
            if prev_user_G is not None:
                wt("T", "G", prev_user_G)
            def mm_bc(e, a=a, w=w):
                e.matmul(PSD[:, 0:w], bconst[0:1, 0, :], srow[:, a:a + w], start=True, stop=True)
                return e.matmul(PSD[:, 256:256 + w], bconst[0:1, 1, :], vrow[:, a:a + w], start=True, stop=True)
            t2 = op("T", mm_bc, "T")
            wt("V", "T", t2)
            op("V", lambda e, w=w: e.tensor_copy(out=BCN[:, 1:1 + w], in_=PSD[:, 0:w]))
            vv = op("V", lambda e, w=w: e.tensor_copy(out=BCS[:, 1:1 + w], in_=PSD[:, 256:256 + w]), "V")
            if w < 64:
                op("V", lambda e: e.drain())
            return vv

        def emit_ffn(rows_V, w_dG, snap_cb=None, stats_S=None, xload_of=None):
            snap_cb2 = [snap_cb]
            bc_user = None
            reluS = {}
            kactV = {}
            sigS = {}
            rvV = {}
            addG = {}
            mixV = {}
            t1hist = []
            mix_hist = []
            vT = {}
            NRV = 0
            def stageA(i):
                nonlocal bc_user
                a, b = SL[i]
                w = b - a
                bc_user = emit_bcast_slice(i, a, b, rows_V, bc_user)
                if i >= 2:
                    wt("V", "T", vT[(i - 2, NCH - 1)])
                if i == 0:
                    op("V", lambda e: e.memset(LNSLB[:, :, 0:1], 0.0))
                else:
                    op("V", lambda e: e.tensor_copy(out=LNSLB[:, :, 0:1], in_=LNSLB[:, :, 256:257]))
                if w < 64:
                    op("V", lambda e: e.drain())
                if xload_of is not None:
                    wt("V", "dY", xload_of(i))
                op("V", lambda e, a=a, w=w: e.tensor_tensor(
                    out=LNSL[:, :, 1:1 + w], in0=X[:, :, a:a + w],
                    in1=BCN[:, 1:1 + w].unsqueeze(1).broadcast_to([128, NCH, w]),
                    op=ALU.add))
                if w < 64:
                    op("V", lambda e: e.drain())
                lnV = op("V", lambda e, w=w: e.tensor_tensor(
                    out=LNSLB[:, :, 1:1 + w], in0=LNSL[:, :, 1:1 + w],
                    in1=BCS[:, 1:1 + w].unsqueeze(1).broadcast_to([128, NCH, w]),
                    op=ALU.mult), "V")
                if w < 64:
                    op("V", lambda e: e.drain())
                if i == len(SL) - 1 and 15 not in snaps:
                    snaps[15] = ({k: len(v) for k, v in streams.items()}, dict(cnt),
                                 ("V", lnV), "LNSL131")
                pg = i % 2
                op("V", lambda e, w=w: e.tensor_tensor(
                    out=DSLB[:, :, 0:w], in0=LNSLB[:, :, 1:1 + w],
                    in1=LNSLB[:, :, 0:w], op=ALU.subtract))
                for mi, dst2 in [(0, XR2), (1, XK2)]:
                    for c in range(NCH):
                        op("V", lambda e, c=c, w=w, mi=mi, dst2=dst2, pg=pg:
                           e.scalar_tensor_tensor(
                               out=dst2[:, c, pg, 0:w], in0=DSLB[:, c, 0:w],
                               scalar=mu_sb[:, mi, c:c + 1], in1=LNSLB[:, c, 0:w],
                               op0=ALU.mult, op1=ALU.add), "V")
                mixV[i] = cnt["V"]

            def stageB(i):
                a, b = SL[i]
                w = b - a
                pg = i % 2
                wt("T", "V", mixV[i])
                if i == 0:
                    wt("T", "dG", w_dG[0])
                for dp in range(NCH):
                    half = (dp % 2) * 512
                    prev_relu = reluS.get((i, dp - 2)) if dp >= 2 else reluS.get((i - 1, dp + 6))
                    if prev_relu is not None:
                        wt("T", "S", prev_relu)
                    def mm_k(e, w=w, pg=pg, dp=dp, half=half):
                        ins = None
                        for c in range(NCH):
                            ins = e.matmul(PSA[:, half:half + w], WKT[:, c, dp * 128:dp * 128 + 128],
                                           XK2[:, c, pg, 0:w], start=(c == 0), stop=(c == NCH - 1))
                        return ins
                    kTt = op("T", mm_k, "T")
                    wt("S", "T", kTt)
                    op("S", lambda e, w=w, dp=dp, half=half: e.activation(
                        out=KTMP2[:, dp % 2, 0:w], in_=PSA[:, half:half + w], func=AF.Relu))
                    reluS[(i, dp)] = op("S", lambda e, w=w, dp=dp: e.activation(
                        out=KACT[:, dp, 0:w], in_=KTMP2[:, dp % 2, 0:w], func=AF.Square), "S")
                    kactV[(i, dp)] = reluS[(i, dp)]

                if xload_of is not None:
                    wt("G", "dY", xload_of(i))
                if i == 0:
                    wt("T", "dG", w_dG[2])
                if i == 0 and stats_S is not None:
                    wt("T", "S", stats_S())
                for dp in range(NCH):
                    half = (dp % 2) * 512
                    n = i * NCH + dp
                    prev_sig = sigS.get(n - 2)
                    if prev_sig is not None:
                        wt("T", "S", prev_sig)
                    def mm_r(e, w=w, pg=pg, dp=dp, half=half):
                        ins = None
                        for c in range(NCH):
                            ins = e.matmul(PSB[:, half:half + w], WRT[:, c, dp * 128:dp * 128 + 128],
                                           XR2[:, c, pg, 0:w], start=(c == 0), stop=(c == NCH - 1))
                        return ins
                    rTt = op("T", mm_r, "T")
                    if dp == 0:
                        wt("T", "S", kactV[(i, NCH - 1)])
                    prev_rv = rvV.get(n - 2)
                    if prev_rv is not None:
                        wt("T", "V", prev_rv)
                    def mm_v(e, w=w, dp=dp, half=half):
                        ins = None
                        for c in range(NCH):
                            ins = e.matmul(PSC[:, half:half + w], WVT[:, c, dp * 128:dp * 128 + 128],
                                           KACT[:, c, 0:w], start=(c == 0), stop=(c == NCH - 1))
                        return ins
                    vT[(i, dp)] = op("T", mm_v, "T")
                    wt("S", "T", rTt)
                    prev_rv1 = rvV.get(n - 1)
                    if prev_rv1 is not None:
                        wt("S", "V", prev_rv1)
                    sigS[n] = op("S", lambda e, w=w, dp=dp, half=half: e.activation(
                        out=RCH2[:, dp % 2, 0:w], in_=PSB[:, half:half + w], func=AF.Sigmoid), "S")
                    wt("V", "S", sigS[n])
                    wt("V", "T", vT[(i, dp)])
                    prev_add = addG.get(n - 2)
                    if prev_add is not None:
                        wt("V", "G", prev_add)
                    rvV[n] = op("V", lambda e, w=w, dp=dp, half=half: e.scalar_tensor_tensor(
                        out=TCH2[:, dp % 2, 0:w], in0=PSC[:, half:half + w], scalar=0.5,
                        in1=RCH2[:, dp % 2, 0:w], op0=ALU.mult, op1=ALU.mult), "V")
                    wt("G", "V", rvV[n])
                    addG[n] = op("G", lambda e, a=a, b=b, w=w, dp=dp: e.tensor_tensor(
                        out=X[:, dp, a:b], in0=X[:, dp, a:b], in1=TCH2[:, dp % 2, 0:w],
                        op=ALU.add), "G")
                if i == 0 and snap_cb is not None:
                    snap_cb2[0] = None

            for i in range(len(SL) + 1):
                if i < len(SL):
                    stageA(i)
                if i >= 1:
                    stageB(i - 1)
                    if i - 1 == 0 and snap_cb is not None:
                        snap_cb(cnt["G"])
            return cnt["G"], bc_user, addG

        snaps = {}
        def snap(n, gate, src_ap):
            snaps[n] = ({k: len(v) for k, v in streams.items()}, dict(cnt), gate, src_ap)

        rows1_of = make_rows(xt, None, None)
        rows1_of.prefetch = 9
        rows1_t0 = rows1_of(8)
        if "wcast" not in SKIP:
            op("G", lambda e: e.dma_start(out=WKT, in_=wkt_d.rearrange("(c p) d -> p c d", p=128)), "dG", 16)
        FFNWK_dG = cnt["dG"]
        if "wcast" not in SKIP:
            op("G", lambda e: e.dma_start(out=WRT, in_=wrt_d.rearrange("(c p) d -> p c d", p=128)), "dG", 16)
        FFNWR_dG = cnt["dG"]
        if "wcast" not in SKIP:
            op("G", lambda e: e.dma_start(out=WVT, in_=wvt_d.rearrange("(c p) d -> p c d", p=128)), "dG", 16)
        FFNW_dG = cnt["dG"]
        snap(11, ("dG", FFNW_dG), X[:, :, 2:T])
        if stage in (12, 121):
            rows1_full = rows1_of(8)
        else:
            rows1_full = rows1_t0
        snap(12, rows1_full, X[:, :, 2:T])
        snaps[121] = (snaps[12][0], snaps[12][1], rows1_full, "ROWS121")
        def _snap13(g_tick):
            snap(13, ("G", g_tick), X[:, :, 2:T])
            snaps[131] = (snaps[13][0], snaps[13][1], ("G", g_tick), "LNSL131")
            snaps[132] = (snaps[13][0], snaps[13][1], ("G", g_tick), "XRK132")
        ffn1_G, _, ffn1_addG = emit_ffn(rows1_of, (FFNWK_dG, FFNWR_dG, FFNW_dG),
                                        snap_cb=_snap13, stats_S=rows1_of.last_S,
                                        xload_of=lambda i: XL[i])
        ffn1_T = cnt["T"]
        snaps[14] = ({k: len(v) for k, v in streams.items()}, dict(cnt), ("G", ffn1_G), "LNSL131")
        snaps[141] = ({k: len(v) for k, v in streams.items()}, dict(cnt), ("G", ffn1_G), "XRK132")

        snap(1, ("G", ffn1_G), X[:, :, 2:T])
        rows2_of = make_rows(None, "G", ffn1_G,
                             slice_gates=[("G", ffn1_addG[i * NCH + NCH - 1]) for i in range(len(SL))],
                             slots=(X2SLB, SQSLB))

        wt("Y", "G", ffn1_G)
        XSPILL_dY = op("Y", lambda e: e.dma_start(out=xsp, in_=X[:]), "dY", 16)

        bc_user = None
        ln2V = {}
        for i, (a, b) in enumerate(SL):
            w = b - a
            bc_user = emit_bcast_slice(i, a, b, rows2_of, bc_user)
            op("V", lambda e, a=a, w=w: e.tensor_tensor(
                out=LNSCR[:, :, 0:w], in0=X[:, :, a:a + w],
                in1=BCN[:, 1:1 + w].unsqueeze(1).broadcast_to([128, NCH, w]), op=ALU.add))
            if w < 64:
                op("V", lambda e: e.drain())
            bc_user = op("V", lambda e, a=a, w=w: e.tensor_tensor(
                out=LN2BF[:, :, a:a + w], in0=LNSCR[:, :, 0:w],
                in1=BCS[:, 1:1 + w].unsqueeze(1).broadcast_to([128, NCH, w]), op=ALU.mult), "V")
            if w < 64:
                op("V", lambda e: e.drain())
            ln2V[i] = bc_user
        LN2_V = cnt["V"]

        snaps[21] = ({k: len(v) for k, v in streams.items()}, dict(cnt),
                     ("V", LN2_V), "LN2BF21")
        snaps[22] = ({k: len(v) for k, v in streams.items()}, dict(cnt),
                     ("V", LN2_V), "ROWS121")
        wt("V", "dY", XSPILL_dY)
        op("V", lambda e: e.tensor_tensor(out=PL1[:], in0=LN2BF[:, :, 2:2049:2],
                                          in1=LN2BF[:, :, 3:2050:2], op=ALU.add))
        op("V", lambda e: e.tensor_tensor(out=PL2[:], in0=PL1[:, :, 0:1024:2],
                                          in1=PL1[:, :, 1:1024:2], op=ALU.add))
        op("V", lambda e: e.tensor_tensor(out=PL3[:], in0=PL2[:, :, 0:512:2],
                                          in1=PL2[:, :, 1:512:2], op=ALU.add))
        op("V", lambda e: e.tensor_tensor(out=PL4[:], in0=PL3[:, :, 0:256:2],
                                          in1=PL3[:, :, 1:256:2], op=ALU.add))
        op("V", lambda e: e.tensor_tensor(out=PL5[:], in0=PL4[:, :, 0:128:2],
                                          in1=PL4[:, :, 1:128:2], op=ALU.add))
        op("V", lambda e: e.tensor_tensor(out=PL6[:], in0=PL5[:, :, 0:64:2],
                                          in1=PL5[:, :, 1:64:2], op=ALU.add))
        POOL_V = op("V", lambda e: e.tensor_scalar(out=POOLED[:], in0=PL6[:], scalar1=1.0 / 64.0,
                                                   scalar2=None, op0=ALU.mult), "V")

        wt("G", "T", ffn1_T)
        WGL = {}
        for th in wg_load_thunks(0, 0):
            WGL[0] = op("G", th, "dG", 16)
        for th in wg_load_thunks(1, 1):
            WGL[1] = op("G", th, "dG", 16)

        wt("G", "V", POOL_V)
        op("G", lambda e: e.dma_start(out=cc3in.ap().rearrange("(c p) s -> p c s", p=128),
                                      in_=POOLED[:]), "dG", 16)
        wt("G", "dG", cnt["dG"])
        CC3 = op("G", lambda e: e.collective_compute(
            "AllGather", ALU.bypass, replica_groups=RG,
            ins=[cc3in.ap().opt()], outs=[cc3out.ap().opt()]), "CC", 1)
        wt("G", "CC", CC3)
        op("G", lambda e: e.dma_start(
            out=POOLEDG[:, :, 0:32],
            in_=cc3out.ap()[0].rearrange("(c p) s -> p c s", p=128)), "dG", 16)
        PGL_dG = op("G", lambda e: e.dma_start(
            out=POOLEDG[:, :, 32:64],
            in_=cc3out.ap()[1].rearrange("(c p) s -> p c s", p=128)), "dG", 16)

        snap(2, ("dG", PGL_dG), X[:, :, 2:T])
        wt("V", "T", ffn1_T)
        op("V", lambda e: e.memset(BDQ[:], 0.0))
        BDZERO_V = op("V", lambda e: e.memset(BDKLM[:], 0.0), "V")

        lastWG_T = {}
        qlmV = {}
        klmT = {}
        l2T = {}
        k3eS = {}
        k3tV = {}
        vtokS = {}
        k3vT = {}
        kevV = {}
        vevS = {}
        KGP = [KG, KG2]
        VGP = [VG, VG2]
        K3EP = [K3Eb, K3Eb2]
        K3TP = [K3T, K3T2]
        VTOKP = [VTOK, VTOK2]
        evhist = []

        def stage_Q(g):
            wg = WG[g % 2]
            pq = g % 2
            wt("T", "dG", WGL[g])
            for t3 in range(3):
                for si, (a, b) in enumerate(SL5):
                    w = b - a
                    half = ((t3 * 5 + si) % 2) * 512
                    if len(evhist) >= 2:
                        wt("T", evhist[-2][0], evhist[-2][1])
                    if g == 0:
                        wt("T", "V", ln2V[min(len(SL) - 1, (a + w) // 256)])
                    def mm_qkv(e, wg=wg, t3=t3, a=a, w=w, half=half):
                        ins = None
                        for kc in range(NCH):
                            ins = e.matmul(PSA[:, half:half + w], wg[:, kc, t3 * 128:t3 * 128 + 128],
                                           LN2BF[:, kc, a:a + w], start=(kc == 0), stop=(kc == NCH - 1))
                        return ins
                    tt = op("T", mm_qkv, "T")
                    if t3 == 0:
                        if g == 0 and si == 0:
                            wt("S", "dY", XSPILL_dY)
                        wt("S", "T", tt)
                        sv = op("S", lambda e, g=g, a=a, w=w, half=half: e.copy(
                            out=QALL[:, g, a:a + w], in_=PSA[:, half:half + w]), "S")
                        evhist.append(("S", sv))
                    elif t3 == 1:
                        wt("V", "T", tt)
                        if si == 0 and g >= 2:
                            wt("V", "T", k3vT[g - 2])
                        sv = op("V", lambda e, a=a, w=w, half=half, pq=pq: e.tensor_copy(
                            out=KGP[pq][:, a:a + w], in_=PSA[:, half:half + w]), "V")
                        evhist.append(("V", sv))
                    else:
                        wt("S", "T", tt)
                        if si == 0 and g >= 2:
                            wt("S", "T", k3vT[g - 2])
                        sv = op("S", lambda e, a=a, w=w, half=half, pq=pq: e.copy(
                            out=VGP[pq][:, a:a + w], in_=PSA[:, half:half + w]), "S")
                        evhist.append(("S", sv))
            kevV[g] = evhist[-6][1]
            vevS[g] = evhist[-1][1]
            lastWG_T[g] = cnt["T"]

        def stage_R(g):
            wg = WG[g % 2]
            pq = g % 2
            wt("T", "dG", PGL_dG)
            if g >= 1:
                wt("T", "V", qlmV[g - 1])
                wt("T", "S", k3eS[(g - 1, 3)])
            def mm_qlm(e, wg=wg):
                ins = None
                for kc in range(NCH):
                    ins = e.matmul(PSB[:, 512:576], wg[:, kc, 0:128], POOLEDG[:, kc, :],
                                   start=(kc == 0), stop=(kc == NCH - 1))
                return ins
            t_qlm = op("T", mm_qlm, "T")
            def mm_klm(e, wg=wg):
                ins = None
                for kc in range(NCH):
                    ins = e.matmul(PSB[:, 576:640], wg[:, kc, 128:256], POOLEDG[:, kc, :],
                                   start=(kc == 0), stop=(kc == NCH - 1))
                return ins
            klmT[g] = op("T", mm_klm, "T")
            if g + 2 < 8:
                wt("G", "T", klmT[g])
                for th in wg_load_thunks(g + 2, g % 2):
                    WGL[g + 2] = op("G", th, "dG", 16)
            wt("V", "T", klmT[g])
            wt("V", "V", BDZERO_V)
            op("V", lambda e, g=g: e.tensor_scalar(out=BDQ[0:64, g, 0:64], in0=PSB[0:64, 512:576],
                                                   scalar1=0.125, scalar2=None, op0=ALU.mult))
            op("V", lambda e, g=g: e.tensor_scalar(out=BDQ[64:128, g, 64:128], in0=PSB[64:128, 512:576],
                                                   scalar1=0.125, scalar2=None, op0=ALU.mult))
            qlmV[g] = op("V", lambda e, g=g: e.tensor_copy(out=KLM[:, g, :], in_=PSB[:, 576:640]), "V")
            wt("T", "V", qlmV[g])
            l2T[g] = op("T", lambda e, g=g: e.matmul(PSD[:, g * 64:g * 64 + 64], BDQ[:, g, :],
                                                     KLM[:, g, :], start=True, stop=True), "T")

            for si, (a, b) in enumerate(SLO):
                wt("T", "V", kevV[g])
                prev = k3eS.get((g, si - 2)) if si >= 2 else k3eS.get((g - 1, si + 2))
                if prev is not None:
                    wt("T", "S", prev)
                t3t = op("T", lambda e, g=g, a=a, si=si, pq=pq: e.matmul(
                    PSB[:, (si % 2) * 512:(si % 2) * 512 + 512], BDQ[:, g, :],
                    KGP[pq][:, a:a + 512], start=True, stop=True), "T")
                wt("S", "T", t3t)
                k3eS[(g, si)] = op("S", lambda e, g=g, si=si, pq=pq: e.activation(
                    out=K3EP[pq][:, si * 512:si * 512 + 512],
                    in_=PSB[:, (si % 2) * 512:(si % 2) * 512 + 512],
                    func=AF.Exp, accum_out=sums3p[:, g, si:si + 1]), "S")

            for half in range(2):
                base = half * 8
                wt("T", "S", k3eS[(g, 2 * half + 1)])
                prev_ev = k3tV.get((g, half - 1)) if half >= 1 else k3tV.get((g - 1, 1))
                if prev_ev is not None:
                    wt("T", "V", prev_ev)
                def mm_tr3(e, base=base, pq=pq):
                    ins = None
                    for j in range(8):
                        oc = base + j
                        ins = e.transpose(PST[:, j * 128:j * 128 + 128],
                                          K3EP[pq][:, oc * 128:oc * 128 + 128], identbf[:])
                    return ins
                tr = op("T", mm_tr3, "T")
                wt("V", "T", tr)
                if g >= 2 and half == 0:
                    wt("V", "T", k3vT[g - 2])
                k3tV[(g, half)] = op("V", lambda e, base=base, pq=pq: e.tensor_copy(
                    out=K3TP[pq][:, base:base + 8, :],
                    in_=PST[:].rearrange("p (a b) -> p a b", b=128)), "V")
            for half in range(2):
                base = half * 8
                wt("T", "S", vevS[g])
                if half == 0:
                    wt("T", "V", k3tV[(g, 1)])
                    prev_ev = vtokS.get((g - 1, 1))
                else:
                    prev_ev = vtokS.get((g, 0))
                if prev_ev is not None:
                    wt("T", "S", prev_ev)
                def mm_trv(e, base=base, pq=pq):
                    ins = None
                    for j in range(8):
                        a0 = CHO16[base + j][0]
                        ins = e.transpose(PST[:, j * 128:j * 128 + 128],
                                          VGP[pq][:, a0:a0 + 128], identbf[:])
                    return ins
                trv = op("T", mm_trv, "T")
                wt("S", "T", trv)
                if g >= 2 and half == 0:
                    wt("S", "T", k3vT[g - 2])
                vtokS[(g, half)] = op("S", lambda e, base=base, pq=pq: e.copy(
                    out=VTOKP[pq][:, base:base + 8, :],
                    in_=PST[:].rearrange("p (a b) -> p a b", b=128)), "S")

            wt("T", "V", k3tV[(g, 1)])
            wt("T", "S", vtokS[(g, 1)])
            def mm_k3v(e, g=g, pq=pq):
                ins = None
                for oc in range(16):
                    ins = e.matmul(PSC[:, g * 128:g * 128 + 128], K3TP[pq][:, oc, :],
                                   VTOKP[pq][:, oc, :], start=(oc == 0), stop=(oc == 15))
                return ins
            k3vT[g] = op("T", mm_k3v, "T")

        stage_Q(0)
        stage_Q(1)
        for g in range(8):
            stage_R(g)
            if g + 2 < 8:
                stage_Q(g + 2)

        QALL_S = cnt["S"]

        wt("V", "T", k3vT[7])
        op("V", lambda e: e.tensor_copy(
            out=CCS[0:64, 0:512].rearrange("p (g x) -> p g x", x=64),
            in_=PSC[0:64, :].rearrange("p (g x) -> p g x", x=128)[:, :, 0:64]))
        op("V", lambda e: e.tensor_copy(
            out=CCS[64:128, 0:512].rearrange("p (g x) -> p g x", x=64),
            in_=PSC[64:128, :].rearrange("p (g x) -> p g x", x=128)[:, :, 64:128]))
        wt("V", "S", k3eS[(7, 3)])
        CCS_V = op("V", lambda e: e.tensor_reduce(
            out=CCS[:, 512:520], in_=sums3p[:], axis=AX.X, op=ALU.add), "V")
        wt("G", "V", CCS_V)
        op("G", lambda e: e.dma_start(out=cc4in.ap(), in_=CCS[:, 0:520]), "dG", 16)
        wt("G", "dG", cnt["dG"])
        CC4 = op("G", lambda e: e.collective_compute(
            "AllReduce", ALU.add, replica_groups=RG,
            ins=[cc4in.ap().opt()], outs=[cc4out.ap().opt()]), "CC", 1)
        wt("G", "CC", CC4)
        CCR_dG = op("G", lambda e: e.dma_start(out=CCR[:, 0:520], in_=cc4out.ap()), "dG", 16)

        snap(3, ("dG", CCR_dG), CCR)
        wt("G", "T", k3vT[7])
        WPROJ_dG = op("G", lambda e: e.dma_start(
            out=WPROJ[:], in_=wproj_d.rearrange("(c p) d -> p c d", p=128)), "dG", 16)

        wt("S", "T", l2T[7])
        wt("S", "T", k3vT[7])
        K2e_S = op("S", lambda e: e.activation(out=K2E[:], in_=PSD[:].rearrange("p (g x) -> p g x", x=64),
                                               func=AF.Exp), "S")
        wt("V", "S", K2e_S)
        op("V", lambda e: e.tensor_reduce(out=rc8[:], in_=K2E[:], axis=AX.X, op=ALU.add))
        op("V", lambda e: e.drain())
        op("V", lambda e: e.reciprocal(out=rc8[:], in_=rc8[:]))
        op("V", lambda e: e.drain())
        K2n_V = op("V", lambda e: e.tensor_tensor(out=K2E[:], in0=K2E[:],
                                                  in1=rc8[:].unsqueeze(2).broadcast_to([128, 8, 64]),
                                                  op=ALU.mult), "V")
        wt("T", "V", K2n_V)
        def mm_cs(e):
            ins = None
            for g in range(8):
                ins = e.matmul(PSB[0:2, g * 64:g * 64 + 64], onesbd2[:], K2E[:, g, :],
                               start=True, stop=True)
            return ins
        cs_T = op("T", mm_cs, "T")
        wt("V", "T", cs_T)
        op("V", lambda e: e.tensor_copy(out=cs2[:], in_=PSB[0:2, 0:512].rearrange("p (g x) -> p g x", x=64)))
        op("V", lambda e: e.tensor_reduce(out=mx2[:], in_=cs2[:], axis=AX.X, op=ALU.max))
        op("V", lambda e: e.drain())
        mx_V = op("V", lambda e: e.reciprocal(out=mx2[:], in_=mx2[:]), "V")
        op("V", lambda e: e.drain())
        wt("T", "V", mx_V)
        dn_T = op("T", lambda e: e.matmul(PSB[:, 512:520], b2c[:], mx2[:, :], start=True, stop=True), "T")
        wt("V", "T", dn_T)
        dn_V = op("V", lambda e: e.tensor_copy(out=denr[:], in_=PSB[:, 512:520]), "V")
        op("V", lambda e: e.drain())

        op("V", lambda e: e.memset(BDP[:], 0.0))
        op("V", lambda e: e.tensor_copy(out=BDP[0:64, :, 0:64], in_=K2E[0:64, :, :]))
        bdk_V = op("V", lambda e: e.tensor_copy(out=BDP[64:128, :, 64:128], in_=K2E[64:128, :, :]), "V")
        wt("T", "V", bdk_V)
        def mm_trk(e):
            ins = None
            for g in range(8):
                ins = e.transpose(PST[:, g * 128:g * 128 + 128], BDP[:, g, :], identbf[:])
            return ins
        trk_T = op("T", mm_trk, "T")
        wt("V", "T", trk_T)
        op("V", lambda e: e.tensor_copy(out=KTBD[:], in_=PST[:].rearrange("p (a b) -> p a b", b=128)))
        wt("V", "V", dn_V)
        op("V", lambda e: e.tensor_tensor(out=VS_[0:64, :, :], in0=KTBD[0:64, :, 0:64],
                                          in1=denr[0:64, :].unsqueeze(2).broadcast_to([64, 8, 64]), op=ALU.mult))
        op("V", lambda e: e.tensor_tensor(out=VS_[64:128, :, :], in0=KTBD[64:128, :, 64:128],
                                          in1=denr[64:128, :].unsqueeze(2).broadcast_to([64, 8, 64]), op=ALU.mult))
        itV = op("V", lambda e: e.tensor_tensor(out=VTBD[:], in0=BDP[:],
                                                in1=denr[:].unsqueeze(2).broadcast_to([128, 8, 128]), op=ALU.mult), "V")

        pst_user = [None]

        def newton_iter(it, state):
            itV = state["itV"]
            wt("T", "V", itV)
            if it == 0:
                wt("T", "V", CCS_V)
            def mm_P(e):
                ins = None
                for g in range(8):
                    ins = e.matmul(PSC[:, g * 64:g * 64 + 64], KTBD[:, g, :], VS_[:, g, :],
                                   start=True, stop=True)
                return ins
            P_T = op("T", mm_P, "T")
            wt("V", "T", P_T)
            ns1_V = op("V", lambda e: e.scalar_tensor_tensor(
                out=NS1[:], in0=PSC[:, 0:512].rearrange("p (g x) -> p g x", x=64),
                scalar=-1.0, in1=ints[:, 0, :].unsqueeze(1).broadcast_to([128, 8, 64]),
                op0=ALU.mult, op1=ALU.add), "V")
            wt("S", "T", P_T)
            op("S", lambda e: e.copy(out=BDP[0:64, :, 0:64],
                                     in_=PSC[0:64, 0:512].rearrange("p (g x) -> p g x", x=64)))
            bdp_V = op("S", lambda e: e.copy(out=BDP[64:128, :, 64:128],
                                             in_=PSC[64:128, 0:512].rearrange("p (g x) -> p g x", x=64)), "S")
            wt("T", "S", bdp_V)
            if pst_user[0] is not None:
                wt("T", pst_user[0][0], pst_user[0][1])
            def mm_trp(e):
                ins = None
                for g in range(8):
                    ins = e.transpose(PST[:, g * 128:g * 128 + 128], BDP[:, g, :], identbf[:])
                return ins
            trp_T = op("T", mm_trp, "T")
            wt("S", "T", trp_T)
            ptbd_V = op("S", lambda e: e.copy(out=PTBD[:], in_=PST[:].rearrange("p (a b) -> p a b", b=128)), "S")
            pst_user[0] = ("S", ptbd_V)
            wt("T", "S", ptbd_V)
            wt("T", "V", ns1_V)
            def mm_T2(e):
                ins = None
                for g in range(8):
                    ins = e.matmul(PSC[:, 512 + g * 64:512 + g * 64 + 64], PTBD[:, g, :], NS1[:, g, :],
                                   start=True, stop=True)
                return ins
            T2_T = op("T", mm_T2, "T")
            wt("V", "T", T2_T)
            s2_V = op("V", lambda e: e.scalar_tensor_tensor(
                out=NS2[:], in0=PSC[:, 512:1024].rearrange("p (g x) -> p g x", x=64),
                scalar=-1.0, in1=ints[:, 1, :].unsqueeze(1).broadcast_to([128, 8, 64]),
                op0=ALU.mult, op1=ALU.add), "V")
            wt("T", "V", s2_V)
            wt("T", "V", ns1_V)
            wt("T", "S", bdp_V)
            def mm_T3(e):
                ins = None
                for g in range(8):
                    ins = e.matmul(PSC[:, g * 64:g * 64 + 64], PTBD[:, g, :], NS2[:, g, :],
                                   start=True, stop=True)
                return ins
            T3_T = op("T", mm_T3, "T")
            wt("V", "T", T3_T)
            s3_V = op("V", lambda e: e.scalar_tensor_tensor(
                out=NS3[:], in0=PSC[:, 0:512].rearrange("p (g x) -> p g x", x=64),
                scalar=-1.0, in1=ints[:, 2, :].unsqueeze(1).broadcast_to([128, 8, 64]),
                op0=ALU.mult, op1=ALU.add), "V")
            wt("T", "V", s3_V)
            def mm_Vn(e):
                ins = None
                for g in range(8):
                    ins = e.matmul(PSC[:, 512 + g * 64:512 + g * 64 + 64], VTBD[:, g, :], NS3[:, g, :],
                                   start=True, stop=True)
                return ins
            Vn_T = op("T", mm_Vn, "T")
            wt("V", "T", Vn_T)
            op("V", lambda e: e.tensor_scalar(out=VS_[:], in0=PSC[:, 512:1024].rearrange("p (g x) -> p g x", x=64),
                                              scalar1=0.25, scalar2=None, op0=ALU.mult))
            op("V", lambda e: e.tensor_scalar(out=BDP[0:64, :, 0:64],
                                              in0=PSC[0:64, 512:1024].rearrange("p (g x) -> p g x", x=64),
                                              scalar1=0.25, scalar2=None, op0=ALU.mult))
            bdv_V = op("V", lambda e: e.tensor_scalar(out=BDP[64:128, :, 64:128],
                                                      in0=PSC[64:128, 512:1024].rearrange("p (g x) -> p g x", x=64),
                                                      scalar1=0.25, scalar2=None, op0=ALU.mult), "V")
            wt("T", "V", bdv_V)
            if pst_user[0] is not None:
                wt("T", pst_user[0][0], pst_user[0][1])
            def mm_trv2(e):
                ins = None
                for g in range(8):
                    ins = e.transpose(PST[:, g * 128:g * 128 + 128], BDP[:, g, :], identbf[:])
                return ins
            trv2_T = op("T", mm_trv2, "T")
            wt("S", "T", trv2_T)
            itS = op("S", lambda e: e.copy(out=VTBD[:], in_=PST[:].rearrange("p (a b) -> p a b", b=128)), "S")
            pst_user[0] = ("S", itS)
            wt("V", "S", itS)
            state["itV"] = op("V", lambda e: e.engine_nop() if False else e.nop(), "V")

        wt("V", "V", qlmV[7])
        op("V", lambda e: e.tensor_copy(out=BDKLM[0:64, :, 0:64], in_=KLM[0:64, :, :]))
        bdklm_V = op("V", lambda e: e.tensor_copy(out=BDKLM[64:128, :, 64:128], in_=KLM[64:128, :, :]), "V")
        e1S = {}
        e1V = {}
        trS = {}

        tr1T = {}

        def l1_mm(ci):
            c0, c1 = CH17[ci]
            wc = c1 - c0
            pg = ci % 2
            s3 = ci % 3
            PSL = PSA if pg == 0 else PSB
            wt("T", "V", bdklm_V)
            if ci == 0:
                wt("T", "S", QALL_S)
                wt("T", "V", kevV[7])
            if ci == 1:
                wt("T", "S", k3eS[(7, 3)])
                wt("T", "V", dn_V)
            prev = e1S.get(ci - 2)
            if prev is not None:
                wt("T", "S", prev)
            def mm_l1(e, c0=c0, wc=wc, PSL=PSL):
                ins = None
                for g in range(8):
                    ins = e.matmul(PSL[0:wc, g * 128:g * 128 + 128], QALL[:, g, c0:c0 + wc],
                                   BDKLM[:, g, :], start=True, stop=True)
                return ins
            e1T = op("T", mm_l1, "T")
            wt("S", "T", e1T)
            prevv = e1V.get(ci - 3)
            if prevv is not None:
                wt("S", "V", prevv)
            if ci - 3 in tr1T:
                wt("S", "T", tr1T[ci - 3])
            e1S[ci] = op("S", lambda e, wc=wc, s3=s3, PSL=PSL: e.activation(
                out=E1B[s3][0:wc, :], in_=PSL[0:wc, :], func=AF.Exp, scale=0.125), "S")
            wt("V", "S", e1S[ci])
            op("V", lambda e, wc=wc, s3=s3: e.tensor_reduce(
                out=rc16[0:wc, s3 * 16:s3 * 16 + 16],
                in_=E1B[s3][0:wc, :].rearrange("p (h x) -> p h x", x=64),
                axis=AX.X, op=ALU.add))

        def l1_recip(ci, guard=False):
            wc = CH17[ci][1] - CH17[ci][0]
            s3 = ci % 3
            if guard:
                op("V", lambda e: e.drain())
            op("V", lambda e, wc=wc, s3=s3: e.reciprocal(
                out=rc16[0:wc, s3 * 16:s3 * 16 + 16], in_=rc16[0:wc, s3 * 16:s3 * 16 + 16]))

        def l1_fin(ci, guard=False):
            c0, c1 = CH17[ci]
            wc = c1 - c0
            s3 = ci % 3
            if guard:
                op("V", lambda e: e.drain())
            e1V[ci] = op("V", lambda e, wc=wc, s3=s3: e.tensor_tensor(
                out=E1B[s3][0:wc, :].rearrange("p (h x) -> p h x", x=64),
                in0=E1B[s3][0:wc, :].rearrange("p (h x) -> p h x", x=64),
                in1=rc16[0:wc, s3 * 16:s3 * 16 + 16].unsqueeze(2).broadcast_to([wc, 16, 64]),
                op=ALU.mult), "V")
            wt("T", "V", e1V[ci])
            if pst_user[0] is not None:
                wt("T", pst_user[0][0], pst_user[0][1])
            def mm_tr1(e, wc=wc, s3=s3):
                ins = None
                for g in range(8):
                    ins = e.transpose(PST[:, g * wc:g * wc + wc],
                                      E1B[s3][0:wc, g * 128:g * 128 + 128], identbf[0:wc, 0:wc])
                return ins
            tr1T[ci] = op("T", mm_tr1, "T")
            wt("S", "T", tr1T[ci])
            trS[ci] = op("S", lambda e, c0=c0, wc=wc: e.copy(
                out=K1T[:, :, c0:c0 + wc],
                in_=PST[:, 0:8 * wc].rearrange("p (g x) -> p g x", x=wc)), "S")
            pst_user[0] = ("S", trS[ci])

        nw_state = {"itV": itV}
        l1_sched = [[0, 1], [2, 3], [4, 5], [6, 7], [8, 9], [10, 11, 12], [13, 14, 15, 16]]
        for step in range(7):
            for ci in l1_sched[step]:
                l1_mm(ci)
                if ci - 1 >= 0:
                    l1_recip(ci - 1)
                if ci - 2 >= 0:
                    l1_fin(ci - 2)
            if step < 6:
                newton_iter(step, nw_state)
        l1_fin(15)
        l1_recip(16)
        l1_fin(16, guard=True)
        NEWT_V = nw_state["itV"]
        K1T_S = cnt["S"]
        snap(4, ("S", K1T_S), K1T[:, :, 0:2048])

        wt("V", "dG", CCR_dG)
        op("V", lambda e: e.reciprocal(out=s3tot[:], in_=CCR[:, 512:520]))
        op("V", lambda e: e.drain())
        k3vn_V = op("V", lambda e: e.tensor_tensor(
            out=K3VN[:], in0=CCR[:, 0:512].rearrange("p (g x) -> p g x", x=64),
            in1=s3tot[:].unsqueeze(2).broadcast_to([128, 8, 64]), op=ALU.mult), "V")
        wt("T", "V", k3vn_V)
        wt("T", "V", NEWT_V)
        def mm_Z(e):
            ins = None
            for g in range(8):
                ins = e.matmul(PSD[:, g * 64:g * 64 + 64], VTBD[:, g, :], K3VN[:, g, :],
                               start=True, stop=True)
            return ins
        Z_T = op("T", mm_Z, "T")
        wt("V", "T", Z_T)
        op("V", lambda e: e.tensor_copy(out=BDQ[0:64, :, 0:64],
                                        in_=PSD[0:64, :].rearrange("p (g x) -> p g x", x=64)))
        bdz_V = op("V", lambda e: e.tensor_copy(out=BDQ[64:128, :, 64:128],
                                                in_=PSD[64:128, :].rearrange("p (g x) -> p g x", x=64)), "V")

        svS = {}
        flat = []
        for g in range(8):
            for si, (a, b) in enumerate(SL5):
                flat.append((g, si, a, b))
        for n, (g, si, a, b) in enumerate(flat):
            w = b - a
            half = (n % 2) * 512
            wt("T", "V", bdz_V)
            wt("T", "S", K1T_S)
            if n >= 2:
                wt("T", "S", svS[n - 2])
            def mm_sv(e, g=g, a=a, w=w, half=half):
                return e.matmul(PSA[:, half:half + w], BDQ[:, g, :], K1T[:, g, a:a + w],
                                start=True, stop=True)
            svT = op("T", mm_sv, "T")
            wt("S", "T", svT)
            svS[n] = op("S", lambda e, g=g, a=a, w=w, half=half: e.copy(
                out=SVALL[:, g, a:a + w], in_=PSA[:, half:half + w]), "S")
        SV_S = cnt["S"]

        wt("Y", "S", K1T_S)
        XRL1_dY = op("Y", lambda e: e.dma_start(out=X[:, 0:4, :], in_=xsp[:, 0:4, :]), "dY", 16)
        wt("Y", "S", SV_S)
        XRL2_dY = op("Y", lambda e: e.dma_start(out=X[:, 4:8, :], in_=xsp[:, 4:8, :]), "dY", 16)

        projV = {}
        flat = []
        for dp in range(NCH):
            for si, (a, b) in enumerate(SL5):
                flat.append((dp, si, a, b))
        for n, (dp, si, a, b) in enumerate(flat):
            w = b - a
            half = (n % 2) * 512
            if n == 0:
                wt("T", "S", SV_S)
                wt("T", "dG", WPROJ_dG)
            if n >= 2:
                wt("T", "V", projV[n - 2])
            def mm_pj(e, dp=dp, a=a, w=w, half=half):
                ins = None
                for c in range(NCH):
                    ins = e.matmul(PSB[:, half:half + w], WPROJ[:, c, dp * 128:dp * 128 + 128],
                                   SVALL[:, c, a:a + w], start=(c == 0), stop=(c == NCH - 1))
                return ins
            pjT = op("T", mm_pj, "T")
            wt("V", "T", pjT)
            if n == 0:
                wt("V", "dY", XRL1_dY)
            if dp == 4 and si == 0:
                wt("V", "dY", XRL2_dY)
            projV[n] = op("V", lambda e, dp=dp, a=a, w=w, half=half: e.tensor_tensor(
                out=X[:, dp, a:a + w], in0=X[:, dp, a:a + w], in1=PSB[:, half:half + w],
                op=ALU.add), "V")
        MASK_V = op("V", lambda e: e.tensor_scalar(out=X[:, :, 0:2], in0=X[:, :, 0:2],
                                                   scalar1=bmask_sb[:, 0:1], scalar2=None,
                                                   op0=ALU.mult), "V")
        PROJ_T = cnt["T"]
        snap(5, ("V", MASK_V), X[:, :, 2:T])

        rows3_of = make_rows(None, "V", MASK_V,
                             slice_gates=[("V", MASK_V if i == 0 else projV[35 + min(4, i // 2)])
                                          for i in range(len(SL))])
        rows3_of.prefetch = 9
        rows3_of(8)
        wt("G", "T", PROJ_T)
        wt("G", "V", MASK_V)
        op("G", lambda e: e.dma_start(out=WKT, in_=wkt_d.rearrange("(c p) d -> p c d", p=128)), "dG", 16)
        FFNW2K_dG = cnt["dG"]
        op("G", lambda e: e.dma_start(out=WRT, in_=wrt_d.rearrange("(c p) d -> p c d", p=128)), "dG", 16)
        FFNW2R_dG = cnt["dG"]
        op("G", lambda e: e.dma_start(out=WVT, in_=wvt_d.rearrange("(c p) d -> p c d", p=128)), "dG", 16)
        FFNW2_dG = cnt["dG"]
        ffn2_G, _, ffn2_addG = emit_ffn(rows3_of, (FFNW2K_dG, FFNW2R_dG, FFNW2_dG),
                                        stats_S=rows3_of.last_S)

        out_pct = out_d.rearrange("(c p) t -> p c t", p=128)
        for i, (a, b) in enumerate(SL):
            a2 = max(a, 2)
            wt("Y", "G", ffn2_addG[i * NCH + NCH - 1])
            op("Y", lambda e, a2=a2, b=b: e.dma_start(
                out=out_pct[:, :, a2 - 2:b - 2], in_=X[:, :, a2:b]), "dY", 16)
        OUT_dY = cnt["dY"]
        wt("Y", "dY", OUT_dY)

        if stage in snaps:
            lens, cnts, (gsem, gval), src_ap = snaps[stage]
            import os as _os
            if _os.environ.get("KERNEL_DEBUG_SNAPS"):
                print("SNAP", stage, "lens", lens, "gate", gsem, gval, "cnts", cnts)
            for k in streams:
                del streams[k][lens[k]:]
            for k in cnt:
                cnt[k] = cnts[k]
            wt("G", gsem, gval)
            if src_ap == "ROWS121":
                dd = op("G", lambda e: e.dma_start(out=out_d[0:1, 0:2048], in_=srow[:, 2:2050]), "dG", 16)
                dd = op("G", lambda e: e.dma_start(out=out_d[1:2, 0:2048], in_=vrow[:, 2:2050]), "dG", 16)
                dd = op("G", lambda e: e.dma_start(out=out_d[2:3, 0:2048], in_=qrow[:, 2:2050]), "dG", 16)
            elif src_ap == "LNSL131":
                dd = op("G", lambda e: e.dma_start(out=out_d[0:128, 0:2048],
                                                   in_=LNSL[:].rearrange("p a b -> p (a b)")[:, 0:2048]), "dG", 16)
                dd = op("G", lambda e: e.dma_start(out=out_d[128:256, 0:257], in_=BCN[:]), "dG", 16)
                dd = op("G", lambda e: e.dma_start(out=out_d[256:384, 0:257], in_=BCS[:]), "dG", 16)
            elif src_ap == "XRK132":
                dd = op("G", lambda e: e.dma_start(out=out_d[0:128, 0:2048],
                                                   in_=XR2[:].rearrange("p a b c -> p (a b c)")[:, 0:2048]), "dG", 16)
                dd = op("G", lambda e: e.dma_start(out=out_d[128:256, 0:2048],
                                                   in_=XK2[:].rearrange("p a b c -> p (a b c)")[:, 0:2048]), "dG", 16)
                dd = op("G", lambda e: e.dma_start(out=out_d[256:384, 0:2048],
                                                   in_=KACT[:].rearrange("p a b -> p (a b)")), "dG", 16)
            elif src_ap == "ATTN31":
                dd = op("G", lambda e: e.dma_start(out=out_d[0:128, 0:2048], in_=K3Eb), "dG", 16)
                dd = op("G", lambda e: e.dma_start(out=out_d[128:256, 0:1024],
                                                   in_=BDQ[:].rearrange("p a b -> p (a b)")), "dG", 16)
                dd = op("G", lambda e: e.dma_start(out=out_d[256:384, 0:512],
                                                   in_=KLM[:].rearrange("p a b -> p (a b)")), "dG", 16)
                dd = op("G", lambda e: e.dma_start(out=out_d[384:512, 0:2048], in_=KG[:, 0:2048]), "dG", 16)
                dd = op("G", lambda e: e.dma_start(out=out_d[512:640, 0:256],
                                                   in_=POOLED[:].rearrange("p a b -> p (a b)")), "dG", 16)
                dd = op("G", lambda e: e.dma_start(out=out_d[640:768, 0:512],
                                                   in_=POOLEDG[:].rearrange("p a b -> p (a b)")), "dG", 16)
            elif src_ap == "LN2BF21":
                dd = op("G", lambda e: e.dma_start(
                    out=out_d.rearrange("(c p) t -> p c t", p=128),
                    in_=LN2BF[:, :, 2:T]), "dG", 16)
            elif src_ap is CCR:
                dd = op("G", lambda e: e.dma_start(out=out_d[0:128, 0:520], in_=src_ap), "dG", 16)
            else:
                dd = op("G", lambda e: e.dma_start(
                    out=out_d.rearrange("(c p) t -> p c t", p=128), in_=src_ap), "dG", 16)
            wt("G", "dG", dd)

        @block.tensor
        def _(e):
            for f in streams["T"]:
                f(e)

        @block.vector
        def _(e):
            for f in streams["V"]:
                f(e)

        @block.scalar
        def _(e):
            for f in streams["S"]:
                f(e)

        @block.gpsimd
        def _(e):
            for f in streams["G"]:
                f(e)

        @block.sync
        def _(e):
            for f in streams["Y"]:
                f(e)

    return nc


_NC_CACHE = {}


def kernel(L0_attn_x, L0_attn_w_qkv, L0_attn_w_proj, L0_ffn_W_r, L0_ffn_W_k,
           L0_ffn_W_v, L0_ffn_mu_r, L0_ffn_mu_k):
    x = np.ascontiguousarray(np.asarray(L0_attn_x, dtype=np.float32))
    w_qkv = np.ascontiguousarray(np.asarray(L0_attn_w_qkv, dtype=np.float32))
    w_proj = np.ascontiguousarray(np.asarray(L0_attn_w_proj, dtype=np.float32))
    wrt = np.ascontiguousarray(np.asarray(L0_ffn_W_r, dtype=np.float32).T)
    wkt = np.ascontiguousarray(np.asarray(L0_ffn_W_k, dtype=np.float32).T)
    wvt = np.ascontiguousarray(np.asarray(L0_ffn_W_v, dtype=np.float32).T)
    mu = np.stack([np.asarray(L0_ffn_mu_r, dtype=np.float32).reshape(8, 128),
                   np.asarray(L0_ffn_mu_k, dtype=np.float32).reshape(8, 128)], axis=0)
    mu = np.ascontiguousarray(mu)

    ident = np.eye(128, dtype=np.float32).astype(ml_dtypes.bfloat16)
    i64 = np.concatenate([np.eye(64, dtype=np.float32)] * 2, axis=0)
    ints = np.ascontiguousarray(
        np.stack([7.0 * i64, 15.0 * i64, 13.0 * i64], axis=1).astype(ml_dtypes.bfloat16))
    onescol = np.ones((128, 1), dtype=np.float32).astype(ml_dtypes.bfloat16)
    onesbd2 = np.zeros((128, 2), dtype=np.float32)
    onesbd2[0:64, 0] = 1.0
    onesbd2[64:128, 1] = 1.0
    onesbd2 = onesbd2.astype(ml_dtypes.bfloat16)
    bconst = np.zeros((1, 2, 128), dtype=np.float32)
    bconst[0, 0, :] = -1.0 / 1024.0
    bconst[0, 1, :] = 1.0
    b2 = np.zeros((2, 128), dtype=np.float32)
    b2[0, 0:64] = 1.0
    b2[1, 64:128] = 1.0

    import os
    _stage = int(os.environ.get("KERNEL_STAGE", "0"))
    if ("nc", _stage) not in _NC_CACHE:
        _NC_CACHE[("nc", _stage)] = build_nc(_stage)
    nc = _NC_CACHE[("nc", _stage)]

    in_maps = []
    for core in range(8):
        b = core // 2
        half = core % 2
        if half == 0:
            tile = np.concatenate([np.zeros((2, 1024), np.float32), x[b, 0:2048]], axis=0)
            bm = np.zeros((128, 1), np.float32)
        else:
            tile = x[b, 2046:4096]
            bm = np.ones((128, 1), np.float32)
        xtile = np.ascontiguousarray(tile.T)
        in_maps.append({
            "xt": xtile, "wqkv": w_qkv, "wproj": w_proj, "wrt": wrt, "wkt": wkt,
            "wvt": wvt, "mud": mu, "bmaskd": bm, "identbfd": ident, "intsd": ints,
            "onescold": onescol, "onesbd2d": onesbd2, "bconstd": bconst,
            "b2d": b2,
        })

    res = run_bass_kernel_spmd(nc, in_maps, core_ids=list(range(8)))
    out = np.empty((4, 4096, 1024), dtype=np.float32)
    for core in range(8):
        b = core // 2
        half = core % 2
        o = res.results[core]["out"]
        out[b, half * 2048:(half + 1) * 2048] = o.T
    return out


# version 18
# speedup vs baseline: 1.1013x; 1.0155x over previous
import numpy as np
import ml_dtypes
import concourse.bass as bass
import concourse.mybir as mybir
from concourse.bass_utils import run_bass_kernel_spmd

F32 = mybir.dt.float32
BF16 = mybir.dt.bfloat16
AF = mybir.ActivationFunctionType
ALU = mybir.AluOpType
AX = mybir.AxisListType

T = 2050
NCH = 8
EPS = 1e-5

SL = [(i * 256, (i + 1) * 256) for i in range(8)] + [(2048, 2050)]
SL5 = [(0, 512), (512, 1024), (1024, 1536), (1536, 2048), (2048, 2050)]
SLO = [(2 + i * 512, 2 + (i + 1) * 512) for i in range(4)]
CH17 = [(i * 128, i * 128 + 128) for i in range(16)] + [(2048, 2050)]
CHO16 = [(2 + i * 128, 2 + i * 128 + 128) for i in range(16)]
RG = [[0, 1], [2, 3], [4, 5], [6, 7]]


def build_nc(stage=0):
    import os
    SKIP = set(os.environ.get("KERNEL_SKIP", "").split(","))
    nc = bass.Bass("TRN2", target_bir_lowering=False)

    xt = nc.dram_tensor("xt", [1024, T], F32, kind="ExternalInput").ap()
    wqkv = nc.dram_tensor("wqkv", [1024, 3072], F32, kind="ExternalInput").ap()
    wproj_d = nc.dram_tensor("wproj", [1024, 1024], F32, kind="ExternalInput").ap()
    wrt_d = nc.dram_tensor("wrt", [1024, 1024], F32, kind="ExternalInput").ap()
    wkt_d = nc.dram_tensor("wkt", [1024, 1024], F32, kind="ExternalInput").ap()
    wvt_d = nc.dram_tensor("wvt", [1024, 1024], F32, kind="ExternalInput").ap()
    mu_d = nc.dram_tensor("mud", [2, 8, 128], F32, kind="ExternalInput").ap()
    bmask_d = nc.dram_tensor("bmaskd", [128, 1], F32, kind="ExternalInput").ap()
    identbf_d = nc.dram_tensor("identbfd", [128, 128], BF16, kind="ExternalInput").ap()
    ints_d = nc.dram_tensor("intsd", [128, 3, 64], BF16, kind="ExternalInput").ap()
    onescol_d = nc.dram_tensor("onescold", [128, 1], BF16, kind="ExternalInput").ap()
    onesbd2_d = nc.dram_tensor("onesbd2d", [128, 2], BF16, kind="ExternalInput").ap()
    bconst_d = nc.dram_tensor("bconstd", [1, 2, 128], F32, kind="ExternalInput").ap()
    b2_d = nc.dram_tensor("b2d", [2, 128], F32, kind="ExternalInput").ap()

    out_d = nc.dram_tensor("out", [1024, 2048], F32, kind="ExternalOutput").ap()

    xsp = nc.dram_tensor("xsp", [128, NCH, T], F32).ap()
    cc3in = nc.dram_tensor("cc3in", [1024, 32], BF16)
    cc3out = nc.dram_tensor("cc3out", [2, 1024, 32], BF16)
    cc4in = nc.dram_tensor("cc4in", [128, 520], F32)
    cc4out = nc.dram_tensor("cc4out", [128, 520], F32)

    streams = {e: [] for e in ("T", "V", "S", "G", "Y")}
    cnt = {"T": 0, "V": 0, "S": 0, "G": 0, "dY": 0, "dG": 0, "CC": 0}
    semh = {}

    def op(eng, fn, inc=None, n=1):
        if inc is None:
            streams[eng].append(lambda e, fn=fn: fn(e))
            return None
        cnt[inc] += n
        def thunk(e, fn=fn, inc=inc, n=n):
            fn(e).then_inc(semh[inc], n)
        streams[eng].append(thunk)
        return cnt[inc]

    def wt(eng, sem, val):
        if val is not None and val > 0:
            streams[eng].append(lambda e, sem=sem, val=val: e.wait_ge(semh[sem], val))

    from contextlib import ExitStack
    with ExitStack() as stack:
        ec = stack.enter_context
        XRb = ec(nc.sbuf_tensor("XR", [128, 16400], F32))
        WA = ec(nc.sbuf_tensor("WA", [128, 24576], BF16))
        A3 = ec(nc.sbuf_tensor("A3", [128, 16400], BF16))
        SCR = ec(nc.sbuf_tensor("SCR", [128, 4618], F32))
        GT2 = ec(nc.sbuf_tensor("GT2", [128, 4100], BF16))
        ROWS = ec(nc.sbuf_tensor("ROWS", [128, 4, 2050], F32))
        identbf = ec(nc.sbuf_tensor("identbf", [128, 128], BF16))
        ints = ec(nc.sbuf_tensor("ints", [128, 3, 64], BF16))
        onescol = ec(nc.sbuf_tensor("onescol", [128, 1], BF16))
        onesbd2 = ec(nc.sbuf_tensor("onesbd2", [128, 2], BF16))
        bconst = ec(nc.sbuf_tensor("bconst", [1, 2, 128], F32))
        b2c = ec(nc.sbuf_tensor("b2c", [2, 128], F32))
        mu_sb = ec(nc.sbuf_tensor("mu_sb", [128, 2, 8], F32))
        omu_sb = ec(nc.sbuf_tensor("omu_sb", [128, 2, 8], F32))
        bmask_sb = ec(nc.sbuf_tensor("bmask_sb", [128, 1], F32))
        sums3p = ec(nc.sbuf_tensor("sums3p", [128, 8, 4], F32))
        s3tot = ec(nc.sbuf_tensor("s3tot", [128, 8], F32))
        denr = ec(nc.sbuf_tensor("denr", [128, 8], F32))
        rc16 = ec(nc.sbuf_tensor("rc16", [128, 48], F32))
        rc8 = ec(nc.sbuf_tensor("rc8", [128, 8], F32))
        cs2 = ec(nc.sbuf_tensor("cs2", [2, 8, 64], F32))
        mx2 = ec(nc.sbuf_tensor("mx2", [2, 8], F32))
        PSA = ec(nc.psum_tensor("PSA", [128, 1024], F32))
        PSB = ec(nc.psum_tensor("PSB", [128, 1024], F32))
        PSC = ec(nc.psum_tensor("PSC", [128, 1024], F32))
        PSD = ec(nc.psum_tensor("PSD", [128, 512], F32))
        PST = ec(nc.psum_tensor("PST", [128, 1024], BF16))
        semT = ec(nc.semaphore("semT"))
        semV = ec(nc.semaphore("semV"))
        semS = ec(nc.semaphore("semS"))
        semG = ec(nc.semaphore("semG"))
        dY = ec(nc.semaphore("dY"))
        dG = ec(nc.semaphore("dG"))
        scc = ec(nc.semaphore("scc"))
        block = ec(nc.Block())
        semh.update({"T": semT, "V": semV, "S": semS, "G": semG,
                     "dY": dY, "dG": dG, "CC": scc})

        def view(base, off, shape):
            n = int(np.prod(shape))
            apx = base[:, off:off + n]
            if len(shape) == 2:
                apx = apx.rearrange("p (a b) -> p a b", b=shape[1])
            elif len(shape) == 3:
                apx = apx.rearrange("p (a b c) -> p a b c", b=shape[1], c=shape[2])
            return apx

        X = view(XRb, 0, [NCH, T])
        XRbf = XRb[:, :].bitcast(BF16)
        QALL = view(XRbf, 0, [NCH, T])
        K1T = view(XRbf, NCH * T, [NCH, T])
        PL1 = view(XRb, 8200, [NCH, 1024])

        WRT = view(WA, 0, [NCH, 1024])
        WKT = view(WA, 8192, [NCH, 1024])
        WVT = view(WA, 16384, [NCH, 1024])
        WG = [view(WA, 0, [NCH, 384]), view(WA, 3072, [NCH, 384])]
        WG4 = [view(WA, 0, [NCH, 3, 128]), view(WA, 3072, [NCH, 3, 128])]
        BDQ = view(WA, 6144, [NCH, 128])
        K2E = view(WA, 7168, [NCH, 64])
        BDKLM = view(WA, 7680, [NCH, 128])
        KLM = view(WA, 8704, [NCH, 64])
        POOLED = view(WA, 9216, [NCH, 32])
        POOLEDG = view(WA, 9472, [NCH, 64])
        KG = WA[:, 9984:9984 + T]
        VG = WA[:, 12034:12034 + T]
        K3Eb = WA[:, 14084:14084 + 2048]
        K3T = view(WA, 16132, [16, 128])
        VTOK = view(WA, 18180, [16, 128])
        E1B = [WA[:, 20228:20228 + 1024], WA[:, 21252:21252 + 1024],
               WA[:, 22276:22276 + 1024]]
        KG2 = GT2[:, 0:T]
        VG2 = GT2[:, T:2 * T]
        K3Eb2 = K3Eb
        K3T2 = K3T
        VTOK2 = VTOK
        KTBD = view(WA, 0, [NCH, 128])
        PTBD = view(WA, 1024, [NCH, 128])
        VTBD = view(WA, 2048, [NCH, 128])
        VS_ = view(WA, 3072, [NCH, 64])
        NS1 = view(WA, 3584, [NCH, 64])
        NS2 = view(WA, 4096, [NCH, 64])
        NS3 = view(WA, 4608, [NCH, 64])
        BDP = view(WA, 5120, [NCH, 128])
        K3VN = view(WA, 9984, [NCH, 64])
        WPROJ = view(WA, 10496, [NCH, 1024])

        XR2 = view(A3, 0, [NCH, 2, 256])
        XK2 = view(A3, 4096, [NCH, 2, 256])
        KACT = view(A3, 8192, [NCH, 256])
        X2SL = [view(A3, 0, [NCH, 256]), view(A3, 2048, [NCH, 256])]
        SQSL = [view(A3, 4096, [NCH, 256]), view(A3, 6144, [NCH, 256])]
        LNSLB = view(A3, 10240, [NCH, 257])
        DSLB = view(A3, 12296, [NCH, 256])
        LN2BF = view(A3, 0, [NCH, T])
        SVALL = view(A3, 0, [NCH, T])

        LNSL = view(SCR, 0, [NCH, 257])
        T1DB = view(SCR, 2056, [2, 256])
        KTMP2 = view(SCR, 2568, [2, 256])
        RCH2 = view(SCR, 3080, [2, 256])
        TCH2 = view(SCR, 3592, [2, 256])
        BCN = SCR[:, 4104:4104 + 257]
        BCS = SCR[:, 4361:4361 + 257]
        LNSCR = view(SCR, 0, [NCH, 256])
        PL2 = view(SCR, 0, [NCH, 512])
        PL3 = view(XRb, 8200, [NCH, 256])
        PL4 = view(SCR, 0, [NCH, 128])
        PL5 = view(XRb, 8200 + 4096, [NCH, 64])
        PL6 = view(SCR, 1024, [NCH, 32])
        CCS = SCR[:, 0:520]
        CCR = SCR[:, 520:1040]

        srow = ROWS[0:1, 0, :]
        qrow = ROWS[0:1, 1, :]
        mrow = ROWS[0:1, 2, :]
        vrow = ROWS[0:1, 3, :]

        for src, dst in [(identbf_d, identbf), (ints_d, ints), (onescol_d, onescol),
                         (onesbd2_d, onesbd2), (bconst_d, bconst),
                         (b2_d, b2c), (bmask_d, bmask_sb)]:
            op("Y", lambda e, s=src, d=dst: e.dma_start(out=d[:], in_=s), "dY", 16)
        (op("Y", lambda e: e.dma_start(out=mu_sb[:], in_=mu_d.rearrange("i c p -> p i c"),
                                      allow_slow_non_contiguous=True), "dY", 16)
         if "mu" not in SKIP else op("Y", lambda e: e.dma_start(out=mu_sb[:, 0, 0:1], in_=mu_d[0:1, 0, 0:1].transpose([1, 0]) if False else mu_d[0, 0:1, 0:1]), "dY", 16))
        PRO_dY = cnt["dY"]
        XL = {}
        xt_pct = xt.rearrange("(c p) t -> p c t", p=128)
        for i, (a, b) in enumerate(SL):
            XL[i] = op("Y", lambda e, a=a, b=b: e.dma_start(
                out=X[:, :, a:b], in_=xt_pct[:, :, a:b]), "dY", 16)

        def wg_load_thunks(g, buf):
            outs = []
            for t3 in range(3):
                col = t3 * 1024 + g * 128
                outs.append(lambda e, t3=t3, col=col, buf=buf: e.dma_start(
                    out=view(WA, buf * 3072 + t3 * 128, [NCH, 128]) if False else WG4[buf][:, :, t3, :],
                    in_=wqkv[:, col:col + 128].rearrange("(c p) d -> p c d", p=128)))
            return outs

        wt("V", "dY", PRO_dY)
        op("V", lambda e: e.tensor_scalar(out=omu_sb[:], in0=mu_sb[:], scalar1=-1.0,
                                          scalar2=1.0, op0=ALU.mult, op1=ALU.add))

        SCRbf = SCR[:, :].bitcast(BF16)
        X2SLB = [GT2[:, 0:2048].rearrange("p (a b) -> p a b", b=256),
                 GT2[:, 2048:4096].rearrange("p (a b) -> p a b", b=256)]
        SQSLB = [SCRbf[:, 4112:4112 + 2048].rearrange("p (a b) -> p a b", b=256),
                 SCRbf[:, 6160:6160 + 2048].rearrange("p (a b) -> p a b", b=256)]

        def make_rows(src_dram, gate_sem, gate_val, slice_gates=None, slots=None):
            SX, SQ = slots if slots is not None else (X2SL, SQSL)
            """Lazy per-slice LN stats. Returns rows_of(i) -> (sem, tick): emits
            stats slices up to i+2 and per-slice finishes up to i on demand, so
            the T stream interleaves stat matmuls with consuming bcasts."""
            sum_T = {}
            sq_T = {}
            ev_sum_S = {}
            ev_sq_S = {}
            castG = {}
            sqV = {}
            fin_tick = {}

            def emit_stats(i):
                a, b = SL[i]
                w = b - a
                pg = i % 2
                if slice_gates is not None:
                    wt("G", slice_gates[i][0], slice_gates[i][1])
                elif gate_sem is not None and i == 0:
                    wt("G", gate_sem, gate_val)
                if i >= 2:
                    wt("G", "T", sum_T[i - 2])
                    wt("G", "S", sqV[i - 2])
                if src_dram is not None:
                    castG[i] = op("G", lambda e, a=a, w=w, pg=pg: e.dma_start(
                        out=SX[pg][:, :, 0:w],
                        in_=src_dram.rearrange("(c p) t -> p c t", p=128)[:, :, a:a + w]), "dG", 16)
                else:
                    castG[i] = op("G", lambda e, a=a, w=w, pg=pg: e.dma_start(
                        out=SX[pg][:, :, 0:w], in_=X[:, :, a:a + w]), "dG", 16)
                wt("S", "dG", castG[i])
                if i >= 2:
                    wt("S", "T", sq_T[i - 2])
                sqV[i] = op("S", lambda e, w=w, pg=pg: e.activation(
                    out=SQ[pg][:, :, 0:w], in_=SX[pg][:, :, 0:w],
                    func=AF.Square), "S")
                wt("T", "dG", castG[i])
                if i == 0:
                    wt("T", "dY", PRO_dY)
                if i >= 2:
                    wt("T", "S", ev_sum_S[i - 2])
                def mm_sum(e, w=w, pg=pg):
                    ins = None
                    for kc in range(NCH):
                        ins = e.matmul(PSC[0:1, pg * 512:pg * 512 + w], onescol[:],
                                       SX[pg][:, kc, 0:w], start=(kc == 0), stop=(kc == NCH - 1))
                    return ins
                sum_T[i] = op("T", mm_sum, "T")
                wt("T", "S", sqV[i])
                if i >= 2:
                    wt("T", "S", ev_sq_S[i - 2])
                def mm_sq(e, w=w, pg=pg):
                    ins = None
                    for kc in range(NCH):
                        ins = e.matmul(PSB[0:1, pg * 512:pg * 512 + w], onescol[:],
                                       SQ[pg][:, kc, 0:w], start=(kc == 0), stop=(kc == NCH - 1))
                    return ins
                sq_T[i] = op("T", mm_sq, "T")
                wt("S", "T", sum_T[i])
                ev_sum_S[i] = op("S", lambda e, a=a, w=w, pg=pg: e.copy(
                    out=srow[:, a:a + w], in_=PSC[0:1, pg * 512:pg * 512 + w]), "S")
                wt("S", "T", sq_T[i])
                ev_sq_S[i] = op("S", lambda e, a=a, w=w, pg=pg: e.copy(
                    out=qrow[:, a:a + w], in_=PSB[0:1, pg * 512:pg * 512 + w]), "S")

            def emit_finish(k):
                a = SL[k][0]
                b = 2050 if k == 7 else SL[k][1]
                wt("V", "S", ev_sq_S[8 if k == 7 else k])
                op("V", lambda e, a=a, b=b: e.tensor_tensor(
                    out=vrow[:, a:b], in0=srow[:, a:b], in1=srow[:, a:b], op=ALU.mult))
                op("V", lambda e, a=a, b=b: e.scalar_tensor_tensor(
                    out=vrow[:, a:b], in0=vrow[:, a:b], scalar=1.0 / 1024.0,
                    in1=qrow[:, a:b], op0=ALU.mult, op1=ALU.subtract))
                epsV = op("V", lambda e, a=a, b=b: e.tensor_scalar(
                    out=vrow[:, a:b], in0=vrow[:, a:b], scalar1=-1.0 / 1024.0,
                    scalar2=EPS, op0=ALU.mult, op1=ALU.add), "V")
                wt("S", "V", epsV)
                sdS = op("S", lambda e, a=a, b=b: e.activation(
                    out=vrow[:, a:b], in_=vrow[:, a:b], func=AF.Sqrt), "S")
                wt("V", "S", sdS)
                fin_tick[k] = op("V", lambda e, a=a, b=b: e.reciprocal(
                    out=vrow[:, a:b], in_=vrow[:, a:b]), "V")

            def rows_of(i):
                k = min(i, 7)
                for j in range(len(castG), min(i + 1 + rows_of.prefetch, 9)):
                    emit_stats(j)
                for j in range(len(fin_tick), k + 1):
                    if j == 7 and 8 not in castG:
                        emit_stats(8)
                    emit_finish(j)
                return ("V", fin_tick[k])

            rows_of.prefetch = 2
            rows_of.last_S = lambda: ev_sq_S[8]
            return rows_of

        def emit_bcast_slice(i, a, b, rows_of, prev_user_V, prev_user_G=None):
            w = b - a
            tick = rows_of(i)
            wt("T", tick[0], tick[1])
            if prev_user_V is not None:
                wt("T", "V", prev_user_V)
            if prev_user_G is not None:
                wt("T", "G", prev_user_G)
            def mm_bc(e, a=a, w=w):
                e.matmul(PSD[:, 0:w], bconst[0:1, 0, :], srow[:, a:a + w], start=True, stop=True)
                return e.matmul(PSD[:, 256:256 + w], bconst[0:1, 1, :], vrow[:, a:a + w], start=True, stop=True)
            t2 = op("T", mm_bc, "T")
            wt("V", "T", t2)
            op("V", lambda e, w=w: e.tensor_copy(out=BCN[:, 1:1 + w], in_=PSD[:, 0:w]))
            vv = op("V", lambda e, w=w: e.tensor_copy(out=BCS[:, 1:1 + w], in_=PSD[:, 256:256 + w]), "V")
            if w < 64:
                op("V", lambda e: e.drain())
            return vv

        def emit_ffn(rows_V, w_dG, snap_cb=None, stats_S=None, xload_of=None):
            snap_cb2 = [snap_cb]
            bc_user = None
            reluS = {}
            kactV = {}
            sigS = {}
            rvV = {}
            addG = {}
            mixV = {}
            t1hist = []
            mix_hist = []
            vT = {}
            NRV = 0
            def stageA(i):
                nonlocal bc_user
                a, b = SL[i]
                w = b - a
                bc_user = emit_bcast_slice(i, a, b, rows_V, bc_user)
                if i >= 2:
                    wt("V", "T", vT[(i - 2, NCH - 1)])
                if i == 0:
                    op("V", lambda e: e.memset(LNSLB[:, :, 0:1], 0.0))
                else:
                    op("V", lambda e: e.tensor_copy(out=LNSLB[:, :, 0:1], in_=LNSLB[:, :, 256:257]))
                if w < 64:
                    op("V", lambda e: e.drain())
                if xload_of is not None:
                    wt("V", "dY", xload_of(i))
                op("V", lambda e, a=a, w=w: e.tensor_tensor(
                    out=LNSL[:, :, 1:1 + w], in0=X[:, :, a:a + w],
                    in1=BCN[:, 1:1 + w].unsqueeze(1).broadcast_to([128, NCH, w]),
                    op=ALU.add))
                if w < 64:
                    op("V", lambda e: e.drain())
                lnV = op("V", lambda e, w=w: e.tensor_tensor(
                    out=LNSLB[:, :, 1:1 + w], in0=LNSL[:, :, 1:1 + w],
                    in1=BCS[:, 1:1 + w].unsqueeze(1).broadcast_to([128, NCH, w]),
                    op=ALU.mult), "V")
                if w < 64:
                    op("V", lambda e: e.drain())
                if i == len(SL) - 1 and 15 not in snaps:
                    snaps[15] = ({k: len(v) for k, v in streams.items()}, dict(cnt),
                                 ("V", lnV), "LNSL131")
                pg = i % 2
                op("V", lambda e, w=w: e.tensor_tensor(
                    out=DSLB[:, :, 0:w], in0=LNSLB[:, :, 1:1 + w],
                    in1=LNSLB[:, :, 0:w], op=ALU.subtract))
                for mi, dst2 in [(0, XR2), (1, XK2)]:
                    for c in range(NCH):
                        op("V", lambda e, c=c, w=w, mi=mi, dst2=dst2, pg=pg:
                           e.scalar_tensor_tensor(
                               out=dst2[:, c, pg, 0:w], in0=DSLB[:, c, 0:w],
                               scalar=mu_sb[:, mi, c:c + 1], in1=LNSLB[:, c, 0:w],
                               op0=ALU.mult, op1=ALU.add), "V")
                mixV[i] = cnt["V"]

            def stageB(i):
                a, b = SL[i]
                w = b - a
                pg = i % 2
                wt("T", "V", mixV[i])
                if i == 0:
                    wt("T", "dG", w_dG[0])
                for dp in range(NCH):
                    half = (dp % 2) * 512
                    prev_relu = reluS.get((i, dp - 2)) if dp >= 2 else reluS.get((i - 1, dp + 6))
                    if prev_relu is not None:
                        wt("T", "S", prev_relu)
                    def mm_k(e, w=w, pg=pg, dp=dp, half=half):
                        ins = None
                        for c in range(NCH):
                            ins = e.matmul(PSA[:, half:half + w], WKT[:, c, dp * 128:dp * 128 + 128],
                                           XK2[:, c, pg, 0:w], start=(c == 0), stop=(c == NCH - 1))
                        return ins
                    kTt = op("T", mm_k, "T")
                    wt("S", "T", kTt)
                    op("S", lambda e, w=w, dp=dp, half=half: e.activation(
                        out=KTMP2[:, dp % 2, 0:w], in_=PSA[:, half:half + w], func=AF.Relu))
                    reluS[(i, dp)] = op("S", lambda e, w=w, dp=dp: e.activation(
                        out=KACT[:, dp, 0:w], in_=KTMP2[:, dp % 2, 0:w], func=AF.Square), "S")
                    kactV[(i, dp)] = reluS[(i, dp)]

                if xload_of is not None:
                    wt("G", "dY", xload_of(i))
                if i == 0:
                    wt("T", "dG", w_dG[2])
                if i == 0 and stats_S is not None:
                    wt("T", "S", stats_S())
                for dp in range(NCH):
                    half = (dp % 2) * 512
                    n = i * NCH + dp
                    prev_sig = sigS.get(n - 2)
                    if prev_sig is not None:
                        wt("T", "S", prev_sig)
                    def mm_r(e, w=w, pg=pg, dp=dp, half=half):
                        ins = None
                        for c in range(NCH):
                            ins = e.matmul(PSB[:, half:half + w], WRT[:, c, dp * 128:dp * 128 + 128],
                                           XR2[:, c, pg, 0:w], start=(c == 0), stop=(c == NCH - 1))
                        return ins
                    rTt = op("T", mm_r, "T")
                    if dp == 0:
                        wt("T", "S", kactV[(i, NCH - 1)])
                    prev_rv = rvV.get(n - 2)
                    if prev_rv is not None:
                        wt("T", "V", prev_rv)
                    def mm_v(e, w=w, dp=dp, half=half):
                        ins = None
                        for c in range(NCH):
                            ins = e.matmul(PSC[:, half:half + w], WVT[:, c, dp * 128:dp * 128 + 128],
                                           KACT[:, c, 0:w], start=(c == 0), stop=(c == NCH - 1))
                        return ins
                    vT[(i, dp)] = op("T", mm_v, "T")
                    wt("S", "T", rTt)
                    prev_rv1 = rvV.get(n - 1)
                    if prev_rv1 is not None:
                        wt("S", "V", prev_rv1)
                    sigS[n] = op("S", lambda e, w=w, dp=dp, half=half: e.activation(
                        out=RCH2[:, dp % 2, 0:w], in_=PSB[:, half:half + w], func=AF.Sigmoid), "S")
                    wt("V", "S", sigS[n])
                    wt("V", "T", vT[(i, dp)])
                    prev_add = addG.get(n - 2)
                    if prev_add is not None:
                        wt("V", "G", prev_add)
                    rvV[n] = op("V", lambda e, w=w, dp=dp, half=half: e.scalar_tensor_tensor(
                        out=TCH2[:, dp % 2, 0:w], in0=PSC[:, half:half + w], scalar=0.5,
                        in1=RCH2[:, dp % 2, 0:w], op0=ALU.mult, op1=ALU.mult), "V")
                    wt("G", "V", rvV[n])
                    addG[n] = op("G", lambda e, a=a, b=b, w=w, dp=dp: e.tensor_tensor(
                        out=X[:, dp, a:b], in0=X[:, dp, a:b], in1=TCH2[:, dp % 2, 0:w],
                        op=ALU.add), "G")
                if i == 0 and snap_cb is not None:
                    snap_cb2[0] = None

            for i in range(len(SL) + 1):
                if i < len(SL):
                    stageA(i)
                if i >= 1:
                    stageB(i - 1)
                    if i - 1 == 0 and snap_cb is not None:
                        snap_cb(cnt["G"])
            return cnt["G"], bc_user, addG

        snaps = {}
        def snap(n, gate, src_ap):
            snaps[n] = ({k: len(v) for k, v in streams.items()}, dict(cnt), gate, src_ap)

        rows1_of = make_rows(xt, None, None)
        rows1_of.prefetch = 9
        rows1_t0 = rows1_of(8)
        if "wcast" not in SKIP:
            op("G", lambda e: e.dma_start(out=WKT, in_=wkt_d.rearrange("(c p) d -> p c d", p=128)), "dG", 16)
        FFNWK_dG = cnt["dG"]
        if "wcast" not in SKIP:
            op("G", lambda e: e.dma_start(out=WRT, in_=wrt_d.rearrange("(c p) d -> p c d", p=128)), "dG", 16)
        FFNWR_dG = cnt["dG"]
        if "wcast" not in SKIP:
            op("G", lambda e: e.dma_start(out=WVT, in_=wvt_d.rearrange("(c p) d -> p c d", p=128)), "dG", 16)
        FFNW_dG = cnt["dG"]
        snap(11, ("dG", FFNW_dG), X[:, :, 2:T])
        if stage in (12, 121):
            rows1_full = rows1_of(8)
        else:
            rows1_full = rows1_t0
        snap(12, rows1_full, X[:, :, 2:T])
        snaps[121] = (snaps[12][0], snaps[12][1], rows1_full, "ROWS121")
        def _snap13(g_tick):
            snap(13, ("G", g_tick), X[:, :, 2:T])
            snaps[131] = (snaps[13][0], snaps[13][1], ("G", g_tick), "LNSL131")
            snaps[132] = (snaps[13][0], snaps[13][1], ("G", g_tick), "XRK132")
        ffn1_G, _, ffn1_addG = emit_ffn(rows1_of, (FFNWK_dG, FFNWR_dG, FFNW_dG),
                                        snap_cb=_snap13, stats_S=rows1_of.last_S,
                                        xload_of=lambda i: XL[i])
        ffn1_T = cnt["T"]
        snaps[14] = ({k: len(v) for k, v in streams.items()}, dict(cnt), ("G", ffn1_G), "LNSL131")
        snaps[141] = ({k: len(v) for k, v in streams.items()}, dict(cnt), ("G", ffn1_G), "XRK132")

        snap(1, ("G", ffn1_G), X[:, :, 2:T])
        rows2_of = make_rows(None, "G", ffn1_G,
                             slice_gates=[("G", ffn1_addG[i * NCH + NCH - 1]) for i in range(len(SL))],
                             slots=(X2SLB, SQSLB))

        wt("Y", "G", ffn1_G)
        XSPILL_dY = op("Y", lambda e: e.dma_start(out=xsp, in_=X[:]), "dY", 16)

        bc_user = None
        ln2V = {}
        for i, (a, b) in enumerate(SL):
            w = b - a
            bc_user = emit_bcast_slice(i, a, b, rows2_of, bc_user)
            op("V", lambda e, a=a, w=w: e.tensor_tensor(
                out=LNSCR[:, :, 0:w], in0=X[:, :, a:a + w],
                in1=BCN[:, 1:1 + w].unsqueeze(1).broadcast_to([128, NCH, w]), op=ALU.add))
            if w < 64:
                op("V", lambda e: e.drain())
            bc_user = op("V", lambda e, a=a, w=w: e.tensor_tensor(
                out=LN2BF[:, :, a:a + w], in0=LNSCR[:, :, 0:w],
                in1=BCS[:, 1:1 + w].unsqueeze(1).broadcast_to([128, NCH, w]), op=ALU.mult), "V")
            if w < 64:
                op("V", lambda e: e.drain())
            ln2V[i] = bc_user
        LN2_V = cnt["V"]

        snaps[21] = ({k: len(v) for k, v in streams.items()}, dict(cnt),
                     ("V", LN2_V), "LN2BF21")
        snaps[22] = ({k: len(v) for k, v in streams.items()}, dict(cnt),
                     ("V", LN2_V), "ROWS121")
        wt("V", "dY", XSPILL_dY)
        op("V", lambda e: e.tensor_tensor(out=PL1[:], in0=LN2BF[:, :, 2:2049:2],
                                          in1=LN2BF[:, :, 3:2050:2], op=ALU.add))
        op("V", lambda e: e.tensor_tensor(out=PL2[:], in0=PL1[:, :, 0:1024:2],
                                          in1=PL1[:, :, 1:1024:2], op=ALU.add))
        op("V", lambda e: e.tensor_tensor(out=PL3[:], in0=PL2[:, :, 0:512:2],
                                          in1=PL2[:, :, 1:512:2], op=ALU.add))
        op("V", lambda e: e.tensor_tensor(out=PL4[:], in0=PL3[:, :, 0:256:2],
                                          in1=PL3[:, :, 1:256:2], op=ALU.add))
        op("V", lambda e: e.tensor_tensor(out=PL5[:], in0=PL4[:, :, 0:128:2],
                                          in1=PL4[:, :, 1:128:2], op=ALU.add))
        op("V", lambda e: e.tensor_tensor(out=PL6[:], in0=PL5[:, :, 0:64:2],
                                          in1=PL5[:, :, 1:64:2], op=ALU.add))
        POOL_V = op("V", lambda e: e.tensor_scalar(out=POOLED[:], in0=PL6[:], scalar1=1.0 / 64.0,
                                                   scalar2=None, op0=ALU.mult), "V")

        wt("G", "T", ffn1_T)
        WGL = {}
        for th in wg_load_thunks(0, 0):
            WGL[0] = op("G", th, "dG", 16)
        for th in wg_load_thunks(1, 1):
            WGL[1] = op("G", th, "dG", 16)

        wt("G", "V", POOL_V)
        op("G", lambda e: e.dma_start(out=cc3in.ap().rearrange("(c p) s -> p c s", p=128),
                                      in_=POOLED[:]), "dG", 16)
        wt("G", "dG", cnt["dG"])
        CC3 = op("G", lambda e: e.collective_compute(
            "AllGather", ALU.bypass, replica_groups=RG,
            ins=[cc3in.ap().opt()], outs=[cc3out.ap().opt()]), "CC", 1)
        wt("G", "CC", CC3)
        op("G", lambda e: e.dma_start(
            out=POOLEDG[:, :, 0:32],
            in_=cc3out.ap()[0].rearrange("(c p) s -> p c s", p=128)), "dG", 16)
        PGL_dG = op("G", lambda e: e.dma_start(
            out=POOLEDG[:, :, 32:64],
            in_=cc3out.ap()[1].rearrange("(c p) s -> p c s", p=128)), "dG", 16)

        snap(2, ("dG", PGL_dG), X[:, :, 2:T])
        wt("V", "T", ffn1_T)
        op("V", lambda e: e.memset(BDQ[:], 0.0))
        BDZERO_V = op("V", lambda e: e.memset(BDKLM[:], 0.0), "V")

        lastWG_T = {}
        qlmV = {}
        klmT = {}
        l2T = {}
        k3eS = {}
        k3tV = {}
        vtokS = {}
        k3vT = {}
        kevV = {}
        vevS = {}
        KGP = [KG, KG2]
        VGP = [VG, VG2]
        K3EP = [K3Eb, K3Eb2]
        K3TP = [K3T, K3T2]
        VTOKP = [VTOK, VTOK2]
        evhist = []

        def stage_Q(g):
            wg = WG[g % 2]
            pq = g % 2
            wt("T", "dG", WGL[g])
            for t3 in range(3):
                for si, (a, b) in enumerate(SL5):
                    w = b - a
                    half = ((t3 * 5 + si) % 2) * 512
                    if len(evhist) >= 2:
                        wt("T", evhist[-2][0], evhist[-2][1])
                    if g == 0:
                        wt("T", "V", ln2V[min(len(SL) - 1, (a + w) // 256)])
                    def mm_qkv(e, wg=wg, t3=t3, a=a, w=w, half=half):
                        ins = None
                        for kc in range(NCH):
                            ins = e.matmul(PSA[:, half:half + w], wg[:, kc, t3 * 128:t3 * 128 + 128],
                                           LN2BF[:, kc, a:a + w], start=(kc == 0), stop=(kc == NCH - 1))
                        return ins
                    tt = op("T", mm_qkv, "T")
                    if t3 == 0:
                        if g == 0 and si == 0:
                            wt("S", "dY", XSPILL_dY)
                        wt("S", "T", tt)
                        sv = op("S", lambda e, g=g, a=a, w=w, half=half: e.copy(
                            out=QALL[:, g, a:a + w], in_=PSA[:, half:half + w]), "S")
                        evhist.append(("S", sv))
                    elif t3 == 1:
                        wt("V", "T", tt)
                        if si == 0 and g >= 2:
                            wt("V", "T", k3vT[g - 2])
                        sv = op("V", lambda e, a=a, w=w, half=half, pq=pq: e.tensor_copy(
                            out=KGP[pq][:, a:a + w], in_=PSA[:, half:half + w]), "V")
                        evhist.append(("V", sv))
                    else:
                        wt("S", "T", tt)
                        if si == 0 and g >= 2:
                            wt("S", "T", k3vT[g - 2])
                        sv = op("S", lambda e, a=a, w=w, half=half, pq=pq: e.copy(
                            out=VGP[pq][:, a:a + w], in_=PSA[:, half:half + w]), "S")
                        evhist.append(("S", sv))
            kevV[g] = evhist[-6][1]
            vevS[g] = evhist[-1][1]
            lastWG_T[g] = cnt["T"]

        def stage_R(g):
            wg = WG[g % 2]
            pq = g % 2
            wt("T", "dG", PGL_dG)
            if g >= 1:
                wt("T", "V", qlmV[g - 1])
                wt("T", "S", k3eS[(g - 1, 3)])
            def mm_qlm(e, wg=wg):
                ins = None
                for kc in range(NCH):
                    ins = e.matmul(PSB[:, 512:576], wg[:, kc, 0:128], POOLEDG[:, kc, :],
                                   start=(kc == 0), stop=(kc == NCH - 1))
                return ins
            t_qlm = op("T", mm_qlm, "T")
            def mm_klm(e, wg=wg):
                ins = None
                for kc in range(NCH):
                    ins = e.matmul(PSB[:, 576:640], wg[:, kc, 128:256], POOLEDG[:, kc, :],
                                   start=(kc == 0), stop=(kc == NCH - 1))
                return ins
            klmT[g] = op("T", mm_klm, "T")
            if g + 2 < 8:
                wt("G", "T", klmT[g])
                for th in wg_load_thunks(g + 2, g % 2):
                    WGL[g + 2] = op("G", th, "dG", 16)
            wt("V", "T", klmT[g])
            wt("V", "V", BDZERO_V)
            op("V", lambda e, g=g: e.tensor_scalar(out=BDQ[0:64, g, 0:64], in0=PSB[0:64, 512:576],
                                                   scalar1=0.125, scalar2=None, op0=ALU.mult))
            op("V", lambda e, g=g: e.tensor_scalar(out=BDQ[64:128, g, 64:128], in0=PSB[64:128, 512:576],
                                                   scalar1=0.125, scalar2=None, op0=ALU.mult))
            qlmV[g] = op("V", lambda e, g=g: e.tensor_copy(out=KLM[:, g, :], in_=PSB[:, 576:640]), "V")
            wt("T", "V", qlmV[g])
            l2T[g] = op("T", lambda e, g=g: e.matmul(PSD[:, g * 64:g * 64 + 64], BDQ[:, g, :],
                                                     KLM[:, g, :], start=True, stop=True), "T")

            for si, (a, b) in enumerate(SLO):
                wt("T", "V", kevV[g])
                prev = k3eS.get((g, si - 2)) if si >= 2 else k3eS.get((g - 1, si + 2))
                if prev is not None:
                    wt("T", "S", prev)
                t3t = op("T", lambda e, g=g, a=a, si=si, pq=pq: e.matmul(
                    PSB[:, (si % 2) * 512:(si % 2) * 512 + 512], BDQ[:, g, :],
                    KGP[pq][:, a:a + 512], start=True, stop=True), "T")
                wt("S", "T", t3t)
                k3eS[(g, si)] = op("S", lambda e, g=g, si=si, pq=pq: e.activation(
                    out=K3EP[pq][:, si * 512:si * 512 + 512],
                    in_=PSB[:, (si % 2) * 512:(si % 2) * 512 + 512],
                    func=AF.Exp, accum_out=sums3p[:, g, si:si + 1]), "S")

            for half in range(2):
                base = half * 8
                wt("T", "S", k3eS[(g, 2 * half + 1)])
                prev_ev = k3tV.get((g, half - 1)) if half >= 1 else k3tV.get((g - 1, 1))
                if prev_ev is not None:
                    wt("T", "V", prev_ev)
                def mm_tr3(e, base=base, pq=pq):
                    ins = None
                    for j in range(8):
                        oc = base + j
                        ins = e.transpose(PST[:, j * 128:j * 128 + 128],
                                          K3EP[pq][:, oc * 128:oc * 128 + 128], identbf[:])
                    return ins
                tr = op("T", mm_tr3, "T")
                wt("V", "T", tr)
                if g >= 2 and half == 0:
                    wt("V", "T", k3vT[g - 2])
                k3tV[(g, half)] = op("V", lambda e, base=base, pq=pq: e.tensor_copy(
                    out=K3TP[pq][:, base:base + 8, :],
                    in_=PST[:].rearrange("p (a b) -> p a b", b=128)), "V")
            for half in range(2):
                base = half * 8
                wt("T", "S", vevS[g])
                if half == 0:
                    wt("T", "V", k3tV[(g, 1)])
                    prev_ev = vtokS.get((g - 1, 1))
                else:
                    prev_ev = vtokS.get((g, 0))
                if prev_ev is not None:
                    wt("T", "S", prev_ev)
                def mm_trv(e, base=base, pq=pq):
                    ins = None
                    for j in range(8):
                        a0 = CHO16[base + j][0]
                        ins = e.transpose(PST[:, j * 128:j * 128 + 128],
                                          VGP[pq][:, a0:a0 + 128], identbf[:])
                    return ins
                trv = op("T", mm_trv, "T")
                wt("S", "T", trv)
                if g >= 2 and half == 0:
                    wt("S", "T", k3vT[g - 2])
                vtokS[(g, half)] = op("S", lambda e, base=base, pq=pq: e.copy(
                    out=VTOKP[pq][:, base:base + 8, :],
                    in_=PST[:].rearrange("p (a b) -> p a b", b=128)), "S")

            wt("T", "V", k3tV[(g, 1)])
            wt("T", "S", vtokS[(g, 1)])
            def mm_k3v(e, g=g, pq=pq):
                ins = None
                for oc in range(16):
                    ins = e.matmul(PSC[:, g * 128:g * 128 + 128], K3TP[pq][:, oc, :],
                                   VTOKP[pq][:, oc, :], start=(oc == 0), stop=(oc == 15))
                return ins
            k3vT[g] = op("T", mm_k3v, "T")

        stage_Q(0)
        stage_Q(1)
        for g in range(8):
            stage_R(g)
            if g + 2 < 8:
                stage_Q(g + 2)

        QALL_S = cnt["S"]

        wt("V", "T", k3vT[7])
        op("V", lambda e: e.tensor_copy(
            out=CCS[0:64, 0:512].rearrange("p (g x) -> p g x", x=64),
            in_=PSC[0:64, :].rearrange("p (g x) -> p g x", x=128)[:, :, 0:64]))
        op("V", lambda e: e.tensor_copy(
            out=CCS[64:128, 0:512].rearrange("p (g x) -> p g x", x=64),
            in_=PSC[64:128, :].rearrange("p (g x) -> p g x", x=128)[:, :, 64:128]))
        wt("V", "S", k3eS[(7, 3)])
        CCS_V = op("V", lambda e: e.tensor_reduce(
            out=CCS[:, 512:520], in_=sums3p[:], axis=AX.X, op=ALU.add), "V")
        wt("G", "V", CCS_V)
        op("G", lambda e: e.dma_start(out=cc4in.ap(), in_=CCS[:, 0:520]), "dG", 16)
        wt("G", "dG", cnt["dG"])
        CC4 = op("G", lambda e: e.collective_compute(
            "AllReduce", ALU.add, replica_groups=RG,
            ins=[cc4in.ap().opt()], outs=[cc4out.ap().opt()]), "CC", 1)
        wt("G", "CC", CC4)
        CCR_dG = op("G", lambda e: e.dma_start(out=CCR[:, 0:520], in_=cc4out.ap()), "dG", 16)

        snap(3, ("dG", CCR_dG), CCR)
        wt("G", "T", k3vT[7])
        WPROJ_dG = op("G", lambda e: e.dma_start(
            out=WPROJ[:], in_=wproj_d.rearrange("(c p) d -> p c d", p=128)), "dG", 16)

        wt("S", "T", l2T[7])
        wt("S", "T", k3vT[7])
        K2e_S = op("S", lambda e: e.activation(out=K2E[:], in_=PSD[:].rearrange("p (g x) -> p g x", x=64),
                                               func=AF.Exp), "S")
        wt("V", "S", K2e_S)
        op("V", lambda e: e.tensor_reduce(out=rc8[:], in_=K2E[:], axis=AX.X, op=ALU.add))
        op("V", lambda e: e.drain())
        op("V", lambda e: e.reciprocal(out=rc8[:], in_=rc8[:]))
        op("V", lambda e: e.drain())
        K2n_V = op("V", lambda e: e.tensor_tensor(out=K2E[:], in0=K2E[:],
                                                  in1=rc8[:].unsqueeze(2).broadcast_to([128, 8, 64]),
                                                  op=ALU.mult), "V")
        wt("T", "V", K2n_V)
        def mm_cs(e):
            ins = None
            for g in range(8):
                ins = e.matmul(PSB[0:2, g * 64:g * 64 + 64], onesbd2[:], K2E[:, g, :],
                               start=True, stop=True)
            return ins
        cs_T = op("T", mm_cs, "T")
        wt("V", "T", cs_T)
        op("V", lambda e: e.tensor_copy(out=cs2[:], in_=PSB[0:2, 0:512].rearrange("p (g x) -> p g x", x=64)))
        op("V", lambda e: e.tensor_reduce(out=mx2[:], in_=cs2[:], axis=AX.X, op=ALU.max))
        op("V", lambda e: e.drain())
        mx_V = op("V", lambda e: e.reciprocal(out=mx2[:], in_=mx2[:]), "V")
        op("V", lambda e: e.drain())
        wt("T", "V", mx_V)
        dn_T = op("T", lambda e: e.matmul(PSB[:, 512:520], b2c[:], mx2[:, :], start=True, stop=True), "T")
        wt("V", "T", dn_T)
        dn_V = op("V", lambda e: e.tensor_copy(out=denr[:], in_=PSB[:, 512:520]), "V")
        op("V", lambda e: e.drain())

        op("V", lambda e: e.memset(BDP[:], 0.0))
        op("V", lambda e: e.tensor_copy(out=BDP[0:64, :, 0:64], in_=K2E[0:64, :, :]))
        bdk_V = op("V", lambda e: e.tensor_copy(out=BDP[64:128, :, 64:128], in_=K2E[64:128, :, :]), "V")
        wt("T", "V", bdk_V)
        def mm_trk(e):
            ins = None
            for g in range(8):
                ins = e.transpose(PST[:, g * 128:g * 128 + 128], BDP[:, g, :], identbf[:])
            return ins
        trk_T = op("T", mm_trk, "T")
        wt("V", "T", trk_T)
        op("V", lambda e: e.tensor_copy(out=KTBD[:], in_=PST[:].rearrange("p (a b) -> p a b", b=128)))
        wt("V", "V", dn_V)
        op("V", lambda e: e.tensor_tensor(out=VS_[0:64, :, :], in0=KTBD[0:64, :, 0:64],
                                          in1=denr[0:64, :].unsqueeze(2).broadcast_to([64, 8, 64]), op=ALU.mult))
        op("V", lambda e: e.tensor_tensor(out=VS_[64:128, :, :], in0=KTBD[64:128, :, 64:128],
                                          in1=denr[64:128, :].unsqueeze(2).broadcast_to([64, 8, 64]), op=ALU.mult))
        itV = op("V", lambda e: e.tensor_tensor(out=VTBD[:], in0=BDP[:],
                                                in1=denr[:].unsqueeze(2).broadcast_to([128, 8, 128]), op=ALU.mult), "V")

        pst_user = [None]

        def newton_stages(it, state):
            loc = {}

            def st1():
                wt("T", "V", state["itV"])
                if it == 0:
                    wt("T", "V", CCS_V)
                def mm_P(e):
                    ins = None
                    for g in range(8):
                        ins = e.matmul(PSC[:, g * 64:g * 64 + 64], KTBD[:, g, :], VS_[:, g, :],
                                       start=True, stop=True)
                    return ins
                P_T = op("T", mm_P, "T")
                wt("V", "T", P_T)
                loc["ns1_V"] = op("V", lambda e: e.scalar_tensor_tensor(
                    out=NS1[:], in0=PSC[:, 0:512].rearrange("p (g x) -> p g x", x=64),
                    scalar=-1.0, in1=ints[:, 0, :].unsqueeze(1).broadcast_to([128, 8, 64]),
                    op0=ALU.mult, op1=ALU.add), "V")
                wt("S", "T", P_T)
                op("S", lambda e: e.copy(out=BDP[0:64, :, 0:64],
                                         in_=PSC[0:64, 0:512].rearrange("p (g x) -> p g x", x=64)))
                loc["bdp_V"] = op("S", lambda e: e.copy(
                    out=BDP[64:128, :, 64:128],
                    in_=PSC[64:128, 0:512].rearrange("p (g x) -> p g x", x=64)), "S")

            def st2():
                wt("T", "S", loc["bdp_V"])
                if pst_user[0] is not None:
                    wt("T", pst_user[0][0], pst_user[0][1])
                def mm_trp(e):
                    ins = None
                    for g in range(8):
                        ins = e.transpose(PST[:, g * 128:g * 128 + 128], BDP[:, g, :], identbf[:])
                    return ins
                trp_T = op("T", mm_trp, "T")
                wt("S", "T", trp_T)
                ptbd_V = op("S", lambda e: e.copy(out=PTBD[:], in_=PST[:].rearrange("p (a b) -> p a b", b=128)), "S")
                pst_user[0] = ("S", ptbd_V)
                loc["ptbd_V"] = ptbd_V

            def st3():
                wt("T", "S", loc["ptbd_V"])
                wt("T", "V", loc["ns1_V"])
                def mm_T2(e):
                    ins = None
                    for g in range(8):
                        ins = e.matmul(PSC[:, 512 + g * 64:512 + g * 64 + 64], PTBD[:, g, :], NS1[:, g, :],
                                       start=True, stop=True)
                    return ins
                T2_T = op("T", mm_T2, "T")
                wt("V", "T", T2_T)
                loc["s2_V"] = op("V", lambda e: e.scalar_tensor_tensor(
                    out=NS2[:], in0=PSC[:, 512:1024].rearrange("p (g x) -> p g x", x=64),
                    scalar=-1.0, in1=ints[:, 1, :].unsqueeze(1).broadcast_to([128, 8, 64]),
                    op0=ALU.mult, op1=ALU.add), "V")

            def st4():
                wt("T", "V", loc["s2_V"])
                wt("T", "V", loc["ns1_V"])
                wt("T", "S", loc["bdp_V"])
                def mm_T3(e):
                    ins = None
                    for g in range(8):
                        ins = e.matmul(PSC[:, g * 64:g * 64 + 64], PTBD[:, g, :], NS2[:, g, :],
                                       start=True, stop=True)
                    return ins
                T3_T = op("T", mm_T3, "T")
                wt("V", "T", T3_T)
                loc["s3_V"] = op("V", lambda e: e.scalar_tensor_tensor(
                    out=NS3[:], in0=PSC[:, 0:512].rearrange("p (g x) -> p g x", x=64),
                    scalar=-1.0, in1=ints[:, 2, :].unsqueeze(1).broadcast_to([128, 8, 64]),
                    op0=ALU.mult, op1=ALU.add), "V")

            def st5():
                wt("T", "V", loc["s3_V"])
                def mm_Vn(e):
                    ins = None
                    for g in range(8):
                        ins = e.matmul(PSC[:, 512 + g * 64:512 + g * 64 + 64], VTBD[:, g, :], NS3[:, g, :],
                                       start=True, stop=True)
                    return ins
                Vn_T = op("T", mm_Vn, "T")
                wt("V", "T", Vn_T)
                op("V", lambda e: e.tensor_scalar(out=VS_[:], in0=PSC[:, 512:1024].rearrange("p (g x) -> p g x", x=64),
                                                  scalar1=0.25, scalar2=None, op0=ALU.mult))
                op("V", lambda e: e.tensor_scalar(out=BDP[0:64, :, 0:64],
                                                  in0=PSC[0:64, 512:1024].rearrange("p (g x) -> p g x", x=64),
                                                  scalar1=0.25, scalar2=None, op0=ALU.mult))
                bdv_V = op("V", lambda e: e.tensor_scalar(out=BDP[64:128, :, 64:128],
                                                          in0=PSC[64:128, 512:1024].rearrange("p (g x) -> p g x", x=64),
                                                          scalar1=0.25, scalar2=None, op0=ALU.mult), "V")
                state["itV"] = bdv_V

            return [st1, st2, st3, st4, st5]
            if pst_user[0] is not None:
                wt("T", pst_user[0][0], pst_user[0][1])
            def mm_trv2(e):
                ins = None
                for g in range(8):
                    ins = e.transpose(PST[:, g * 128:g * 128 + 128], BDP[:, g, :], identbf[:])
                return ins
            trv2_T = op("T", mm_trv2, "T")
            wt("S", "T", trv2_T)
            itS = op("S", lambda e: e.copy(out=VTBD[:], in_=PST[:].rearrange("p (a b) -> p a b", b=128)), "S")
            pst_user[0] = ("S", itS)
            wt("V", "S", itS)
            state["itV"] = op("V", lambda e: e.engine_nop() if False else e.nop(), "V")

        wt("V", "V", qlmV[7])
        op("V", lambda e: e.tensor_copy(out=BDKLM[0:64, :, 0:64], in_=KLM[0:64, :, :]))
        bdklm_V = op("V", lambda e: e.tensor_copy(out=BDKLM[64:128, :, 64:128], in_=KLM[64:128, :, :]), "V")
        e1S = {}
        e1V = {}
        trS = {}

        tr1T = {}
        PSDbf = PSD[:, :].bitcast(BF16)
        psd_user = [("S", K2e_S)]

        def l1_mm(ci):
            c0, c1 = CH17[ci]
            wc = c1 - c0
            pg = ci % 2
            s3 = ci % 3
            PSL = PSA if pg == 0 else PSB
            wt("T", "V", bdklm_V)
            if ci == 0:
                wt("T", "S", QALL_S)
                wt("T", "V", kevV[7])
            if ci == 1:
                wt("T", "S", k3eS[(7, 3)])
                wt("T", "V", dn_V)
            prev = e1S.get(ci - 2)
            if prev is not None:
                wt("T", "S", prev)
            def mm_l1(e, c0=c0, wc=wc, PSL=PSL):
                ins = None
                for g in range(8):
                    ins = e.matmul(PSL[0:wc, g * 128:g * 128 + 128], QALL[:, g, c0:c0 + wc],
                                   BDKLM[:, g, :], start=True, stop=True)
                return ins
            e1T = op("T", mm_l1, "T")
            wt("S", "T", e1T)
            prevv = e1V.get(ci - 3)
            if prevv is not None:
                wt("S", "V", prevv)
            if ci - 3 in tr1T:
                wt("S", "T", tr1T[ci - 3])
            e1S[ci] = op("S", lambda e, wc=wc, s3=s3, PSL=PSL: e.activation(
                out=E1B[s3][0:wc, :], in_=PSL[0:wc, :], func=AF.Exp, scale=0.125), "S")
            wt("V", "S", e1S[ci])
            op("V", lambda e, wc=wc, s3=s3: e.tensor_reduce(
                out=rc16[0:wc, s3 * 16:s3 * 16 + 16],
                in_=E1B[s3][0:wc, :].rearrange("p (h x) -> p h x", x=64),
                axis=AX.X, op=ALU.add))

        def l1_recip(ci, guard=False):
            wc = CH17[ci][1] - CH17[ci][0]
            s3 = ci % 3
            if guard:
                op("V", lambda e: e.drain())
            op("V", lambda e, wc=wc, s3=s3: e.reciprocal(
                out=rc16[0:wc, s3 * 16:s3 * 16 + 16], in_=rc16[0:wc, s3 * 16:s3 * 16 + 16]))

        def l1_fin(ci, guard=False):
            c0, c1 = CH17[ci]
            wc = c1 - c0
            s3 = ci % 3
            if guard:
                op("V", lambda e: e.drain())
            e1V[ci] = op("V", lambda e, wc=wc, s3=s3: e.tensor_tensor(
                out=E1B[s3][0:wc, :].rearrange("p (h x) -> p h x", x=64),
                in0=E1B[s3][0:wc, :].rearrange("p (h x) -> p h x", x=64),
                in1=rc16[0:wc, s3 * 16:s3 * 16 + 16].unsqueeze(2).broadcast_to([wc, 16, 64]),
                op=ALU.mult), "V")
            user = pst_user if ci % 2 == 0 else psd_user
            bankT = PST if ci % 2 == 0 else PSDbf
            wt("T", "V", e1V[ci])
            if user[0] is not None:
                wt("T", user[0][0], user[0][1])
            def mm_tr1(e, wc=wc, s3=s3, bankT=bankT):
                ins = None
                for g in range(8):
                    ins = e.transpose(bankT[:, g * wc:g * wc + wc],
                                      E1B[s3][0:wc, g * 128:g * 128 + 128], identbf[0:wc, 0:wc])
                return ins
            tr1T[ci] = op("T", mm_tr1, "T")
            wt("S", "T", tr1T[ci])
            trS[ci] = op("S", lambda e, c0=c0, wc=wc, bankT=bankT: e.copy(
                out=K1T[:, :, c0:c0 + wc],
                in_=bankT[:, 0:8 * wc].rearrange("p (g x) -> p g x", x=wc)), "S")
            user[0] = ("S", trS[ci])

        nw_state = {"itV": itV}
        from collections import deque
        pending = deque()
        nwdone = 0
        bounds = {1: 0, 3: 1, 5: 2, 7: 3, 9: 4, 12: 5}
        for ci in range(17):
            l1_mm(ci)
            if ci - 1 >= 0:
                l1_recip(ci - 1)
            if ci - 2 >= 0:
                l1_fin(ci - 2)
            if ci in bounds:
                sts = newton_stages(bounds[ci], nw_state)
                for st in sts[:3]:
                    st()
                pending.extend(sts[3:])
            elif pending:
                while pending:
                    pending.popleft()()
        while pending:
            pending.popleft()()
        l1_fin(15)
        l1_recip(16)
        l1_fin(16, guard=True)
        NEWT_V = nw_state["itV"]
        K1T_S = cnt["S"]
        snap(4, ("S", K1T_S), K1T[:, :, 0:2048])

        wt("V", "dG", CCR_dG)
        op("V", lambda e: e.reciprocal(out=s3tot[:], in_=CCR[:, 512:520]))
        op("V", lambda e: e.drain())
        k3vn_V = op("V", lambda e: e.tensor_tensor(
            out=K3VN[:], in0=CCR[:, 0:512].rearrange("p (g x) -> p g x", x=64),
            in1=s3tot[:].unsqueeze(2).broadcast_to([128, 8, 64]), op=ALU.mult), "V")
        wt("T", "V", k3vn_V)
        wt("T", "V", NEWT_V)
        wt("T", "S", K1T_S)
        def mm_Z(e):
            ins = None
            for g in range(8):
                ins = e.matmul(PSD[:, g * 64:g * 64 + 64], VTBD[:, g, :], K3VN[:, g, :],
                               start=True, stop=True)
            return ins
        Z_T = op("T", mm_Z, "T")
        wt("V", "T", Z_T)
        op("V", lambda e: e.tensor_copy(out=BDQ[0:64, :, 0:64],
                                        in_=PSD[0:64, :].rearrange("p (g x) -> p g x", x=64)))
        bdz_V = op("V", lambda e: e.tensor_copy(out=BDQ[64:128, :, 64:128],
                                                in_=PSD[64:128, :].rearrange("p (g x) -> p g x", x=64)), "V")

        svS = {}
        flat = []
        for g in range(8):
            for si, (a, b) in enumerate(SL5):
                flat.append((g, si, a, b))
        for n, (g, si, a, b) in enumerate(flat):
            w = b - a
            half = (n % 2) * 512
            wt("T", "V", bdz_V)
            wt("T", "S", K1T_S)
            if n >= 2:
                wt("T", "S", svS[n - 2])
            def mm_sv(e, g=g, a=a, w=w, half=half):
                return e.matmul(PSA[:, half:half + w], BDQ[:, g, :], K1T[:, g, a:a + w],
                                start=True, stop=True)
            svT = op("T", mm_sv, "T")
            wt("S", "T", svT)
            svS[n] = op("S", lambda e, g=g, a=a, w=w, half=half: e.copy(
                out=SVALL[:, g, a:a + w], in_=PSA[:, half:half + w]), "S")
        SV_S = cnt["S"]

        wt("Y", "S", K1T_S)
        XRL1_dY = op("Y", lambda e: e.dma_start(out=X[:, 0:4, :], in_=xsp[:, 0:4, :]), "dY", 16)
        wt("Y", "S", SV_S)
        XRL2_dY = op("Y", lambda e: e.dma_start(out=X[:, 4:8, :], in_=xsp[:, 4:8, :]), "dY", 16)

        projV = {}
        flat = []
        for dp in range(NCH):
            for si, (a, b) in enumerate(SL5):
                flat.append((dp, si, a, b))
        for n, (dp, si, a, b) in enumerate(flat):
            w = b - a
            half = (n % 2) * 512
            if n == 0:
                wt("T", "S", SV_S)
                wt("T", "dG", WPROJ_dG)
            if n >= 2:
                wt("T", "V", projV[n - 2])
            def mm_pj(e, dp=dp, a=a, w=w, half=half):
                ins = None
                for c in range(NCH):
                    ins = e.matmul(PSB[:, half:half + w], WPROJ[:, c, dp * 128:dp * 128 + 128],
                                   SVALL[:, c, a:a + w], start=(c == 0), stop=(c == NCH - 1))
                return ins
            pjT = op("T", mm_pj, "T")
            wt("V", "T", pjT)
            if n == 0:
                wt("V", "dY", XRL1_dY)
            if dp == 4 and si == 0:
                wt("V", "dY", XRL2_dY)
            projV[n] = op("V", lambda e, dp=dp, a=a, w=w, half=half: e.tensor_tensor(
                out=X[:, dp, a:a + w], in0=X[:, dp, a:a + w], in1=PSB[:, half:half + w],
                op=ALU.add), "V")
        MASK_V = op("V", lambda e: e.tensor_scalar(out=X[:, :, 0:2], in0=X[:, :, 0:2],
                                                   scalar1=bmask_sb[:, 0:1], scalar2=None,
                                                   op0=ALU.mult), "V")
        PROJ_T = cnt["T"]
        snap(5, ("V", MASK_V), X[:, :, 2:T])

        rows3_of = make_rows(None, "V", MASK_V,
                             slice_gates=[("V", MASK_V if i == 0 else projV[35 + min(4, i // 2)])
                                          for i in range(len(SL))])
        rows3_of.prefetch = 9
        rows3_of(8)
        wt("G", "T", PROJ_T)
        wt("G", "V", MASK_V)
        op("G", lambda e: e.dma_start(out=WKT, in_=wkt_d.rearrange("(c p) d -> p c d", p=128)), "dG", 16)
        FFNW2K_dG = cnt["dG"]
        op("G", lambda e: e.dma_start(out=WRT, in_=wrt_d.rearrange("(c p) d -> p c d", p=128)), "dG", 16)
        FFNW2R_dG = cnt["dG"]
        op("G", lambda e: e.dma_start(out=WVT, in_=wvt_d.rearrange("(c p) d -> p c d", p=128)), "dG", 16)
        FFNW2_dG = cnt["dG"]
        ffn2_G, _, ffn2_addG = emit_ffn(rows3_of, (FFNW2K_dG, FFNW2R_dG, FFNW2_dG),
                                        stats_S=rows3_of.last_S)

        out_pct = out_d.rearrange("(c p) t -> p c t", p=128)
        for i, (a, b) in enumerate(SL):
            a2 = max(a, 2)
            wt("Y", "G", ffn2_addG[i * NCH + NCH - 1])
            op("Y", lambda e, a2=a2, b=b: e.dma_start(
                out=out_pct[:, :, a2 - 2:b - 2], in_=X[:, :, a2:b]), "dY", 16)
        OUT_dY = cnt["dY"]
        wt("Y", "dY", OUT_dY)

        if stage in snaps:
            lens, cnts, (gsem, gval), src_ap = snaps[stage]
            import os as _os
            if _os.environ.get("KERNEL_DEBUG_SNAPS"):
                print("SNAP", stage, "lens", lens, "gate", gsem, gval, "cnts", cnts)
            for k in streams:
                del streams[k][lens[k]:]
            for k in cnt:
                cnt[k] = cnts[k]
            wt("G", gsem, gval)
            if src_ap == "ROWS121":
                dd = op("G", lambda e: e.dma_start(out=out_d[0:1, 0:2048], in_=srow[:, 2:2050]), "dG", 16)
                dd = op("G", lambda e: e.dma_start(out=out_d[1:2, 0:2048], in_=vrow[:, 2:2050]), "dG", 16)
                dd = op("G", lambda e: e.dma_start(out=out_d[2:3, 0:2048], in_=qrow[:, 2:2050]), "dG", 16)
            elif src_ap == "LNSL131":
                dd = op("G", lambda e: e.dma_start(out=out_d[0:128, 0:2048],
                                                   in_=LNSL[:].rearrange("p a b -> p (a b)")[:, 0:2048]), "dG", 16)
                dd = op("G", lambda e: e.dma_start(out=out_d[128:256, 0:257], in_=BCN[:]), "dG", 16)
                dd = op("G", lambda e: e.dma_start(out=out_d[256:384, 0:257], in_=BCS[:]), "dG", 16)
            elif src_ap == "XRK132":
                dd = op("G", lambda e: e.dma_start(out=out_d[0:128, 0:2048],
                                                   in_=XR2[:].rearrange("p a b c -> p (a b c)")[:, 0:2048]), "dG", 16)
                dd = op("G", lambda e: e.dma_start(out=out_d[128:256, 0:2048],
                                                   in_=XK2[:].rearrange("p a b c -> p (a b c)")[:, 0:2048]), "dG", 16)
                dd = op("G", lambda e: e.dma_start(out=out_d[256:384, 0:2048],
                                                   in_=KACT[:].rearrange("p a b -> p (a b)")), "dG", 16)
            elif src_ap == "ATTN31":
                dd = op("G", lambda e: e.dma_start(out=out_d[0:128, 0:2048], in_=K3Eb), "dG", 16)
                dd = op("G", lambda e: e.dma_start(out=out_d[128:256, 0:1024],
                                                   in_=BDQ[:].rearrange("p a b -> p (a b)")), "dG", 16)
                dd = op("G", lambda e: e.dma_start(out=out_d[256:384, 0:512],
                                                   in_=KLM[:].rearrange("p a b -> p (a b)")), "dG", 16)
                dd = op("G", lambda e: e.dma_start(out=out_d[384:512, 0:2048], in_=KG[:, 0:2048]), "dG", 16)
                dd = op("G", lambda e: e.dma_start(out=out_d[512:640, 0:256],
                                                   in_=POOLED[:].rearrange("p a b -> p (a b)")), "dG", 16)
                dd = op("G", lambda e: e.dma_start(out=out_d[640:768, 0:512],
                                                   in_=POOLEDG[:].rearrange("p a b -> p (a b)")), "dG", 16)
            elif src_ap == "LN2BF21":
                dd = op("G", lambda e: e.dma_start(
                    out=out_d.rearrange("(c p) t -> p c t", p=128),
                    in_=LN2BF[:, :, 2:T]), "dG", 16)
            elif src_ap is CCR:
                dd = op("G", lambda e: e.dma_start(out=out_d[0:128, 0:520], in_=src_ap), "dG", 16)
            else:
                dd = op("G", lambda e: e.dma_start(
                    out=out_d.rearrange("(c p) t -> p c t", p=128), in_=src_ap), "dG", 16)
            wt("G", "dG", dd)

        @block.tensor
        def _(e):
            for f in streams["T"]:
                f(e)

        @block.vector
        def _(e):
            for f in streams["V"]:
                f(e)

        @block.scalar
        def _(e):
            for f in streams["S"]:
                f(e)

        @block.gpsimd
        def _(e):
            for f in streams["G"]:
                f(e)

        @block.sync
        def _(e):
            for f in streams["Y"]:
                f(e)

    return nc


_NC_CACHE = {}


def kernel(L0_attn_x, L0_attn_w_qkv, L0_attn_w_proj, L0_ffn_W_r, L0_ffn_W_k,
           L0_ffn_W_v, L0_ffn_mu_r, L0_ffn_mu_k):
    x = np.ascontiguousarray(np.asarray(L0_attn_x, dtype=np.float32))
    w_qkv = np.ascontiguousarray(np.asarray(L0_attn_w_qkv, dtype=np.float32))
    w_proj = np.ascontiguousarray(np.asarray(L0_attn_w_proj, dtype=np.float32))
    wrt = np.ascontiguousarray(np.asarray(L0_ffn_W_r, dtype=np.float32).T)
    wkt = np.ascontiguousarray(np.asarray(L0_ffn_W_k, dtype=np.float32).T)
    wvt = np.ascontiguousarray(np.asarray(L0_ffn_W_v, dtype=np.float32).T)
    mu = np.stack([np.asarray(L0_ffn_mu_r, dtype=np.float32).reshape(8, 128),
                   np.asarray(L0_ffn_mu_k, dtype=np.float32).reshape(8, 128)], axis=0)
    mu = np.ascontiguousarray(mu)

    ident = np.eye(128, dtype=np.float32).astype(ml_dtypes.bfloat16)
    i64 = np.concatenate([np.eye(64, dtype=np.float32)] * 2, axis=0)
    ints = np.ascontiguousarray(
        np.stack([7.0 * i64, 15.0 * i64, 13.0 * i64], axis=1).astype(ml_dtypes.bfloat16))
    onescol = np.ones((128, 1), dtype=np.float32).astype(ml_dtypes.bfloat16)
    onesbd2 = np.zeros((128, 2), dtype=np.float32)
    onesbd2[0:64, 0] = 1.0
    onesbd2[64:128, 1] = 1.0
    onesbd2 = onesbd2.astype(ml_dtypes.bfloat16)
    bconst = np.zeros((1, 2, 128), dtype=np.float32)
    bconst[0, 0, :] = -1.0 / 1024.0
    bconst[0, 1, :] = 1.0
    b2 = np.zeros((2, 128), dtype=np.float32)
    b2[0, 0:64] = 1.0
    b2[1, 64:128] = 1.0

    import os
    _stage = int(os.environ.get("KERNEL_STAGE", "0"))
    if ("nc", _stage) not in _NC_CACHE:
        _NC_CACHE[("nc", _stage)] = build_nc(_stage)
    nc = _NC_CACHE[("nc", _stage)]

    in_maps = []
    for core in range(8):
        b = core // 2
        half = core % 2
        if half == 0:
            tile = np.concatenate([np.zeros((2, 1024), np.float32), x[b, 0:2048]], axis=0)
            bm = np.zeros((128, 1), np.float32)
        else:
            tile = x[b, 2046:4096]
            bm = np.ones((128, 1), np.float32)
        xtile = np.ascontiguousarray(tile.T)
        in_maps.append({
            "xt": xtile, "wqkv": w_qkv, "wproj": w_proj, "wrt": wrt, "wkt": wkt,
            "wvt": wvt, "mud": mu, "bmaskd": bm, "identbfd": ident, "intsd": ints,
            "onescold": onescol, "onesbd2d": onesbd2, "bconstd": bconst,
            "b2d": b2,
        })

    res = run_bass_kernel_spmd(nc, in_maps, core_ids=list(range(8)))
    out = np.empty((4, 4096, 1024), dtype=np.float32)
    for core in range(8):
        b = core // 2
        half = core % 2
        o = res.results[core]["out"]
        out[b, half * 2048:(half + 1) * 2048] = o.T
    return out


# version 19
# speedup vs baseline: 1.1060x; 1.0043x over previous
import numpy as np
import ml_dtypes
import concourse.bass as bass
import concourse.mybir as mybir
from concourse.bass_utils import run_bass_kernel_spmd

F32 = mybir.dt.float32
BF16 = mybir.dt.bfloat16
AF = mybir.ActivationFunctionType
ALU = mybir.AluOpType
AX = mybir.AxisListType

T = 2050
NCH = 8
EPS = 1e-5

SL = [(i * 256, (i + 1) * 256) for i in range(8)] + [(2048, 2050)]
SL5 = [(0, 512), (512, 1024), (1024, 1536), (1536, 2048), (2048, 2050)]
SLO = [(2 + i * 512, 2 + (i + 1) * 512) for i in range(4)]
CH17 = [(i * 128, i * 128 + 128) for i in range(16)] + [(2048, 2050)]
CHO16 = [(2 + i * 128, 2 + i * 128 + 128) for i in range(16)]
RG = [[0, 1], [2, 3], [4, 5], [6, 7]]


def build_nc(stage=0):
    import os
    SKIP = set(os.environ.get("KERNEL_SKIP", "").split(","))
    nc = bass.Bass("TRN2", target_bir_lowering=False)

    xt = nc.dram_tensor("xt", [1024, T], F32, kind="ExternalInput").ap()
    wqkv = nc.dram_tensor("wqkv", [1024, 3072], F32, kind="ExternalInput").ap()
    wproj_d = nc.dram_tensor("wproj", [1024, 1024], F32, kind="ExternalInput").ap()
    wrt_d = nc.dram_tensor("wrt", [1024, 1024], F32, kind="ExternalInput").ap()
    wkt_d = nc.dram_tensor("wkt", [1024, 1024], F32, kind="ExternalInput").ap()
    wvt_d = nc.dram_tensor("wvt", [1024, 1024], F32, kind="ExternalInput").ap()
    mu_d = nc.dram_tensor("mud", [2, 8, 128], F32, kind="ExternalInput").ap()
    bmask_d = nc.dram_tensor("bmaskd", [128, 1], F32, kind="ExternalInput").ap()
    identbf_d = nc.dram_tensor("identbfd", [128, 128], BF16, kind="ExternalInput").ap()
    ints_d = nc.dram_tensor("intsd", [128, 3, 64], BF16, kind="ExternalInput").ap()
    onescol_d = nc.dram_tensor("onescold", [128, 1], BF16, kind="ExternalInput").ap()
    onesbd2_d = nc.dram_tensor("onesbd2d", [128, 2], BF16, kind="ExternalInput").ap()
    bconst_d = nc.dram_tensor("bconstd", [1, 2, 128], F32, kind="ExternalInput").ap()
    b2_d = nc.dram_tensor("b2d", [2, 128], F32, kind="ExternalInput").ap()

    out_d = nc.dram_tensor("out", [1024, 2048], F32, kind="ExternalOutput").ap()

    xsp = nc.dram_tensor("xsp", [128, NCH, T], F32).ap()
    cc3in = nc.dram_tensor("cc3in", [1024, 32], BF16)
    cc3out = nc.dram_tensor("cc3out", [2, 1024, 32], BF16)
    cc4in = nc.dram_tensor("cc4in", [128, 520], F32)
    cc4out = nc.dram_tensor("cc4out", [128, 520], F32)

    streams = {e: [] for e in ("T", "V", "S", "G", "Y")}
    cnt = {"T": 0, "V": 0, "S": 0, "G": 0, "dY": 0, "dG": 0, "CC": 0,
           "dX": 0, "dX2": 0, "dS0": 0, "dS1": 0, "dS2": 0,
           "dW0": 0, "dW1": 0, "dW2": 0, "dQ0": 0, "dQ1": 0, "dE": 0}
    semh = {}

    def op(eng, fn, inc=None, n=1):
        if inc is None:
            streams[eng].append(lambda e, fn=fn: fn(e))
            return None
        cnt[inc] += n
        def thunk(e, fn=fn, inc=inc, n=n):
            fn(e).then_inc(semh[inc], n)
        streams[eng].append(thunk)
        return cnt[inc]

    def wt(eng, sem, val):
        if val is not None and val > 0:
            streams[eng].append(lambda e, sem=sem, val=val: e.wait_ge(semh[sem], val))

    from contextlib import ExitStack
    with ExitStack() as stack:
        ec = stack.enter_context
        XRb = ec(nc.sbuf_tensor("XR", [128, 16400], F32))
        WA = ec(nc.sbuf_tensor("WA", [128, 24576], BF16))
        A3 = ec(nc.sbuf_tensor("A3", [128, 16400], BF16))
        SCR = ec(nc.sbuf_tensor("SCR", [128, 4618], F32))
        GT2 = ec(nc.sbuf_tensor("GT2", [128, 4100], BF16))
        ROWS = ec(nc.sbuf_tensor("ROWS", [128, 4, 2050], F32))
        identbf = ec(nc.sbuf_tensor("identbf", [128, 128], BF16))
        ints = ec(nc.sbuf_tensor("ints", [128, 3, 64], BF16))
        onescol = ec(nc.sbuf_tensor("onescol", [128, 1], BF16))
        onesbd2 = ec(nc.sbuf_tensor("onesbd2", [128, 2], BF16))
        bconst = ec(nc.sbuf_tensor("bconst", [1, 2, 128], F32))
        b2c = ec(nc.sbuf_tensor("b2c", [2, 128], F32))
        mu_sb = ec(nc.sbuf_tensor("mu_sb", [128, 2, 8], F32))
        omu_sb = ec(nc.sbuf_tensor("omu_sb", [128, 2, 8], F32))
        bmask_sb = ec(nc.sbuf_tensor("bmask_sb", [128, 1], F32))
        sums3p = ec(nc.sbuf_tensor("sums3p", [128, 8, 4], F32))
        s3tot = ec(nc.sbuf_tensor("s3tot", [128, 8], F32))
        denr = ec(nc.sbuf_tensor("denr", [128, 8], F32))
        rc16 = ec(nc.sbuf_tensor("rc16", [128, 48], F32))
        rc8 = ec(nc.sbuf_tensor("rc8", [128, 8], F32))
        cs2 = ec(nc.sbuf_tensor("cs2", [2, 8, 64], F32))
        mx2 = ec(nc.sbuf_tensor("mx2", [2, 8], F32))
        PSA = ec(nc.psum_tensor("PSA", [128, 1024], F32))
        PSB = ec(nc.psum_tensor("PSB", [128, 1024], F32))
        PSC = ec(nc.psum_tensor("PSC", [128, 1024], F32))
        PSD = ec(nc.psum_tensor("PSD", [128, 512], F32))
        PST = ec(nc.psum_tensor("PST", [128, 1024], BF16))
        semT = ec(nc.semaphore("semT"))
        semV = ec(nc.semaphore("semV"))
        semS = ec(nc.semaphore("semS"))
        semG = ec(nc.semaphore("semG"))
        dY = ec(nc.semaphore("dY"))
        dG = ec(nc.semaphore("dG"))
        scc = ec(nc.semaphore("scc"))
        block = ec(nc.Block())
        semh.update({"T": semT, "V": semV, "S": semS, "G": semG,
                     "dY": dY, "dG": dG, "CC": scc})
        for _sn in ("dX", "dX2", "dS0", "dS1", "dS2",
                    "dW0", "dW1", "dW2", "dQ0", "dQ1", "dE"):
            semh[_sn] = ec(nc.semaphore(_sn))

        def view(base, off, shape):
            n = int(np.prod(shape))
            apx = base[:, off:off + n]
            if len(shape) == 2:
                apx = apx.rearrange("p (a b) -> p a b", b=shape[1])
            elif len(shape) == 3:
                apx = apx.rearrange("p (a b c) -> p a b c", b=shape[1], c=shape[2])
            return apx

        X = view(XRb, 0, [NCH, T])
        XRbf = XRb[:, :].bitcast(BF16)
        QALL = view(XRbf, 0, [NCH, T])
        K1T = view(XRbf, NCH * T, [NCH, T])
        PL1 = view(XRb, 8200, [NCH, 1024])

        WRT = view(WA, 0, [NCH, 1024])
        WKT = view(WA, 8192, [NCH, 1024])
        WVT = view(WA, 16384, [NCH, 1024])
        WG = [view(WA, 0, [NCH, 384]), view(WA, 3072, [NCH, 384])]
        WG4 = [view(WA, 0, [NCH, 3, 128]), view(WA, 3072, [NCH, 3, 128])]
        BDQ = view(WA, 6144, [NCH, 128])
        K2E = view(WA, 7168, [NCH, 64])
        BDKLM = view(WA, 7680, [NCH, 128])
        KLM = view(WA, 8704, [NCH, 64])
        POOLED = view(WA, 9216, [NCH, 32])
        POOLEDG = view(WA, 9472, [NCH, 64])
        KG = WA[:, 9984:9984 + T]
        VG = WA[:, 12034:12034 + T]
        K3Eb = WA[:, 14084:14084 + 2048]
        K3T = view(WA, 16132, [16, 128])
        VTOK = view(WA, 18180, [16, 128])
        E1B = [WA[:, 20228:20228 + 1024], WA[:, 21252:21252 + 1024],
               WA[:, 22276:22276 + 1024]]
        KG2 = GT2[:, 0:T]
        VG2 = GT2[:, T:2 * T]
        K3Eb2 = K3Eb
        K3T2 = K3T
        VTOK2 = VTOK
        KTBD = view(WA, 0, [NCH, 128])
        PTBD = view(WA, 1024, [NCH, 128])
        VTBD = view(WA, 2048, [NCH, 128])
        VS_ = view(WA, 3072, [NCH, 64])
        NS1 = view(WA, 3584, [NCH, 64])
        NS2 = view(WA, 4096, [NCH, 64])
        NS3 = view(WA, 4608, [NCH, 64])
        BDP = view(WA, 5120, [NCH, 128])
        K3VN = view(WA, 9984, [NCH, 64])
        WPROJ = view(WA, 10496, [NCH, 1024])

        XR2 = view(A3, 0, [NCH, 2, 256])
        XK2 = view(A3, 4096, [NCH, 2, 256])
        KACT = view(A3, 8192, [NCH, 256])
        X2SL = [view(A3, 0, [NCH, 256]), view(A3, 2048, [NCH, 256])]
        SQSL = [view(A3, 4096, [NCH, 256]), view(A3, 6144, [NCH, 256])]
        LNSLB = view(A3, 10240, [NCH, 257])
        DSLB = view(A3, 12296, [NCH, 256])
        LN2BF = view(A3, 0, [NCH, T])
        SVALL = view(A3, 0, [NCH, T])

        LNSL = view(SCR, 0, [NCH, 257])
        T1DB = view(SCR, 2056, [2, 256])
        KTMP2 = view(SCR, 2568, [2, 256])
        RCH2 = view(SCR, 3080, [2, 256])
        TCH2 = view(SCR, 3592, [2, 256])
        BCN = SCR[:, 4104:4104 + 257]
        BCS = SCR[:, 4361:4361 + 257]
        LNSCR = view(SCR, 0, [NCH, 256])
        PL2 = view(SCR, 0, [NCH, 512])
        PL3 = view(XRb, 8200, [NCH, 256])
        PL4 = view(SCR, 0, [NCH, 128])
        PL5 = view(XRb, 8200 + 4096, [NCH, 64])
        PL6 = view(SCR, 1024, [NCH, 32])
        CCS = SCR[:, 0:520]
        CCR = SCR[:, 520:1040]

        srow = ROWS[0:1, 0, :]
        qrow = ROWS[0:1, 1, :]
        mrow = ROWS[0:1, 2, :]
        vrow = ROWS[0:1, 3, :]

        for src, dst in [(identbf_d, identbf), (ints_d, ints), (onescol_d, onescol),
                         (onesbd2_d, onesbd2), (bconst_d, bconst),
                         (b2_d, b2c), (bmask_d, bmask_sb)]:
            op("Y", lambda e, s=src, d=dst: e.dma_start(out=d[:], in_=s), "dY", 16)
        (op("Y", lambda e: e.dma_start(out=mu_sb[:], in_=mu_d.rearrange("i c p -> p i c"),
                                      allow_slow_non_contiguous=True), "dY", 16)
         if "mu" not in SKIP else op("Y", lambda e: e.dma_start(out=mu_sb[:, 0, 0:1], in_=mu_d[0:1, 0, 0:1].transpose([1, 0]) if False else mu_d[0, 0:1, 0:1]), "dY", 16))
        PRO_dY = cnt["dY"]
        op("Y", lambda e: e.dma_start(out=X[:], in_=xt.rearrange("(c p) t -> p c t", p=128)), "dX", 16)

        def wg_load_thunks(g, buf):
            outs = []
            for t3 in range(3):
                col = t3 * 1024 + g * 128
                outs.append(lambda e, t3=t3, col=col, buf=buf: e.dma_start(
                    out=view(WA, buf * 3072 + t3 * 128, [NCH, 128]) if False else WG4[buf][:, :, t3, :],
                    in_=wqkv[:, col:col + 128].rearrange("(c p) d -> p c d", p=128)))
            return outs

        wt("V", "dY", PRO_dY)
        wt("V", "dX", 16)
        op("V", lambda e: e.tensor_scalar(out=omu_sb[:], in0=mu_sb[:], scalar1=-1.0,
                                          scalar2=1.0, op0=ALU.mult, op1=ALU.add))

        SCRbf = SCR[:, :].bitcast(BF16)
        X2SLB = [GT2[:, 0:2048].rearrange("p (a b) -> p a b", b=256),
                 GT2[:, 2048:4096].rearrange("p (a b) -> p a b", b=256)]
        SQSLB = [SCRbf[:, 4112:4112 + 2048].rearrange("p (a b) -> p a b", b=256),
                 SCRbf[:, 6160:6160 + 2048].rearrange("p (a b) -> p a b", b=256)]

        def make_rows(src_dram, gate_sem, gate_val, slice_gates=None, slots=None):
            SX, SQ = slots if slots is not None else (X2SL, SQSL)
            """Lazy per-slice LN stats. Returns rows_of(i) -> (sem, tick): emits
            stats slices up to i+2 and per-slice finishes up to i on demand, so
            the T stream interleaves stat matmuls with consuming bcasts."""
            sum_T = {}
            sq_T = {}
            ev_sum_S = {}
            ev_sq_S = {}
            castG = {}
            sqV = {}
            fin_tick = {}

            def emit_stats(i):
                a, b = SL[i]
                w = b - a
                pg = i % 2
                if slice_gates is not None:
                    wt("G", slice_gates[i][0], slice_gates[i][1])
                elif gate_sem is not None and i == 0:
                    wt("G", gate_sem, gate_val)
                if i >= 2:
                    wt("G", "T", sum_T[i - 2])
                    wt("G", "S", sqV[i - 2])
                dsn = "dS%d" % (i % 3)
                if src_dram is not None:
                    castG[i] = op("G", lambda e, a=a, w=w, pg=pg: e.dma_start(
                        out=SX[pg][:, :, 0:w],
                        in_=src_dram.rearrange("(c p) t -> p c t", p=128)[:, :, a:a + w]), dsn, 16)
                else:
                    castG[i] = op("G", lambda e, a=a, w=w, pg=pg: e.dma_start(
                        out=SX[pg][:, :, 0:w], in_=X[:, :, a:a + w]), dsn, 16)
                wt("S", dsn, castG[i])
                if i >= 2:
                    wt("S", "T", sq_T[i - 2])
                sqV[i] = op("S", lambda e, w=w, pg=pg: e.activation(
                    out=SQ[pg][:, :, 0:w], in_=SX[pg][:, :, 0:w],
                    func=AF.Square), "S")
                wt("T", "dS%d" % (i % 3), castG[i])
                if i == 0:
                    wt("T", "dY", PRO_dY)
                if i >= 2:
                    wt("T", "S", ev_sum_S[i - 2])
                def mm_sum(e, w=w, pg=pg):
                    ins = None
                    for kc in range(NCH):
                        ins = e.matmul(PSC[0:1, pg * 512:pg * 512 + w], onescol[:],
                                       SX[pg][:, kc, 0:w], start=(kc == 0), stop=(kc == NCH - 1))
                    return ins
                sum_T[i] = op("T", mm_sum, "T")
                wt("T", "S", sqV[i])
                if i >= 2:
                    wt("T", "S", ev_sq_S[i - 2])
                def mm_sq(e, w=w, pg=pg):
                    ins = None
                    for kc in range(NCH):
                        ins = e.matmul(PSB[0:1, pg * 512:pg * 512 + w], onescol[:],
                                       SQ[pg][:, kc, 0:w], start=(kc == 0), stop=(kc == NCH - 1))
                    return ins
                sq_T[i] = op("T", mm_sq, "T")
                wt("S", "T", sum_T[i])
                ev_sum_S[i] = op("S", lambda e, a=a, w=w, pg=pg: e.copy(
                    out=srow[:, a:a + w], in_=PSC[0:1, pg * 512:pg * 512 + w]), "S")
                wt("S", "T", sq_T[i])
                ev_sq_S[i] = op("S", lambda e, a=a, w=w, pg=pg: e.copy(
                    out=qrow[:, a:a + w], in_=PSB[0:1, pg * 512:pg * 512 + w]), "S")

            def emit_finish(k):
                a = SL[k][0]
                b = 2050 if k == 7 else SL[k][1]
                wt("V", "S", ev_sq_S[8 if k == 7 else k])
                op("V", lambda e, a=a, b=b: e.tensor_tensor(
                    out=vrow[:, a:b], in0=srow[:, a:b], in1=srow[:, a:b], op=ALU.mult))
                op("V", lambda e, a=a, b=b: e.scalar_tensor_tensor(
                    out=vrow[:, a:b], in0=vrow[:, a:b], scalar=1.0 / 1024.0,
                    in1=qrow[:, a:b], op0=ALU.mult, op1=ALU.subtract))
                epsV = op("V", lambda e, a=a, b=b: e.tensor_scalar(
                    out=vrow[:, a:b], in0=vrow[:, a:b], scalar1=-1.0 / 1024.0,
                    scalar2=EPS, op0=ALU.mult, op1=ALU.add), "V")
                wt("S", "V", epsV)
                sdS = op("S", lambda e, a=a, b=b: e.activation(
                    out=vrow[:, a:b], in_=vrow[:, a:b], func=AF.Sqrt), "S")
                wt("V", "S", sdS)
                fin_tick[k] = op("V", lambda e, a=a, b=b: e.reciprocal(
                    out=vrow[:, a:b], in_=vrow[:, a:b]), "V")

            def rows_of(i):
                k = min(i, 7)
                for j in range(len(castG), min(i + 1 + rows_of.prefetch, 9)):
                    emit_stats(j)
                for j in range(len(fin_tick), k + 1):
                    if j == 7 and 8 not in castG:
                        emit_stats(8)
                    emit_finish(j)
                return ("V", fin_tick[k])

            rows_of.prefetch = 2
            rows_of.last_S = lambda: ev_sq_S[8]
            return rows_of

        def emit_bcast_slice(i, a, b, rows_of, prev_user_V, prev_user_G=None):
            w = b - a
            tick = rows_of(i)
            wt("T", tick[0], tick[1])
            if prev_user_V is not None:
                wt("T", "V", prev_user_V)
            if prev_user_G is not None:
                wt("T", "G", prev_user_G)
            def mm_bc(e, a=a, w=w):
                e.matmul(PSD[:, 0:w], bconst[0:1, 0, :], srow[:, a:a + w], start=True, stop=True)
                return e.matmul(PSD[:, 256:256 + w], bconst[0:1, 1, :], vrow[:, a:a + w], start=True, stop=True)
            t2 = op("T", mm_bc, "T")
            wt("V", "T", t2)
            op("V", lambda e, w=w: e.tensor_copy(out=BCN[:, 1:1 + w], in_=PSD[:, 0:w]))
            vv = op("V", lambda e, w=w: e.tensor_copy(out=BCS[:, 1:1 + w], in_=PSD[:, 256:256 + w]), "V")
            if w < 64:
                op("V", lambda e: e.drain())
            return vv

        def emit_ffn(rows_V, w_dG, snap_cb=None, stats_S=None):
            snap_cb2 = [snap_cb]
            bc_user = None
            reluS = {}
            kactV = {}
            sigS = {}
            rvV = {}
            addG = {}
            mixV = {}
            t1hist = []
            mix_hist = []
            vT = {}
            NRV = 0
            def stageA(i):
                nonlocal bc_user
                a, b = SL[i]
                w = b - a
                bc_user = emit_bcast_slice(i, a, b, rows_V, bc_user)
                if i >= 2:
                    wt("V", "T", vT[(i - 2, NCH - 1)])
                if i == 0:
                    op("V", lambda e: e.memset(LNSLB[:, :, 0:1], 0.0))
                else:
                    op("V", lambda e: e.tensor_copy(out=LNSLB[:, :, 0:1], in_=LNSLB[:, :, 256:257]))
                if w < 64:
                    op("V", lambda e: e.drain())
                op("V", lambda e, a=a, w=w: e.tensor_tensor(
                    out=LNSL[:, :, 1:1 + w], in0=X[:, :, a:a + w],
                    in1=BCN[:, 1:1 + w].unsqueeze(1).broadcast_to([128, NCH, w]),
                    op=ALU.add))
                if w < 64:
                    op("V", lambda e: e.drain())
                lnV = op("V", lambda e, w=w: e.tensor_tensor(
                    out=LNSLB[:, :, 1:1 + w], in0=LNSL[:, :, 1:1 + w],
                    in1=BCS[:, 1:1 + w].unsqueeze(1).broadcast_to([128, NCH, w]),
                    op=ALU.mult), "V")
                if w < 64:
                    op("V", lambda e: e.drain())
                if i == len(SL) - 1 and 15 not in snaps:
                    snaps[15] = ({k: len(v) for k, v in streams.items()}, dict(cnt),
                                 ("V", lnV), "LNSL131")
                pg = i % 2
                op("V", lambda e, w=w: e.tensor_tensor(
                    out=DSLB[:, :, 0:w], in0=LNSLB[:, :, 1:1 + w],
                    in1=LNSLB[:, :, 0:w], op=ALU.subtract))
                for mi, dst2 in [(0, XR2), (1, XK2)]:
                    for c in range(NCH):
                        op("V", lambda e, c=c, w=w, mi=mi, dst2=dst2, pg=pg:
                           e.scalar_tensor_tensor(
                               out=dst2[:, c, pg, 0:w], in0=DSLB[:, c, 0:w],
                               scalar=mu_sb[:, mi, c:c + 1], in1=LNSLB[:, c, 0:w],
                               op0=ALU.mult, op1=ALU.add), "V")
                mixV[i] = cnt["V"]

            def stageB(i):
                a, b = SL[i]
                w = b - a
                pg = i % 2
                wt("T", "V", mixV[i])
                if i == 0:
                    wt("T", w_dG[0][0], w_dG[0][1])
                for dp in range(NCH):
                    half = (dp % 2) * 512
                    prev_relu = reluS.get((i, dp - 2)) if dp >= 2 else reluS.get((i - 1, dp + 6))
                    if prev_relu is not None:
                        wt("T", "S", prev_relu)
                    def mm_k(e, w=w, pg=pg, dp=dp, half=half):
                        ins = None
                        for c in range(NCH):
                            ins = e.matmul(PSA[:, half:half + w], WKT[:, c, dp * 128:dp * 128 + 128],
                                           XK2[:, c, pg, 0:w], start=(c == 0), stop=(c == NCH - 1))
                        return ins
                    kTt = op("T", mm_k, "T")
                    wt("S", "T", kTt)
                    op("S", lambda e, w=w, dp=dp, half=half: e.activation(
                        out=KTMP2[:, dp % 2, 0:w], in_=PSA[:, half:half + w], func=AF.Relu))
                    reluS[(i, dp)] = op("S", lambda e, w=w, dp=dp: e.activation(
                        out=KACT[:, dp, 0:w], in_=KTMP2[:, dp % 2, 0:w], func=AF.Square), "S")
                    kactV[(i, dp)] = reluS[(i, dp)]

                if i == 0:
                    wt("T", w_dG[1][0], w_dG[1][1])
                    wt("T", w_dG[2][0], w_dG[2][1])
                if i == 0 and stats_S is not None:
                    wt("T", "S", stats_S())
                for dp in range(NCH):
                    half = (dp % 2) * 512
                    n = i * NCH + dp
                    prev_sig = sigS.get(n - 2)
                    if prev_sig is not None:
                        wt("T", "S", prev_sig)
                    def mm_r(e, w=w, pg=pg, dp=dp, half=half):
                        ins = None
                        for c in range(NCH):
                            ins = e.matmul(PSB[:, half:half + w], WRT[:, c, dp * 128:dp * 128 + 128],
                                           XR2[:, c, pg, 0:w], start=(c == 0), stop=(c == NCH - 1))
                        return ins
                    rTt = op("T", mm_r, "T")
                    if dp == 0:
                        wt("T", "S", kactV[(i, NCH - 1)])
                    prev_rv = rvV.get(n - 2)
                    if prev_rv is not None:
                        wt("T", "V", prev_rv)
                    def mm_v(e, w=w, dp=dp, half=half):
                        ins = None
                        for c in range(NCH):
                            ins = e.matmul(PSC[:, half:half + w], WVT[:, c, dp * 128:dp * 128 + 128],
                                           KACT[:, c, 0:w], start=(c == 0), stop=(c == NCH - 1))
                        return ins
                    vT[(i, dp)] = op("T", mm_v, "T")
                    wt("S", "T", rTt)
                    prev_rv1 = rvV.get(n - 1)
                    if prev_rv1 is not None:
                        wt("S", "V", prev_rv1)
                    sigS[n] = op("S", lambda e, w=w, dp=dp, half=half: e.activation(
                        out=RCH2[:, dp % 2, 0:w], in_=PSB[:, half:half + w], func=AF.Sigmoid), "S")
                    wt("V", "S", sigS[n])
                    wt("V", "T", vT[(i, dp)])
                    prev_add = addG.get(n - 2)
                    if prev_add is not None:
                        wt("V", "G", prev_add)
                    rvV[n] = op("V", lambda e, w=w, dp=dp, half=half: e.scalar_tensor_tensor(
                        out=TCH2[:, dp % 2, 0:w], in0=PSC[:, half:half + w], scalar=0.5,
                        in1=RCH2[:, dp % 2, 0:w], op0=ALU.mult, op1=ALU.mult), "V")
                    wt("G", "V", rvV[n])
                    addG[n] = op("G", lambda e, a=a, b=b, w=w, dp=dp: e.tensor_tensor(
                        out=X[:, dp, a:b], in0=X[:, dp, a:b], in1=TCH2[:, dp % 2, 0:w],
                        op=ALU.add), "G")
                if i == 0 and snap_cb is not None:
                    snap_cb2[0] = None

            for i in range(len(SL) + 1):
                if i < len(SL):
                    stageA(i)
                if i >= 1:
                    stageB(i - 1)
                    if i - 1 == 0 and snap_cb is not None:
                        snap_cb(cnt["G"])
            return cnt["G"], bc_user, addG

        snaps = {}
        def snap(n, gate, src_ap):
            snaps[n] = ({k: len(v) for k, v in streams.items()}, dict(cnt), gate, src_ap)

        rows1_of = make_rows(xt, None, None)
        rows1_of.prefetch = 9
        rows1_t0 = rows1_of(8)
        if "wcast" not in SKIP:
            op("G", lambda e: e.dma_start(out=WKT, in_=wkt_d.rearrange("(c p) d -> p c d", p=128)), "dW0", 16)
        FFNWK_dG = cnt["dW0"]
        if "wcast" not in SKIP:
            op("G", lambda e: e.dma_start(out=WRT, in_=wrt_d.rearrange("(c p) d -> p c d", p=128)), "dW1", 16)
        FFNWR_dG = cnt["dW1"]
        if "wcast" not in SKIP:
            op("G", lambda e: e.dma_start(out=WVT, in_=wvt_d.rearrange("(c p) d -> p c d", p=128)), "dW2", 16)
        FFNW_dG = cnt["dW2"]
        snap(11, ("dW2", FFNW_dG), X[:, :, 2:T])
        if stage in (12, 121):
            rows1_full = rows1_of(8)
        else:
            rows1_full = rows1_t0
        snap(12, rows1_full, X[:, :, 2:T])
        snaps[121] = (snaps[12][0], snaps[12][1], rows1_full, "ROWS121")
        def _snap13(g_tick):
            snap(13, ("G", g_tick), X[:, :, 2:T])
            snaps[131] = (snaps[13][0], snaps[13][1], ("G", g_tick), "LNSL131")
            snaps[132] = (snaps[13][0], snaps[13][1], ("G", g_tick), "XRK132")
        wt("G", "dX", 16)
        ffn1_G, _, ffn1_addG = emit_ffn(rows1_of, (("dW0", FFNWK_dG), ("dW1", FFNWR_dG), ("dW2", FFNW_dG)),
                                        snap_cb=_snap13, stats_S=rows1_of.last_S)
        ffn1_T = cnt["T"]
        snaps[14] = ({k: len(v) for k, v in streams.items()}, dict(cnt), ("G", ffn1_G), "LNSL131")
        snaps[141] = ({k: len(v) for k, v in streams.items()}, dict(cnt), ("G", ffn1_G), "XRK132")

        snap(1, ("G", ffn1_G), X[:, :, 2:T])
        rows2_of = make_rows(None, "G", ffn1_G,
                             slice_gates=[("G", ffn1_addG[i * NCH + NCH - 1]) for i in range(len(SL))],
                             slots=(X2SLB, SQSLB))

        wt("Y", "G", ffn1_G)
        XSPILL_dY = op("Y", lambda e: e.dma_start(out=xsp, in_=X[:]), "dY", 16)

        bc_user = None
        ln2V = {}
        for i, (a, b) in enumerate(SL):
            w = b - a
            bc_user = emit_bcast_slice(i, a, b, rows2_of, bc_user)
            op("V", lambda e, a=a, w=w: e.tensor_tensor(
                out=LNSCR[:, :, 0:w], in0=X[:, :, a:a + w],
                in1=BCN[:, 1:1 + w].unsqueeze(1).broadcast_to([128, NCH, w]), op=ALU.add))
            if w < 64:
                op("V", lambda e: e.drain())
            bc_user = op("V", lambda e, a=a, w=w: e.tensor_tensor(
                out=LN2BF[:, :, a:a + w], in0=LNSCR[:, :, 0:w],
                in1=BCS[:, 1:1 + w].unsqueeze(1).broadcast_to([128, NCH, w]), op=ALU.mult), "V")
            if w < 64:
                op("V", lambda e: e.drain())
            ln2V[i] = bc_user
        LN2_V = cnt["V"]

        snaps[21] = ({k: len(v) for k, v in streams.items()}, dict(cnt),
                     ("V", LN2_V), "LN2BF21")
        snaps[22] = ({k: len(v) for k, v in streams.items()}, dict(cnt),
                     ("V", LN2_V), "ROWS121")
        wt("V", "dY", XSPILL_dY)
        op("V", lambda e: e.tensor_tensor(out=PL1[:], in0=LN2BF[:, :, 2:2049:2],
                                          in1=LN2BF[:, :, 3:2050:2], op=ALU.add))
        op("V", lambda e: e.tensor_tensor(out=PL2[:], in0=PL1[:, :, 0:1024:2],
                                          in1=PL1[:, :, 1:1024:2], op=ALU.add))
        op("V", lambda e: e.tensor_tensor(out=PL3[:], in0=PL2[:, :, 0:512:2],
                                          in1=PL2[:, :, 1:512:2], op=ALU.add))
        op("V", lambda e: e.tensor_tensor(out=PL4[:], in0=PL3[:, :, 0:256:2],
                                          in1=PL3[:, :, 1:256:2], op=ALU.add))
        op("V", lambda e: e.tensor_tensor(out=PL5[:], in0=PL4[:, :, 0:128:2],
                                          in1=PL4[:, :, 1:128:2], op=ALU.add))
        op("V", lambda e: e.tensor_tensor(out=PL6[:], in0=PL5[:, :, 0:64:2],
                                          in1=PL5[:, :, 1:64:2], op=ALU.add))
        POOL_V = op("V", lambda e: e.tensor_scalar(out=POOLED[:], in0=PL6[:], scalar1=1.0 / 64.0,
                                                   scalar2=None, op0=ALU.mult), "V")

        wt("G", "T", ffn1_T)
        WGL = {}
        for th in wg_load_thunks(0, 0):
            WGL[0] = op("G", th, "dQ0", 16)
        for th in wg_load_thunks(1, 1):
            WGL[1] = op("G", th, "dQ1", 16)

        wt("G", "V", POOL_V)
        op("G", lambda e: e.dma_start(out=cc3in.ap().rearrange("(c p) s -> p c s", p=128),
                                      in_=POOLED[:]), "dE", 16)
        wt("G", "dE", cnt["dE"])
        CC3 = op("G", lambda e: e.collective_compute(
            "AllGather", ALU.bypass, replica_groups=RG,
            ins=[cc3in.ap().opt()], outs=[cc3out.ap().opt()]), "CC", 1)
        wt("G", "CC", CC3)
        op("G", lambda e: e.dma_start(
            out=POOLEDG[:, :, 0:32],
            in_=cc3out.ap()[0].rearrange("(c p) s -> p c s", p=128)), "dE", 16)
        PGL_dG = op("G", lambda e: e.dma_start(
            out=POOLEDG[:, :, 32:64],
            in_=cc3out.ap()[1].rearrange("(c p) s -> p c s", p=128)), "dE", 16)

        snap(2, ("dE", PGL_dG), X[:, :, 2:T])
        wt("V", "T", ffn1_T)
        op("V", lambda e: e.memset(BDQ[:], 0.0))
        BDZERO_V = op("V", lambda e: e.memset(BDKLM[:], 0.0), "V")

        lastWG_T = {}
        qlmV = {}
        klmT = {}
        l2T = {}
        k3eS = {}
        k3tV = {}
        vtokS = {}
        k3vT = {}
        kevV = {}
        vevS = {}
        KGP = [KG, KG2]
        VGP = [VG, VG2]
        K3EP = [K3Eb, K3Eb2]
        K3TP = [K3T, K3T2]
        VTOKP = [VTOK, VTOK2]
        evhist = []

        def stage_Q(g):
            wg = WG[g % 2]
            pq = g % 2
            wt("T", "dQ%d" % (g % 2), WGL[g])
            for t3 in range(3):
                for si, (a, b) in enumerate(SL5):
                    w = b - a
                    half = ((t3 * 5 + si) % 2) * 512
                    if len(evhist) >= 2:
                        wt("T", evhist[-2][0], evhist[-2][1])
                    if g == 0:
                        wt("T", "V", ln2V[min(len(SL) - 1, (a + w) // 256)])
                    def mm_qkv(e, wg=wg, t3=t3, a=a, w=w, half=half):
                        ins = None
                        for kc in range(NCH):
                            ins = e.matmul(PSA[:, half:half + w], wg[:, kc, t3 * 128:t3 * 128 + 128],
                                           LN2BF[:, kc, a:a + w], start=(kc == 0), stop=(kc == NCH - 1))
                        return ins
                    tt = op("T", mm_qkv, "T")
                    if t3 == 0:
                        if g == 0 and si == 0:
                            wt("S", "dY", XSPILL_dY)
                        wt("S", "T", tt)
                        sv = op("S", lambda e, g=g, a=a, w=w, half=half: e.copy(
                            out=QALL[:, g, a:a + w], in_=PSA[:, half:half + w]), "S")
                        evhist.append(("S", sv))
                    elif t3 == 1:
                        wt("V", "T", tt)
                        if si == 0 and g >= 2:
                            wt("V", "T", k3vT[g - 2])
                        sv = op("V", lambda e, a=a, w=w, half=half, pq=pq: e.tensor_copy(
                            out=KGP[pq][:, a:a + w], in_=PSA[:, half:half + w]), "V")
                        evhist.append(("V", sv))
                    else:
                        wt("S", "T", tt)
                        if si == 0 and g >= 2:
                            wt("S", "T", k3vT[g - 2])
                        sv = op("S", lambda e, a=a, w=w, half=half, pq=pq: e.copy(
                            out=VGP[pq][:, a:a + w], in_=PSA[:, half:half + w]), "S")
                        evhist.append(("S", sv))
            kevV[g] = evhist[-6][1]
            vevS[g] = evhist[-1][1]
            lastWG_T[g] = cnt["T"]

        def stage_R(g):
            wg = WG[g % 2]
            pq = g % 2
            wt("T", "dE", PGL_dG)
            if g >= 1:
                wt("T", "V", qlmV[g - 1])
                wt("T", "S", k3eS[(g - 1, 3)])
            def mm_qlm(e, wg=wg):
                ins = None
                for kc in range(NCH):
                    ins = e.matmul(PSB[:, 512:576], wg[:, kc, 0:128], POOLEDG[:, kc, :],
                                   start=(kc == 0), stop=(kc == NCH - 1))
                return ins
            t_qlm = op("T", mm_qlm, "T")
            def mm_klm(e, wg=wg):
                ins = None
                for kc in range(NCH):
                    ins = e.matmul(PSB[:, 576:640], wg[:, kc, 128:256], POOLEDG[:, kc, :],
                                   start=(kc == 0), stop=(kc == NCH - 1))
                return ins
            klmT[g] = op("T", mm_klm, "T")
            if g + 2 < 8:
                wt("G", "T", klmT[g])
                for th in wg_load_thunks(g + 2, g % 2):
                    WGL[g + 2] = op("G", th, "dQ%d" % (g % 2), 16)
            wt("V", "T", klmT[g])
            wt("V", "V", BDZERO_V)
            op("V", lambda e, g=g: e.tensor_scalar(out=BDQ[0:64, g, 0:64], in0=PSB[0:64, 512:576],
                                                   scalar1=0.125, scalar2=None, op0=ALU.mult))
            op("V", lambda e, g=g: e.tensor_scalar(out=BDQ[64:128, g, 64:128], in0=PSB[64:128, 512:576],
                                                   scalar1=0.125, scalar2=None, op0=ALU.mult))
            qlmV[g] = op("V", lambda e, g=g: e.tensor_copy(out=KLM[:, g, :], in_=PSB[:, 576:640]), "V")
            wt("T", "V", qlmV[g])
            l2T[g] = op("T", lambda e, g=g: e.matmul(PSD[:, g * 64:g * 64 + 64], BDQ[:, g, :],
                                                     KLM[:, g, :], start=True, stop=True), "T")

            for si, (a, b) in enumerate(SLO):
                wt("T", "V", kevV[g])
                prev = k3eS.get((g, si - 2)) if si >= 2 else k3eS.get((g - 1, si + 2))
                if prev is not None:
                    wt("T", "S", prev)
                t3t = op("T", lambda e, g=g, a=a, si=si, pq=pq: e.matmul(
                    PSB[:, (si % 2) * 512:(si % 2) * 512 + 512], BDQ[:, g, :],
                    KGP[pq][:, a:a + 512], start=True, stop=True), "T")
                wt("S", "T", t3t)
                k3eS[(g, si)] = op("S", lambda e, g=g, si=si, pq=pq: e.activation(
                    out=K3EP[pq][:, si * 512:si * 512 + 512],
                    in_=PSB[:, (si % 2) * 512:(si % 2) * 512 + 512],
                    func=AF.Exp, accum_out=sums3p[:, g, si:si + 1]), "S")

            for half in range(2):
                base = half * 8
                wt("T", "S", k3eS[(g, 2 * half + 1)])
                prev_ev = k3tV.get((g, half - 1)) if half >= 1 else k3tV.get((g - 1, 1))
                if prev_ev is not None:
                    wt("T", "V", prev_ev)
                def mm_tr3(e, base=base, pq=pq):
                    ins = None
                    for j in range(8):
                        oc = base + j
                        ins = e.transpose(PST[:, j * 128:j * 128 + 128],
                                          K3EP[pq][:, oc * 128:oc * 128 + 128], identbf[:])
                    return ins
                tr = op("T", mm_tr3, "T")
                wt("V", "T", tr)
                if g >= 2 and half == 0:
                    wt("V", "T", k3vT[g - 2])
                k3tV[(g, half)] = op("V", lambda e, base=base, pq=pq: e.tensor_copy(
                    out=K3TP[pq][:, base:base + 8, :],
                    in_=PST[:].rearrange("p (a b) -> p a b", b=128)), "V")
            for half in range(2):
                base = half * 8
                wt("T", "S", vevS[g])
                if half == 0:
                    wt("T", "V", k3tV[(g, 1)])
                    prev_ev = vtokS.get((g - 1, 1))
                else:
                    prev_ev = vtokS.get((g, 0))
                if prev_ev is not None:
                    wt("T", "S", prev_ev)
                def mm_trv(e, base=base, pq=pq):
                    ins = None
                    for j in range(8):
                        a0 = CHO16[base + j][0]
                        ins = e.transpose(PST[:, j * 128:j * 128 + 128],
                                          VGP[pq][:, a0:a0 + 128], identbf[:])
                    return ins
                trv = op("T", mm_trv, "T")
                wt("S", "T", trv)
                if g >= 2 and half == 0:
                    wt("S", "T", k3vT[g - 2])
                vtokS[(g, half)] = op("S", lambda e, base=base, pq=pq: e.copy(
                    out=VTOKP[pq][:, base:base + 8, :],
                    in_=PST[:].rearrange("p (a b) -> p a b", b=128)), "S")

            wt("T", "V", k3tV[(g, 1)])
            wt("T", "S", vtokS[(g, 1)])
            def mm_k3v(e, g=g, pq=pq):
                ins = None
                for oc in range(16):
                    ins = e.matmul(PSC[:, g * 128:g * 128 + 128], K3TP[pq][:, oc, :],
                                   VTOKP[pq][:, oc, :], start=(oc == 0), stop=(oc == 15))
                return ins
            k3vT[g] = op("T", mm_k3v, "T")

        stage_Q(0)
        stage_Q(1)
        for g in range(8):
            stage_R(g)
            if g + 2 < 8:
                stage_Q(g + 2)

        QALL_S = cnt["S"]

        wt("V", "T", k3vT[7])
        op("V", lambda e: e.tensor_copy(
            out=CCS[0:64, 0:512].rearrange("p (g x) -> p g x", x=64),
            in_=PSC[0:64, :].rearrange("p (g x) -> p g x", x=128)[:, :, 0:64]))
        op("V", lambda e: e.tensor_copy(
            out=CCS[64:128, 0:512].rearrange("p (g x) -> p g x", x=64),
            in_=PSC[64:128, :].rearrange("p (g x) -> p g x", x=128)[:, :, 64:128]))
        wt("V", "S", k3eS[(7, 3)])
        CCS_V = op("V", lambda e: e.tensor_reduce(
            out=CCS[:, 512:520], in_=sums3p[:], axis=AX.X, op=ALU.add), "V")
        wt("G", "V", CCS_V)
        op("G", lambda e: e.dma_start(out=cc4in.ap(), in_=CCS[:, 0:520]), "dE", 16)
        wt("G", "dE", cnt["dE"])
        CC4 = op("G", lambda e: e.collective_compute(
            "AllReduce", ALU.add, replica_groups=RG,
            ins=[cc4in.ap().opt()], outs=[cc4out.ap().opt()]), "CC", 1)
        wt("G", "CC", CC4)
        CCR_dG = op("G", lambda e: e.dma_start(out=CCR[:, 0:520], in_=cc4out.ap()), "dE", 16)

        snap(3, ("dE", CCR_dG), CCR)
        wt("G", "T", k3vT[7])
        WPROJ_dG = op("G", lambda e: e.dma_start(
            out=WPROJ[:], in_=wproj_d.rearrange("(c p) d -> p c d", p=128)), "dG", 16)

        wt("S", "T", l2T[7])
        wt("S", "T", k3vT[7])
        K2e_S = op("S", lambda e: e.activation(out=K2E[:], in_=PSD[:].rearrange("p (g x) -> p g x", x=64),
                                               func=AF.Exp), "S")
        wt("V", "S", K2e_S)
        op("V", lambda e: e.tensor_reduce(out=rc8[:], in_=K2E[:], axis=AX.X, op=ALU.add))
        op("V", lambda e: e.drain())
        op("V", lambda e: e.reciprocal(out=rc8[:], in_=rc8[:]))
        op("V", lambda e: e.drain())
        K2n_V = op("V", lambda e: e.tensor_tensor(out=K2E[:], in0=K2E[:],
                                                  in1=rc8[:].unsqueeze(2).broadcast_to([128, 8, 64]),
                                                  op=ALU.mult), "V")
        wt("T", "V", K2n_V)
        def mm_cs(e):
            ins = None
            for g in range(8):
                ins = e.matmul(PSB[0:2, g * 64:g * 64 + 64], onesbd2[:], K2E[:, g, :],
                               start=True, stop=True)
            return ins
        cs_T = op("T", mm_cs, "T")
        wt("V", "T", cs_T)
        op("V", lambda e: e.tensor_copy(out=cs2[:], in_=PSB[0:2, 0:512].rearrange("p (g x) -> p g x", x=64)))
        op("V", lambda e: e.tensor_reduce(out=mx2[:], in_=cs2[:], axis=AX.X, op=ALU.max))
        op("V", lambda e: e.drain())
        mx_V = op("V", lambda e: e.reciprocal(out=mx2[:], in_=mx2[:]), "V")
        op("V", lambda e: e.drain())
        wt("T", "V", mx_V)
        dn_T = op("T", lambda e: e.matmul(PSB[:, 512:520], b2c[:], mx2[:, :], start=True, stop=True), "T")
        wt("V", "T", dn_T)
        dn_V = op("V", lambda e: e.tensor_copy(out=denr[:], in_=PSB[:, 512:520]), "V")
        op("V", lambda e: e.drain())

        op("V", lambda e: e.memset(BDP[:], 0.0))
        op("V", lambda e: e.tensor_copy(out=BDP[0:64, :, 0:64], in_=K2E[0:64, :, :]))
        bdk_V = op("V", lambda e: e.tensor_copy(out=BDP[64:128, :, 64:128], in_=K2E[64:128, :, :]), "V")
        wt("T", "V", bdk_V)
        def mm_trk(e):
            ins = None
            for g in range(8):
                ins = e.transpose(PST[:, g * 128:g * 128 + 128], BDP[:, g, :], identbf[:])
            return ins
        trk_T = op("T", mm_trk, "T")
        wt("V", "T", trk_T)
        op("V", lambda e: e.tensor_copy(out=KTBD[:], in_=PST[:].rearrange("p (a b) -> p a b", b=128)))
        wt("V", "V", dn_V)
        op("V", lambda e: e.tensor_tensor(out=VS_[0:64, :, :], in0=KTBD[0:64, :, 0:64],
                                          in1=denr[0:64, :].unsqueeze(2).broadcast_to([64, 8, 64]), op=ALU.mult))
        op("V", lambda e: e.tensor_tensor(out=VS_[64:128, :, :], in0=KTBD[64:128, :, 64:128],
                                          in1=denr[64:128, :].unsqueeze(2).broadcast_to([64, 8, 64]), op=ALU.mult))
        itV = op("V", lambda e: e.tensor_tensor(out=VTBD[:], in0=BDP[:],
                                                in1=denr[:].unsqueeze(2).broadcast_to([128, 8, 128]), op=ALU.mult), "V")

        pst_user = [None]

        def newton_stages(it, state):
            loc = {}

            def st1():
                wt("T", "V", state["itV"])
                if it == 0:
                    wt("T", "V", CCS_V)
                def mm_P(e):
                    ins = None
                    for g in range(8):
                        ins = e.matmul(PSC[:, g * 64:g * 64 + 64], KTBD[:, g, :], VS_[:, g, :],
                                       start=True, stop=True)
                    return ins
                P_T = op("T", mm_P, "T")
                wt("V", "T", P_T)
                loc["ns1_V"] = op("V", lambda e: e.scalar_tensor_tensor(
                    out=NS1[:], in0=PSC[:, 0:512].rearrange("p (g x) -> p g x", x=64),
                    scalar=-1.0, in1=ints[:, 0, :].unsqueeze(1).broadcast_to([128, 8, 64]),
                    op0=ALU.mult, op1=ALU.add), "V")
                wt("S", "T", P_T)
                op("S", lambda e: e.copy(out=BDP[0:64, :, 0:64],
                                         in_=PSC[0:64, 0:512].rearrange("p (g x) -> p g x", x=64)))
                loc["bdp_V"] = op("S", lambda e: e.copy(
                    out=BDP[64:128, :, 64:128],
                    in_=PSC[64:128, 0:512].rearrange("p (g x) -> p g x", x=64)), "S")

            def st2():
                wt("T", "S", loc["bdp_V"])
                if pst_user[0] is not None:
                    wt("T", pst_user[0][0], pst_user[0][1])
                def mm_trp(e):
                    ins = None
                    for g in range(8):
                        ins = e.transpose(PST[:, g * 128:g * 128 + 128], BDP[:, g, :], identbf[:])
                    return ins
                trp_T = op("T", mm_trp, "T")
                wt("S", "T", trp_T)
                ptbd_V = op("S", lambda e: e.copy(out=PTBD[:], in_=PST[:].rearrange("p (a b) -> p a b", b=128)), "S")
                pst_user[0] = ("S", ptbd_V)
                loc["ptbd_V"] = ptbd_V

            def st3():
                wt("T", "S", loc["ptbd_V"])
                wt("T", "V", loc["ns1_V"])
                def mm_T2(e):
                    ins = None
                    for g in range(8):
                        ins = e.matmul(PSC[:, 512 + g * 64:512 + g * 64 + 64], PTBD[:, g, :], NS1[:, g, :],
                                       start=True, stop=True)
                    return ins
                T2_T = op("T", mm_T2, "T")
                wt("V", "T", T2_T)
                loc["s2_V"] = op("V", lambda e: e.scalar_tensor_tensor(
                    out=NS2[:], in0=PSC[:, 512:1024].rearrange("p (g x) -> p g x", x=64),
                    scalar=-1.0, in1=ints[:, 1, :].unsqueeze(1).broadcast_to([128, 8, 64]),
                    op0=ALU.mult, op1=ALU.add), "V")

            def st4():
                wt("T", "V", loc["s2_V"])
                wt("T", "V", loc["ns1_V"])
                wt("T", "S", loc["bdp_V"])
                def mm_T3(e):
                    ins = None
                    for g in range(8):
                        ins = e.matmul(PSC[:, g * 64:g * 64 + 64], PTBD[:, g, :], NS2[:, g, :],
                                       start=True, stop=True)
                    return ins
                T3_T = op("T", mm_T3, "T")
                wt("V", "T", T3_T)
                loc["s3_V"] = op("V", lambda e: e.scalar_tensor_tensor(
                    out=NS3[:], in0=PSC[:, 0:512].rearrange("p (g x) -> p g x", x=64),
                    scalar=-1.0, in1=ints[:, 2, :].unsqueeze(1).broadcast_to([128, 8, 64]),
                    op0=ALU.mult, op1=ALU.add), "V")

            def st5():
                wt("T", "V", loc["s3_V"])
                def mm_Vn(e):
                    ins = None
                    for g in range(8):
                        ins = e.matmul(PSC[:, 512 + g * 64:512 + g * 64 + 64], VTBD[:, g, :], NS3[:, g, :],
                                       start=True, stop=True)
                    return ins
                Vn_T = op("T", mm_Vn, "T")
                wt("V", "T", Vn_T)
                op("V", lambda e: e.tensor_scalar(out=VS_[:], in0=PSC[:, 512:1024].rearrange("p (g x) -> p g x", x=64),
                                                  scalar1=0.25, scalar2=None, op0=ALU.mult))
                op("V", lambda e: e.tensor_scalar(out=BDP[0:64, :, 0:64],
                                                  in0=PSC[0:64, 512:1024].rearrange("p (g x) -> p g x", x=64),
                                                  scalar1=0.25, scalar2=None, op0=ALU.mult))
                bdv_V = op("V", lambda e: e.tensor_scalar(out=BDP[64:128, :, 64:128],
                                                          in0=PSC[64:128, 512:1024].rearrange("p (g x) -> p g x", x=64),
                                                          scalar1=0.25, scalar2=None, op0=ALU.mult), "V")
                state["itV"] = bdv_V

            return [st1, st2, st3, st4, st5]
            if pst_user[0] is not None:
                wt("T", pst_user[0][0], pst_user[0][1])
            def mm_trv2(e):
                ins = None
                for g in range(8):
                    ins = e.transpose(PST[:, g * 128:g * 128 + 128], BDP[:, g, :], identbf[:])
                return ins
            trv2_T = op("T", mm_trv2, "T")
            wt("S", "T", trv2_T)
            itS = op("S", lambda e: e.copy(out=VTBD[:], in_=PST[:].rearrange("p (a b) -> p a b", b=128)), "S")
            pst_user[0] = ("S", itS)
            wt("V", "S", itS)
            state["itV"] = op("V", lambda e: e.engine_nop() if False else e.nop(), "V")

        wt("V", "V", qlmV[7])
        op("V", lambda e: e.tensor_copy(out=BDKLM[0:64, :, 0:64], in_=KLM[0:64, :, :]))
        bdklm_V = op("V", lambda e: e.tensor_copy(out=BDKLM[64:128, :, 64:128], in_=KLM[64:128, :, :]), "V")
        e1S = {}
        e1V = {}
        trS = {}

        tr1T = {}
        PSDbf = PSD[:, :].bitcast(BF16)
        psd_user = [("S", K2e_S)]

        def l1_mm(ci):
            c0, c1 = CH17[ci]
            wc = c1 - c0
            pg = ci % 2
            s3 = ci % 3
            PSL = PSA if pg == 0 else PSB
            wt("T", "V", bdklm_V)
            if ci == 0:
                wt("T", "S", QALL_S)
                wt("T", "V", kevV[7])
            if ci == 1:
                wt("T", "S", k3eS[(7, 3)])
                wt("T", "V", dn_V)
            prev = e1S.get(ci - 2)
            if prev is not None:
                wt("T", "S", prev)
            def mm_l1(e, c0=c0, wc=wc, PSL=PSL):
                ins = None
                for g in range(8):
                    ins = e.matmul(PSL[0:wc, g * 128:g * 128 + 128], QALL[:, g, c0:c0 + wc],
                                   BDKLM[:, g, :], start=True, stop=True)
                return ins
            e1T = op("T", mm_l1, "T")
            wt("S", "T", e1T)
            prevv = e1V.get(ci - 3)
            if prevv is not None:
                wt("S", "V", prevv)
            if ci - 3 in tr1T:
                wt("S", "T", tr1T[ci - 3])
            e1S[ci] = op("S", lambda e, wc=wc, s3=s3, PSL=PSL: e.activation(
                out=E1B[s3][0:wc, :], in_=PSL[0:wc, :], func=AF.Exp, scale=0.125), "S")
            wt("V", "S", e1S[ci])
            op("V", lambda e, wc=wc, s3=s3: e.tensor_reduce(
                out=rc16[0:wc, s3 * 16:s3 * 16 + 16],
                in_=E1B[s3][0:wc, :].rearrange("p (h x) -> p h x", x=64),
                axis=AX.X, op=ALU.add))

        def l1_recip(ci, guard=False):
            wc = CH17[ci][1] - CH17[ci][0]
            s3 = ci % 3
            if guard:
                op("V", lambda e: e.drain())
            op("V", lambda e, wc=wc, s3=s3: e.reciprocal(
                out=rc16[0:wc, s3 * 16:s3 * 16 + 16], in_=rc16[0:wc, s3 * 16:s3 * 16 + 16]))

        def l1_fin(ci, guard=False):
            c0, c1 = CH17[ci]
            wc = c1 - c0
            s3 = ci % 3
            if guard:
                op("V", lambda e: e.drain())
            e1V[ci] = op("V", lambda e, wc=wc, s3=s3: e.tensor_tensor(
                out=E1B[s3][0:wc, :].rearrange("p (h x) -> p h x", x=64),
                in0=E1B[s3][0:wc, :].rearrange("p (h x) -> p h x", x=64),
                in1=rc16[0:wc, s3 * 16:s3 * 16 + 16].unsqueeze(2).broadcast_to([wc, 16, 64]),
                op=ALU.mult), "V")
            user = pst_user if ci % 2 == 0 else psd_user
            bankT = PST if ci % 2 == 0 else PSDbf
            wt("T", "V", e1V[ci])
            if user[0] is not None:
                wt("T", user[0][0], user[0][1])
            def mm_tr1(e, wc=wc, s3=s3, bankT=bankT):
                ins = None
                for g in range(8):
                    ins = e.transpose(bankT[:, g * wc:g * wc + wc],
                                      E1B[s3][0:wc, g * 128:g * 128 + 128], identbf[0:wc, 0:wc])
                return ins
            tr1T[ci] = op("T", mm_tr1, "T")
            wt("S", "T", tr1T[ci])
            trS[ci] = op("S", lambda e, c0=c0, wc=wc, bankT=bankT: e.copy(
                out=K1T[:, :, c0:c0 + wc],
                in_=bankT[:, 0:8 * wc].rearrange("p (g x) -> p g x", x=wc)), "S")
            user[0] = ("S", trS[ci])

        nw_state = {"itV": itV}
        from collections import deque
        pending = deque()
        nwdone = 0
        bounds = {1: 0, 3: 1, 5: 2, 7: 3, 9: 4, 12: 5}
        for ci in range(17):
            l1_mm(ci)
            if ci - 1 >= 0:
                l1_recip(ci - 1)
            if ci - 2 >= 0:
                l1_fin(ci - 2)
            if ci in bounds:
                sts = newton_stages(bounds[ci], nw_state)
                for st in sts[:3]:
                    st()
                pending.extend(sts[3:])
            elif pending:
                while pending:
                    pending.popleft()()
        while pending:
            pending.popleft()()
        l1_fin(15)
        l1_recip(16)
        l1_fin(16, guard=True)
        NEWT_V = nw_state["itV"]
        K1T_S = cnt["S"]
        snap(4, ("S", K1T_S), K1T[:, :, 0:2048])

        wt("V", "dE", CCR_dG)
        op("V", lambda e: e.reciprocal(out=s3tot[:], in_=CCR[:, 512:520]))
        op("V", lambda e: e.drain())
        k3vn_V = op("V", lambda e: e.tensor_tensor(
            out=K3VN[:], in0=CCR[:, 0:512].rearrange("p (g x) -> p g x", x=64),
            in1=s3tot[:].unsqueeze(2).broadcast_to([128, 8, 64]), op=ALU.mult), "V")
        wt("T", "V", k3vn_V)
        wt("T", "V", NEWT_V)
        wt("T", "S", K1T_S)
        def mm_Z(e):
            ins = None
            for g in range(8):
                ins = e.matmul(PSD[:, g * 64:g * 64 + 64], VTBD[:, g, :], K3VN[:, g, :],
                               start=True, stop=True)
            return ins
        Z_T = op("T", mm_Z, "T")
        wt("V", "T", Z_T)
        op("V", lambda e: e.tensor_copy(out=BDQ[0:64, :, 0:64],
                                        in_=PSD[0:64, :].rearrange("p (g x) -> p g x", x=64)))
        bdz_V = op("V", lambda e: e.tensor_copy(out=BDQ[64:128, :, 64:128],
                                                in_=PSD[64:128, :].rearrange("p (g x) -> p g x", x=64)), "V")

        svS = {}
        flat = []
        for g in range(8):
            for si, (a, b) in enumerate(SL5):
                flat.append((g, si, a, b))
        for n, (g, si, a, b) in enumerate(flat):
            w = b - a
            half = (n % 2) * 512
            wt("T", "V", bdz_V)
            wt("T", "S", K1T_S)
            if n >= 2:
                wt("T", "S", svS[n - 2])
            def mm_sv(e, g=g, a=a, w=w, half=half):
                return e.matmul(PSA[:, half:half + w], BDQ[:, g, :], K1T[:, g, a:a + w],
                                start=True, stop=True)
            svT = op("T", mm_sv, "T")
            wt("S", "T", svT)
            svS[n] = op("S", lambda e, g=g, a=a, w=w, half=half: e.copy(
                out=SVALL[:, g, a:a + w], in_=PSA[:, half:half + w]), "S")
        SV_S = cnt["S"]

        wt("Y", "S", K1T_S)
        XRL1_dY = op("Y", lambda e: e.dma_start(out=X[:, 0:4, :], in_=xsp[:, 0:4, :]), "dX", 16)
        wt("Y", "S", SV_S)
        XRL2_dY = op("Y", lambda e: e.dma_start(out=X[:, 4:8, :], in_=xsp[:, 4:8, :]), "dX2", 16)

        projV = {}
        flat = []
        for dp in range(NCH):
            for si, (a, b) in enumerate(SL5):
                flat.append((dp, si, a, b))
        for n, (dp, si, a, b) in enumerate(flat):
            w = b - a
            half = (n % 2) * 512
            if n == 0:
                wt("T", "S", SV_S)
                wt("T", "dG", WPROJ_dG)
            if n >= 2:
                wt("T", "V", projV[n - 2])
            def mm_pj(e, dp=dp, a=a, w=w, half=half):
                ins = None
                for c in range(NCH):
                    ins = e.matmul(PSB[:, half:half + w], WPROJ[:, c, dp * 128:dp * 128 + 128],
                                   SVALL[:, c, a:a + w], start=(c == 0), stop=(c == NCH - 1))
                return ins
            pjT = op("T", mm_pj, "T")
            wt("V", "T", pjT)
            if n == 0:
                wt("V", "dX", XRL1_dY)
            if dp == 4 and si == 0:
                wt("V", "dX2", XRL2_dY)
            projV[n] = op("V", lambda e, dp=dp, a=a, w=w, half=half: e.tensor_tensor(
                out=X[:, dp, a:a + w], in0=X[:, dp, a:a + w], in1=PSB[:, half:half + w],
                op=ALU.add), "V")
        MASK_V = op("V", lambda e: e.tensor_scalar(out=X[:, :, 0:2], in0=X[:, :, 0:2],
                                                   scalar1=bmask_sb[:, 0:1], scalar2=None,
                                                   op0=ALU.mult), "V")
        PROJ_T = cnt["T"]
        snap(5, ("V", MASK_V), X[:, :, 2:T])

        rows3_of = make_rows(None, "V", MASK_V,
                             slice_gates=[("V", MASK_V if i == 0 else projV[35 + min(4, i // 2)])
                                          for i in range(len(SL))])
        rows3_of.prefetch = 9
        rows3_of(8)
        wt("G", "T", PROJ_T)
        wt("G", "V", MASK_V)
        op("G", lambda e: e.dma_start(out=WKT, in_=wkt_d.rearrange("(c p) d -> p c d", p=128)), "dW0", 16)
        FFNW2K_dG = cnt["dW0"]
        op("G", lambda e: e.dma_start(out=WRT, in_=wrt_d.rearrange("(c p) d -> p c d", p=128)), "dW1", 16)
        FFNW2R_dG = cnt["dW1"]
        op("G", lambda e: e.dma_start(out=WVT, in_=wvt_d.rearrange("(c p) d -> p c d", p=128)), "dW2", 16)
        FFNW2_dG = cnt["dW2"]
        ffn2_G, _, ffn2_addG = emit_ffn(rows3_of, (("dW0", FFNW2K_dG), ("dW1", FFNW2R_dG), ("dW2", FFNW2_dG)),
                                        stats_S=rows3_of.last_S)

        out_pct = out_d.rearrange("(c p) t -> p c t", p=128)
        for i, (a, b) in enumerate(SL):
            a2 = max(a, 2)
            wt("Y", "G", ffn2_addG[i * NCH + NCH - 1])
            op("Y", lambda e, a2=a2, b=b: e.dma_start(
                out=out_pct[:, :, a2 - 2:b - 2], in_=X[:, :, a2:b]), "dY", 16)
        OUT_dY = cnt["dY"]
        wt("Y", "dY", OUT_dY)

        if stage in snaps:
            lens, cnts, (gsem, gval), src_ap = snaps[stage]
            import os as _os
            if _os.environ.get("KERNEL_DEBUG_SNAPS"):
                print("SNAP", stage, "lens", lens, "gate", gsem, gval, "cnts", cnts)
            for k in streams:
                del streams[k][lens[k]:]
            for k in cnt:
                cnt[k] = cnts[k]
            wt("G", gsem, gval)
            if src_ap == "ROWS121":
                dd = op("G", lambda e: e.dma_start(out=out_d[0:1, 0:2048], in_=srow[:, 2:2050]), "dG", 16)
                dd = op("G", lambda e: e.dma_start(out=out_d[1:2, 0:2048], in_=vrow[:, 2:2050]), "dG", 16)
                dd = op("G", lambda e: e.dma_start(out=out_d[2:3, 0:2048], in_=qrow[:, 2:2050]), "dG", 16)
            elif src_ap == "LNSL131":
                dd = op("G", lambda e: e.dma_start(out=out_d[0:128, 0:2048],
                                                   in_=LNSL[:].rearrange("p a b -> p (a b)")[:, 0:2048]), "dG", 16)
                dd = op("G", lambda e: e.dma_start(out=out_d[128:256, 0:257], in_=BCN[:]), "dG", 16)
                dd = op("G", lambda e: e.dma_start(out=out_d[256:384, 0:257], in_=BCS[:]), "dG", 16)
            elif src_ap == "XRK132":
                dd = op("G", lambda e: e.dma_start(out=out_d[0:128, 0:2048],
                                                   in_=XR2[:].rearrange("p a b c -> p (a b c)")[:, 0:2048]), "dG", 16)
                dd = op("G", lambda e: e.dma_start(out=out_d[128:256, 0:2048],
                                                   in_=XK2[:].rearrange("p a b c -> p (a b c)")[:, 0:2048]), "dG", 16)
                dd = op("G", lambda e: e.dma_start(out=out_d[256:384, 0:2048],
                                                   in_=KACT[:].rearrange("p a b -> p (a b)")), "dG", 16)
            elif src_ap == "ATTN31":
                dd = op("G", lambda e: e.dma_start(out=out_d[0:128, 0:2048], in_=K3Eb), "dG", 16)
                dd = op("G", lambda e: e.dma_start(out=out_d[128:256, 0:1024],
                                                   in_=BDQ[:].rearrange("p a b -> p (a b)")), "dG", 16)
                dd = op("G", lambda e: e.dma_start(out=out_d[256:384, 0:512],
                                                   in_=KLM[:].rearrange("p a b -> p (a b)")), "dG", 16)
                dd = op("G", lambda e: e.dma_start(out=out_d[384:512, 0:2048], in_=KG[:, 0:2048]), "dG", 16)
                dd = op("G", lambda e: e.dma_start(out=out_d[512:640, 0:256],
                                                   in_=POOLED[:].rearrange("p a b -> p (a b)")), "dG", 16)
                dd = op("G", lambda e: e.dma_start(out=out_d[640:768, 0:512],
                                                   in_=POOLEDG[:].rearrange("p a b -> p (a b)")), "dG", 16)
            elif src_ap == "LN2BF21":
                dd = op("G", lambda e: e.dma_start(
                    out=out_d.rearrange("(c p) t -> p c t", p=128),
                    in_=LN2BF[:, :, 2:T]), "dG", 16)
            elif src_ap is CCR:
                dd = op("G", lambda e: e.dma_start(out=out_d[0:128, 0:520], in_=src_ap), "dG", 16)
            else:
                dd = op("G", lambda e: e.dma_start(
                    out=out_d.rearrange("(c p) t -> p c t", p=128), in_=src_ap), "dG", 16)
            wt("G", "dG", dd)

        @block.tensor
        def _(e):
            for f in streams["T"]:
                f(e)

        @block.vector
        def _(e):
            for f in streams["V"]:
                f(e)

        @block.scalar
        def _(e):
            for f in streams["S"]:
                f(e)

        @block.gpsimd
        def _(e):
            for f in streams["G"]:
                f(e)

        @block.sync
        def _(e):
            for f in streams["Y"]:
                f(e)

    return nc


_NC_CACHE = {}


def kernel(L0_attn_x, L0_attn_w_qkv, L0_attn_w_proj, L0_ffn_W_r, L0_ffn_W_k,
           L0_ffn_W_v, L0_ffn_mu_r, L0_ffn_mu_k):
    x = np.ascontiguousarray(np.asarray(L0_attn_x, dtype=np.float32))
    w_qkv = np.ascontiguousarray(np.asarray(L0_attn_w_qkv, dtype=np.float32))
    w_proj = np.ascontiguousarray(np.asarray(L0_attn_w_proj, dtype=np.float32))
    wrt = np.ascontiguousarray(np.asarray(L0_ffn_W_r, dtype=np.float32).T)
    wkt = np.ascontiguousarray(np.asarray(L0_ffn_W_k, dtype=np.float32).T)
    wvt = np.ascontiguousarray(np.asarray(L0_ffn_W_v, dtype=np.float32).T)
    mu = np.stack([np.asarray(L0_ffn_mu_r, dtype=np.float32).reshape(8, 128),
                   np.asarray(L0_ffn_mu_k, dtype=np.float32).reshape(8, 128)], axis=0)
    mu = np.ascontiguousarray(mu)

    ident = np.eye(128, dtype=np.float32).astype(ml_dtypes.bfloat16)
    i64 = np.concatenate([np.eye(64, dtype=np.float32)] * 2, axis=0)
    ints = np.ascontiguousarray(
        np.stack([7.0 * i64, 15.0 * i64, 13.0 * i64], axis=1).astype(ml_dtypes.bfloat16))
    onescol = np.ones((128, 1), dtype=np.float32).astype(ml_dtypes.bfloat16)
    onesbd2 = np.zeros((128, 2), dtype=np.float32)
    onesbd2[0:64, 0] = 1.0
    onesbd2[64:128, 1] = 1.0
    onesbd2 = onesbd2.astype(ml_dtypes.bfloat16)
    bconst = np.zeros((1, 2, 128), dtype=np.float32)
    bconst[0, 0, :] = -1.0 / 1024.0
    bconst[0, 1, :] = 1.0
    b2 = np.zeros((2, 128), dtype=np.float32)
    b2[0, 0:64] = 1.0
    b2[1, 64:128] = 1.0

    import os
    _stage = int(os.environ.get("KERNEL_STAGE", "0"))
    if ("nc", _stage) not in _NC_CACHE:
        _NC_CACHE[("nc", _stage)] = build_nc(_stage)
    nc = _NC_CACHE[("nc", _stage)]

    in_maps = []
    for core in range(8):
        b = core // 2
        half = core % 2
        if half == 0:
            tile = np.concatenate([np.zeros((2, 1024), np.float32), x[b, 0:2048]], axis=0)
            bm = np.zeros((128, 1), np.float32)
        else:
            tile = x[b, 2046:4096]
            bm = np.ones((128, 1), np.float32)
        xtile = np.ascontiguousarray(tile.T)
        in_maps.append({
            "xt": xtile, "wqkv": w_qkv, "wproj": w_proj, "wrt": wrt, "wkt": wkt,
            "wvt": wvt, "mud": mu, "bmaskd": bm, "identbfd": ident, "intsd": ints,
            "onescold": onescol, "onesbd2d": onesbd2, "bconstd": bconst,
            "b2d": b2,
        })

    res = run_bass_kernel_spmd(nc, in_maps, core_ids=list(range(8)))
    out = np.empty((4, 4096, 1024), dtype=np.float32)
    for core in range(8):
        b = core // 2
        half = core % 2
        o = res.results[core]["out"]
        out[b, half * 2048:(half + 1) * 2048] = o.T
    return out
